# version 02
# baseline (speedup 1.0000x reference)
from contextlib import ExitStack
import numpy as np
import ml_dtypes
import concourse.bacc as bacc
import concourse.mybir as mybir
from concourse.bass_utils import run_bass_kernel_spmd

F32 = mybir.dt.float32
BF16 = mybir.dt.bfloat16
I16 = mybir.dt.int16
AF = mybir.ActivationFunctionType
OP = mybir.AluOpType
AX = mybir.AxisListType
BF = ml_dtypes.bfloat16

S, D, H, DH, ROT = 4096, 1024, 16, 64, 32
NG, HPG = 4, 4
E, EPC, TOPK, CAP, I_FF = 32, 4, 4, 1024, 512
RGRP = [[0, 1, 2, 3, 4, 5, 6, 7]]
N_CORES, NQT, NTC = 8, 8, 8

NBF, NF32 = 78848, 7536
O_WG, O_XB, O_QG, O_KG, O_VG = 0, 10240, 18432, 26624, 34816
O_AG, O_PT, O_UNN, O_RB = 43008, 75776, 77824, 78336
O_WO, O_H1B = 0, 8192
O_H1FM, O_SHACT, O_DISP, O_EOUT = 43008, 18432, 34816, 0
FB_EOST, FB_PAY, FB_WDN, FB_GK, FB_ACTE, FB_WGU = 0, 1024, 2048, 2048, 6144, 10240
O_SHACT2, O_DISP2 = 18432, 34816
FB_OH, FB_DEC, FB_GW = 75776, 76032, 76160
F_POS, F_IW, F_GS, F_GC, F_TT, F_OFS, F_HL, F_KEPT = 5232, 6256, 6512, 6640, 6768, 6896, 6928, 6992
F_ACC0, F_H1TF, F_TMPF, F_XRESF = 0, 4192, 5232, 6512
F_RECIP, F_MBIAS, F_RM, F_XRES, F_TMP, F_H1T, F_SCAL, F_RT = \
    0, 1024, 1060, 1120, 3168, 4192, 5216, 5232

SEM_OWNER = {"LD": "sync", "WLD": "sync", "SST": "sync",
             "XLD": "gpsimd", "GLD": "gpsimd", "GST": "gpsimd", "COL": "gpsimd",
             "MM": "tensor", "ACT": "scalar", "VEC": "vector"}
SEM_BY = {"LD": 16, "WLD": 16, "SST": 16, "XLD": 16, "GLD": 16, "GST": 16,
          "COL": 1, "MM": 1, "ACT": 1, "VEC": 1}
ENGS = ["sync", "gpsimd", "tensor", "scalar", "vector"]


def host_constants():
    c = {}
    t = np.arange(S, dtype=np.float64)
    inv = 1.0 / (10000.0 ** (np.arange(0, ROT, 2, dtype=np.float64) / ROT))
    ang = t[None, :] * inv[:, None]
    cos16, sin16 = np.cos(ang), np.sin(ang)
    p = np.arange(128)
    rot_row = (p % 64) < 32
    ce, se = np.tile(cos16, (8, 1)), np.tile(sin16, (8, 1))
    ce[~rot_row] = 1.0
    se[~rot_row] = 0.0
    c["cos_ext"], c["sin_ext"] = ce.astype(BF), se.astype(BF)
    m = np.zeros((128, 4, 512), np.float32)
    for j in range(4):
        for kp in range(128):
            m[kp, j, j * 128 + kp:] = 1.0
    c["mask01"] = m.astype(BF)
    c["ones_col"] = np.ones((128, 1), BF)
    c["ones_row1"] = np.ones((1, 128), BF)
    c["ones_row1f"] = np.ones((1, 128), np.float32)
    tr = np.zeros((128, 132), np.float32)
    for tp in range(128):
        tr[0:tp, tp] = 1.0
    tr[:, 128] = 1.0
    c["trilT"] = tr.astype(BF)
    c["iota32"] = np.tile(np.arange(32, dtype=np.float32)[None, :], (128, 1))
    c["identf"] = np.eye(128, dtype=np.float32)
    return c


def percore_inputs(inputs):
    x = np.asarray(inputs["L0_attn_x"], dtype=np.float32)
    w_q = np.ascontiguousarray(np.asarray(inputs["L0_attn_w_q"], dtype=np.float32))
    w_kv = np.asarray(inputs["L0_attn_w_kv"], dtype=np.float32)
    w_out = np.ascontiguousarray(np.asarray(inputs["L0_attn_w_out"], dtype=np.float32))
    gate_w = np.asarray(inputs["L0_ffn_gate_w"], dtype=np.float32)
    w_gu = np.asarray(inputs["L0_ffn_w_gate_up"], dtype=np.float32)
    w_dn = np.asarray(inputs["L0_ffn_w_down"], dtype=np.float32)
    sh_gu = np.ascontiguousarray(np.asarray(inputs["L0_ffn_shared_w_gate_up"], dtype=np.float32))
    sh_dn = np.ascontiguousarray(np.asarray(inputs["L0_ffn_shared_w_down"], dtype=np.float32))
    consts = host_constants()
    maps = []
    for cid in range(N_CORES):
        m = dict(consts)
        m["x_tm"] = np.ascontiguousarray(x[cid])
        m["x_fmT"] = np.ascontiguousarray(x[cid].T)
        m["w_q"] = w_q
        m["w_k"] = np.ascontiguousarray(w_kv[:, 0:D])
        m["w_v"] = np.ascontiguousarray(w_kv[:, D:2 * D])
        m["w_out"] = w_out
        m["gate_wT"] = np.ascontiguousarray(gate_w.T)
        m["w_gu"] = np.ascontiguousarray(w_gu[cid * EPC:(cid + 1) * EPC])
        m["w_dn"] = np.ascontiguousarray(w_dn[cid * EPC:(cid + 1) * EPC])
        m["sh_gu"] = sh_gu
        m["sh_dn"] = sh_dn
        pm = np.zeros((32, 8), np.float32)
        pm[:, 0:cid] = 1.0
        m["prevmask"] = pm
        gid = np.zeros((128, 32), np.float32)
        for tau in range(128):
            gid[tau, :] = cid * S + np.arange(32) * 128 + tau + 1
        hi = np.floor_divide(gid, 128.0)
        m["gid_hi"] = hi.astype(BF)
        m["gid_lo"] = (gid - hi * 128.0).astype(BF)
        maps.append(m)
    return maps


class _Cut(Exception):
    pass


def build_nc(debug=(), cut=None):
    nc = bacc.Bacc("TRN2", target_bir_lowering=False,
                   detect_race_conditions=False)
    d = {}

    def din(n, s, t):
        d[n] = nc.dram_tensor(n, s, t, kind="ExternalInput")

    def dout(n, s, t):
        d[n] = nc.dram_tensor(n, s, t, kind="ExternalOutput")

    def dint(n, s, t):
        d[n] = nc.dram_tensor(n, s, t)

    din("x_tm", [S, D], F32)
    din("x_fmT", [D, S], F32)
    din("w_q", [D, D], F32)
    din("w_k", [D, D], F32)
    din("w_v", [D, D], F32)
    din("w_out", [D, D], F32)
    din("gate_wT", [D, E], F32)
    din("w_gu", [EPC, D, 2 * I_FF], F32)
    din("w_dn", [EPC, I_FF, D], F32)
    din("sh_gu", [D, 2 * I_FF], F32)
    din("sh_dn", [I_FF, D], F32)
    din("cos_ext", [128, S], BF16)
    din("sin_ext", [128, S], BF16)
    din("mask01", [128, 4, 512], BF16)
    din("ones_col", [128, 1], BF16)
    din("ones_row1", [1, 128], BF16)
    din("ones_row1f", [1, 128], F32)
    din("trilT", [128, 132], BF16)
    din("iota32", [128, 32], F32)
    din("identf", [128, 128], F32)
    din("prevmask", [32, 8], F32)
    din("gid_hi", [128, 32], BF16)
    din("gid_lo", [128, 32], BF16)
    dint("wb_q", [D, D], BF16)
    dint("wb_k", [D, D], BF16)
    dint("wq_c", [D, D], BF16)
    dint("wk_c", [D, D], BF16)
    dint("wv_c", [D, D], BF16)
    dint("wo_c", [D, D], BF16)
    dint("h1_tm_d", [S, D], F32)
    dint("h1_pub", [S, D], BF16)
    dint("h1_all", [N_CORES * S, D], BF16)
    dint("cnt_loc", [1, E], F32)
    dint("cnt_all", [N_CORES, E], F32)
    dint("slot_loc", [E * CAP, 128], BF16)
    dint("slot_rs", [EPC * CAP, 128], BF16)
    dint("rt_scr", [12, S], F32)
    dint("eo_loc", [EPC * CAP, D], BF16)
    dint("eo_all", [E * CAP, D], BF16)
    dout("out", [S, D], F32)
    dbg = []
    if "h1" in debug:
        dout("dbg_h1", [S, D], F32)
        dbg.append("dbg_h1")
    if "arena" in debug:
        dout("dbg_ag", [128, 8 * S], BF16)
        dout("dbg_f32", [128, NF32], F32)
        dbg.append("dbg_ag")
    if "ffn" in debug:
        dout("dbg_cnt", [N_CORES, E], F32)
        dout("dbg_srs", [EPC * CAP, 128], BF16)
        dout("dbg_eo", [EPC * CAP, D], BF16)
        dout("dbg_rt", [12, S], F32)
        dbg.append("dbg_cnt")

    marks = {}
    state = {"emit": False, "en": None, "e": None}
    val = {}
    CUT = cut

    with ExitStack() as ctx:
        AB = ctx.enter_context(nc.sbuf_tensor("AB", [128, NBF], BF16))
        A32 = ctx.enter_context(nc.sbuf_tensor("A32", [128, NF32], F32))
        cb = {}
        for nm, sh, tt in [("cos_e", [128, S], BF16), ("sin_e", [128, S], BF16),
                           ("mask01", [128, 4, 512], BF16), ("ones_col", [128, 1], BF16),
                           ("ones_row1", [1, 128], BF16), ("ones_row1f", [1, 128], F32),
                           ("trilT", [128, 132], BF16),
                           ("iota32", [128, 32], F32), ("identf", [128, 128], F32),
                           ("prevmask", [32, 8], F32), ("gid_hi", [128, 32], BF16),
                           ("gid_lo", [128, 32], BF16)]:
            cb[nm] = ctx.enter_context(nc.sbuf_tensor("sb_" + nm, sh, tt))
        idx16 = ctx.enter_context(nc.sbuf_tensor("idx16", [128, 1280], I16))
        idxu = ctx.enter_context(nc.sbuf_tensor("idxu", [128, 8], mybir.dt.uint32))
        ps0 = ctx.enter_context(nc.psum_tensor("ps0", [128, 1024], F32))
        ps1 = ctx.enter_context(nc.psum_tensor("ps1", [128, 1024], F32))
        pavd = ctx.enter_context(nc.psum_tensor("pavd", [128, 1536], F32))
        pmisc = ctx.enter_context(nc.psum_tensor("pmisc", [128, 512], F32))
        sems = {k: ctx.enter_context(nc.semaphore(k)) for k in SEM_OWNER}
        blk = ctx.enter_context(nc.Block())

        def bfv(off, shape):
            n = int(np.prod(shape))
            ap = AB[:, off:off + n]
            if len(shape) > 1:
                kw = {f"a{i}": shape[i] for i in range(len(shape))}
                pat = " ".join(f"a{i}" for i in range(len(shape)))
                ap = ap.rearrange(f"p ({pat}) -> p {pat}", **kw)
            return ap

        def fv(off, shape):
            n = int(np.prod(shape))
            ap = A32[:, off:off + n]
            if len(shape) > 1:
                kw = {f"a{i}": shape[i] for i in range(len(shape))}
                pat = " ".join(f"a{i}" for i in range(len(shape)))
                ap = ap.rearrange(f"p ({pat}) -> p {pat}", **kw)
            return ap

        def op(sem, builder):
            val[sem] = val.get(sem, 0) + SEM_BY[sem]
            if state["emit"] and state["en"] == SEM_OWNER[sem]:
                builder(state["e"]).then_inc(sems[sem], SEM_BY[sem])
                if sem in ("VEC", "ACT"):
                    state["e"].wait_ge(sems[sem], val[sem])

        def opn(sem, builder):
            if state["emit"] and state["en"] == SEM_OWNER[sem]:
                builder(state["e"])

        def mk(label):
            if not state["emit"]:
                marks[label] = dict(val)

        def wt(eng, label, sem):
            if state["emit"] and state["en"] == eng:
                v = marks[label].get(sem, 0)
                if v > 0:
                    state["e"].wait_ge(sems[sem], v)

        def maybe_cut(name):
            if CUT == name:
                raise _Cut()

        def program():
            if "dbg_ag" in dbg:
                op("VEC", lambda e: e.memset(A32[:, :], 0.0))
                op("VEC", lambda e: e.memset(AB[:, :], 0.0))
                mk("dbginit")
                wt("sync", "dbginit", "VEC")
                wt("gpsimd", "dbginit", "VEC")
            for nm in ["cos_ext", "sin_ext", "mask01", "ones_col", "ones_row1",
                       "ones_row1f", "trilT", "iota32", "identf", "prevmask",
                       "gid_hi", "gid_lo"]:
                dst = {"cos_ext": "cos_e", "sin_ext": "sin_e"}.get(nm, nm)
                op("LD", lambda e, nm=nm, dst=dst: e.dma_start(out=cb[dst][:], in_=d[nm][:]))
            mk("consts")
            for src, dst in [("w_q", "wq_c"), ("w_k", "wk_c"), ("w_v", "wv_c"),
                             ("w_out", "wo_c")]:
                for hh in range(2):
                    op("XLD", lambda e, s=src, t=dst, hh=hh: e.dma_start(
                        out=d[t][:, hh * 512:(hh + 1) * 512],
                        in_=d[s][:, hh * 512:(hh + 1) * 512]))
            mk("wcast")
            maybe_cut("consts")

            wstage = bfv(O_WG, [2, 1024])
            for si, (src, dst) in enumerate([("wq_c", "wb_q"), ("wk_c", "wb_k")]):
                for fc in range(8):
                    it = si * 8 + fc
                    wt("sync", "wcast", "XLD")
                    if it:
                        wt("sync", f"wbTT_{it - 1}", "VEC")
                    op("WLD", lambda e, s=src, fc=fc: e.dma_start(
                        out=wstage[:, 0, :], in_=d[s][fc * 128:(fc + 1) * 128, :]))
                    mk(f"wbL_{it}")
                    wt("vector", f"wbL_{it}", "WLD")
                    if it:
                        wt("vector", f"wbS_{it - 1}", "GLD")
                    op("VEC", lambda e: e.memset(wstage[:, 1, :], 0.0))
                    s4 = wstage[:, 0, :].rearrange("p (h q) -> p h q", h=16)
                    d4 = wstage[:, 1, :].rearrange("p (h q) -> p h q", h=16)
                    op("VEC", lambda e: e.tensor_scalar(d4[:, :, 0:16], s4[:, :, 16:32],
                                                        -1.0, None, OP.mult))
                    op("VEC", lambda e: e.tensor_scalar(d4[:, :, 16:32], s4[:, :, 0:16],
                                                        1.0, None, OP.mult))
                    mk(f"wbTT_{it}")
                    wt("gpsimd", f"wbTT_{it}", "VEC")
                    op("GLD", lambda e, t=dst, fc=fc: e.dma_start(
                        out=d[t][fc * 128:(fc + 1) * 128, :], in_=wstage[:, 1, :]))
                    mk(f"wbS_{it}")
            mk("wb_done")
            maybe_cut("wb")

            wg = bfv(O_WG, [8, 5, 256])
            xb = bfv(O_XB, [2, 8, 512])
            qg = bfv(O_QG, [2, S])
            kg = bfv(O_KG, [2, S])
            vg = bfv(O_VG, [32, 256])
            agf = bfv(O_AG, [8, S])
            pt = bfv(O_PT, [2, 2, 512])
            unn = bfv(O_UNN, [512])
            rb16 = bfv(O_RB, [512])
            recip = fv(F_RECIP, [1024])
            mbias = fv(F_MBIAS, [36])
            rmx = fv(F_RM, [56])
            tmpa = fv(F_XRES, [2, 1024])

            for g in range(NG):
                wt("sync", "wb_done", "GLD")
                if g > 0:
                    mkname = f"projMM_{g - 1}_{NTC - 1}"
                    wt("sync", mkname, "MM")
                for fc in range(8):
                    for wi, nm in enumerate(["wq_c", "wb_q", "wk_c", "wb_k", "wv_c"]):
                        op("WLD", lambda e, fc=fc, wi=wi, nm=nm: e.dma_start(
                            out=wg[:, fc, wi, :],
                            in_=d[nm][fc * 128:(fc + 1) * 128, g * 256:(g + 1) * 256]))
                mk(f"wg_{g}")

                for tc in range(NTC):
                    par = tc % 2
                    if tc >= 2:
                        wt("gpsimd", f"projMM_{g}_{tc - 2}", "MM")
                    elif g > 0:
                        wt("gpsimd", f"projMM_{g - 1}_{NTC - 1}", "MM")
                    else:
                        wt("gpsimd", "wb_done", "GLD")
                    for fc in range(8):
                        op("XLD", lambda e, fc=fc, tc=tc, par=par: e.dma_start(
                            out=xb[:, par, fc, :],
                            in_=d["x_fmT"][fc * 128:(fc + 1) * 128,
                                           tc * 512:(tc + 1) * 512]))
                    mk(f"xld_{g}_{tc}")

                    wt("tensor", f"xld_{g}_{tc}", "XLD")
                    wt("tensor", f"wg_{g}", "WLD")
                    if tc > 0:
                        wt("tensor", f"ropek_{g}_{tc - 1}", "VEC")
                    elif g > 0:
                        wt("tensor", f"lastexp_{g - 1}", "ACT")
                    for half in range(2):
                        for wi, pbuf in ((0, ps0), (1, ps1)):
                            for fc in range(8):
                                def bq(e, half=half, wi=wi, pbuf=pbuf, fc=fc, par=par):
                                    return e.matmul(
                                        pbuf[:, half * 512:(half + 1) * 512],
                                        wg[:, fc, wi, half * 128:(half + 1) * 128],
                                        xb[:, par, fc, :],
                                        start=(fc == 0), stop=(fc == 7))
                                if fc == 7:
                                    op("MM", bq)
                                else:
                                    opn("MM", bq)
                    mk(f"qmm_{g}_{tc}")
                    wt("vector", f"qmm_{g}_{tc}", "MM")
                    for half in range(2):
                        sl = slice(half * 512, (half + 1) * 512)
                        tsl = slice(tc * 512, (tc + 1) * 512)
                        op("VEC", lambda e, sl=sl, tsl=tsl: e.tensor_tensor(
                            tmpa[:, 0, 0:512], ps1[:, sl], cb["sin_e"][:, tsl], OP.mult))
                        op("VEC", lambda e, sl=sl, tsl=tsl: e.tensor_tensor(
                            tmpa[:, 0, 512:1024], ps0[:, sl], cb["cos_e"][:, tsl], OP.mult))
                        op("VEC", lambda e, half=half, tsl=tsl: e.tensor_tensor(
                            qg[:, half, tsl], tmpa[:, 0, 512:1024], tmpa[:, 0, 0:512],
                            OP.add))
                    mk(f"ropeq_{g}_{tc}")
                    wt("tensor", f"ropeq_{g}_{tc}", "VEC")
                    for half in range(2):
                        for wi, pbuf in ((2, ps0), (3, ps1)):
                            for fc in range(8):
                                def bk(e, half=half, wi=wi, pbuf=pbuf, fc=fc, par=par):
                                    return e.matmul(
                                        pbuf[:, half * 512:(half + 1) * 512],
                                        wg[:, fc, wi, half * 128:(half + 1) * 128],
                                        xb[:, par, fc, :],
                                        start=(fc == 0), stop=(fc == 7))
                                if fc == 7:
                                    op("MM", bk)
                                else:
                                    opn("MM", bk)
                    mk(f"kmm_{g}_{tc}")
                    wt("vector", f"kmm_{g}_{tc}", "MM")
                    for half in range(2):
                        sl = slice(half * 512, (half + 1) * 512)
                        tsl = slice(tc * 512, (tc + 1) * 512)
                        op("VEC", lambda e, sl=sl, tsl=tsl: e.tensor_tensor(
                            tmpa[:, 0, 0:512], ps1[:, sl], cb["sin_e"][:, tsl], OP.mult))
                        op("VEC", lambda e, sl=sl, tsl=tsl: e.tensor_tensor(
                            tmpa[:, 0, 512:1024], ps0[:, sl], cb["cos_e"][:, tsl], OP.mult))
                        op("VEC", lambda e, half=half, tsl=tsl: e.tensor_tensor(
                            kg[:, half, tsl], tmpa[:, 0, 512:1024], tmpa[:, 0, 0:512],
                            OP.add))
                    mk(f"ropek_{g}_{tc}")
                    wt("tensor", f"ropek_{g}_{tc}", "VEC")
                    if tc == 0 and g > 0:
                        wt("tensor", f"lastdone_{g - 1}", "VEC")
                    for sub in range(4):
                        if tc > 0 or sub > 1:
                            psub = f"vev_{g}_{tc if sub > 1 else tc - 1}_" \
                                   f"{sub - 2 if sub > 1 else sub + 2}"
                            wt("tensor", psub, "ACT")
                        pv = ps0 if sub % 2 == 0 else ps1
                        for fc in range(8):
                            def bv(e, sub=sub, fc=fc, par=par, pv=pv):
                                return e.matmul(
                                    pv[:, 0:256],
                                    xb[:, par, fc, sub * 128:(sub + 1) * 128],
                                    wg[:, fc, 4, :],
                                    start=(fc == 0), stop=(fc == 7))
                            if fc == 7:
                                op("MM", bv)
                            else:
                                opn("MM", bv)
                        mk(f"vmm_{g}_{tc}_{sub}")
                        wt("scalar", f"vmm_{g}_{tc}_{sub}", "MM")
                        op("ACT", lambda e, tc=tc, sub=sub, pv=pv: e.copy(
                            vg[:, tc * 4 + sub, :], pv[:, 0:256]))
                        mk(f"vev_{g}_{tc}_{sub}")
                    mk(f"projMM_{g}_{tc}")
                    maybe_cut(f"proj_{g}_{tc}")
                for ch in range(2):
                    for ti, buf in ((0, qg), (1, kg)):
                        si = 2 * ch + ti
                        for qt in range(NQT):
                            it = si * NQT + qt
                            wt("scalar", f"ropek_{g}_{NTC - 1}", "VEC")
                            wt("scalar", f"projMM_{g}_{NTC - 1}", "MM")
                            if it:
                                wt("scalar", f"nmm_{g}_{it - 1}", "MM")
                            op("ACT", lambda e, ch=ch, buf=buf, qt=qt: e.activation(
                                unn[:], buf[:, ch, qt * 512:(qt + 1) * 512],
                                AF.Square, 0.0, 1.0))
                            mk(f"nsq_{g}_{it}")
                            wt("tensor", f"nsq_{g}_{it}", "ACT")
                            if it:
                                wt("tensor", f"nmx_{g}_{it - 1}", "VEC")
                            op("MM", lambda e: e.matmul(
                                pmisc[0:1, 0:512], cb["ones_col"][:], unn[:],
                                start=True, stop=True))
                            mk(f"nmm_{g}_{it}")
                            wt("vector", f"nmm_{g}_{it}", "MM")
                            if ti == 0:
                                op("VEC", lambda e, ch=ch, qt=qt: e.tensor_reduce(
                                    rmx[0:1, ch * 8 + qt:ch * 8 + qt + 1],
                                    pmisc[0:1, 0:512], AX.X, OP.max))
                            else:
                                op("VEC", lambda e: e.tensor_reduce(
                                    rmx[0:1, 18:19], pmisc[0:1, 0:512], AX.X, OP.max))
                                if qt == 0:
                                    op("VEC", lambda e, ch=ch: e.tensor_copy(
                                        rmx[0:1, 16 + ch:17 + ch], rmx[0:1, 18:19]))
                                else:
                                    op("VEC", lambda e, ch=ch: e.tensor_tensor(
                                        rmx[0:1, 16 + ch:17 + ch],
                                        rmx[0:1, 16 + ch:17 + ch],
                                        rmx[0:1, 18:19], OP.max))
                            mk(f"nmx_{g}_{it}")
                wt("scalar", f"nmx_{g}_{4 * NQT - 1}", "VEC")
                op("ACT", lambda e: e.activation(rmx[0:1, 19:37], rmx[0:1, 0:18],
                                                 AF.Sqrt, 0.0, 1.0))
                mk(f"msq_{g}")
                wt("vector", f"msq_{g}", "ACT")
                for ch in range(2):
                    op("VEC", lambda e, ch=ch: e.tensor_scalar(
                        rmx[0:1, 37 + ch * 8:45 + ch * 8],
                        rmx[0:1, 19 + ch * 8:27 + ch * 8],
                        rmx[0:1, 35 + ch:36 + ch], -0.125, OP.mult, OP.mult))
                op("VEC", lambda e: e.tensor_scalar(rmx[0:1, 37:53], rmx[0:1, 37:53],
                                                    -1.0, None, OP.add))
                op("VEC", lambda e: e.tensor_copy(rb16[0:1, 16:32], rmx[0:1, 37:53]))
                mk(f"mrow_{g}")
                wt("tensor", f"mrow_{g}", "VEC")
                op("MM", lambda e: e.matmul(pmisc[:, 16:32], cb["ones_row1"][:],
                                            rb16[0:1, 16:32], start=True, stop=True))
                mk(f"mbc_{g}")
                wt("scalar", f"mbc_{g}", "MM")
                wt("scalar", f"vev_{g}_{NTC - 1}_3", "ACT")
                op("ACT", lambda e: e.copy(mbias[:, 0:16], pmisc[:, 16:32]))
                mk(f"mbias_{g}")
                maybe_cut(f"mbias_{g}")

                for hp in range(2):
                    for qt in range(NQT):
                        nkj = (qt + 1) * 4
                        for kj in range(nkj):
                            par = kj % 2
                            psc = ps0 if par == 0 else ps1
                            if kj >= 2:
                                wt("tensor", f"exp_{g}_{hp}_{qt}_{kj - 2}", "ACT")
                                wt("tensor", f"expm_{g}_{hp}_{qt}_{kj - 2}", "VEC")
                            else:
                                if hp == 0 and qt == 0:
                                    wt("tensor", f"mbias_{g}", "ACT")
                                else:
                                    pq = (hp, qt - 1) if qt else (hp - 1, NQT - 1)
                                    pk = (pq[1] + 1) * 4 - 2 + kj
                                    wt("tensor", f"exp_{g}_{pq[0]}_{pq[1]}_{pk}", "ACT")
                                    wt("tensor", f"expm_{g}_{pq[0]}_{pq[1]}_{pk}", "VEC")
                            for hd in range(2):
                                def bs(e, hd=hd, hp=hp, qt=qt, kj=kj, psc=psc):
                                    base = hd * 64
                                    return e.matmul(
                                        psc[:, hd * 512:(hd + 1) * 512],
                                        kg[base:base + 64, hp, kj * 128:(kj + 1) * 128],
                                        qg[base:base + 64, hp, qt * 512:(qt + 1) * 512],
                                        start=True, stop=True,
                                        tile_position=(base, 0),
                                        skip_group_check=True)
                                if hd == 1:
                                    op("MM", bs)
                                else:
                                    opn("MM", bs)
                            mk(f"sc_{g}_{hp}_{qt}_{kj}")
                            maybe_cut(f"sc_{g}_{hp}_{qt}_{kj}")
                            wt("scalar", f"sc_{g}_{hp}_{qt}_{kj}", "MM")
                            if kj >= 2:
                                wt("scalar", f"avm_{g}_{hp}_{qt}_{kj - 2}", "MM")
                            elif qt > 0 or hp > 0:
                                pq = (hp, qt - 1) if qt else (hp - 1, NQT - 1)
                                pk = (pq[1] + 1) * 4 - 2 + kj
                                wt("scalar", f"avm_{g}_{pq[0]}_{pq[1]}_{pk}", "MM")
                            elif g > 0:
                                pk = NQT * 4 - 2 + kj
                                wt("scalar", f"avm_{g - 1}_1_{NQT - 1}_{pk}", "MM")
                            for hd in range(2):
                                op("ACT", lambda e, par=par, hd=hd, qt=qt, hp=hp, psc=psc:
                                   e.activation(pt[:, par, hd, :],
                                                psc[:, hd * 512:(hd + 1) * 512], AF.Exp,
                                                mbias[:, hp * 8 + qt:hp * 8 + qt + 1],
                                                0.125))
                            mk(f"exp_{g}_{hp}_{qt}_{kj}")
                            maybe_cut(f"exp_{g}_{hp}_{qt}_{kj}")
                            diag = kj >= qt * 4
                            if diag:
                                wt("vector", f"exp_{g}_{hp}_{qt}_{kj}", "ACT")
                                for hd in range(2):
                                    op("VEC", lambda e, par=par, hd=hd, j=kj - qt * 4:
                                       e.tensor_tensor(pt[:, par, hd, :], pt[:, par, hd, :],
                                                       cb["mask01"][:, j, :], OP.mult))
                            mk(f"expm_{g}_{hp}_{qt}_{kj}")
                            wt("tensor", f"exp_{g}_{hp}_{qt}_{kj}", "ACT")
                            if diag:
                                wt("tensor", f"expm_{g}_{hp}_{qt}_{kj}", "VEC")
                            if kj == 0:
                                if hp > 0 or qt > 0:
                                    pq = (hp, qt - 1) if qt else (hp - 1, NQT - 1)
                                    wt("tensor", f"unn_{g}_{pq[0]}_{pq[1]}", "ACT")
                                    wt("tensor", f"recip_{g}_{pq[0]}_{pq[1]}", "VEC")
                            first, last = (kj == 0), (kj == nkj - 1)
                            op("MM", lambda e, kj=kj, hp=hp, par=par, first=first, last=last:
                               e.matmul(pavd[0:64, 0:512], vg[:, kj, hp * 128:hp * 128 + 64],
                                        pt[:, par, 0, :], start=first, stop=last,
                                        tile_position=(0, 0), skip_group_check=True))
                            op("MM", lambda e, kj=kj, hp=hp, par=par, first=first, last=last:
                               e.matmul(pavd[64:128, 0:512],
                                        vg[:, kj, hp * 128 + 64:hp * 128 + 128],
                                        pt[:, par, 1, :], start=first, stop=last,
                                        tile_position=(0, 64), skip_group_check=True))
                            op("MM", lambda e, par=par, first=first, last=last:
                               e.matmul(pavd[0:1, 512:1024], cb["ones_col"][:],
                                        pt[:, par, 0, :], start=first, stop=last,
                                        skip_group_check=True))
                            op("MM", lambda e, par=par, first=first, last=last:
                               e.matmul(pavd[0:1, 1024:1536], cb["ones_col"][:],
                                        pt[:, par, 1, :], start=first, stop=last,
                                        skip_group_check=True))
                            mk(f"avm_{g}_{hp}_{qt}_{kj}")
                            maybe_cut(f"avm_{g}_{hp}_{qt}_{kj}")
                        last_avm = f"avm_{g}_{hp}_{qt}_{nkj - 1}"
                        wt("scalar", last_avm, "MM")
                        if hp > 0 or qt > 0:
                            pq = (hp, qt - 1) if qt else (hp - 1, NQT - 1)
                            wt("scalar", f"norm_{g}_{pq[0]}_{pq[1]}", "VEC")
                        op("ACT", lambda e: e.copy(unn[:], pavd[:, 0:512]))
                        mk(f"unn_{g}_{hp}_{qt}")
                        wt("vector", last_avm, "MM")
                        op("VEC", lambda e: e.reciprocal(recip[0:1, :], pavd[0:1, 512:1536]))
                        op("VEC", lambda e: e.tensor_copy(rb16[0:1, 0:512],
                                                          recip[0:1, 0:512]))
                        mk(f"recip_{g}_{hp}_{qt}")
                        wt("tensor", f"recip_{g}_{hp}_{qt}", "VEC")
                        if hp > 0 or qt > 0:
                            pq = (hp, qt - 1) if qt else (hp - 1, NQT - 1)
                            wt("tensor", f"norm_{g}_{pq[0]}_{pq[1]}", "VEC")
                        elif g > 0:
                            wt("tensor", f"norm_{g - 1}_1_{NQT - 1}", "VEC")
                        op("MM", lambda e: e.matmul(pmisc[0:64, :], cb["ones_row1"][0:1, 0:64],
                                                    rb16[0:1, 0:512],
                                                    start=True, stop=True,
                                                    tile_position=(0, 0)))
                        mk(f"bc1_{g}_{hp}_{qt}")
                        wt("vector", f"bc1_{g}_{hp}_{qt}", "MM")
                        op("VEC", lambda e: e.tensor_copy(rb16[0:1, 0:512],
                                                          recip[0:1, 512:1024]))
                        mk(f"recipB_{g}_{hp}_{qt}")
                        wt("tensor", f"recipB_{g}_{hp}_{qt}", "VEC")
                        op("MM", lambda e: e.matmul(pmisc[64:128, :],
                                                    cb["ones_row1"][0:1, 0:64],
                                                    rb16[0:1, 0:512],
                                                    start=True, stop=True,
                                                    tile_position=(0, 64)))
                        mk(f"bc_{g}_{hp}_{qt}")
                        wt("vector", f"bc_{g}_{hp}_{qt}", "MM")
                        wt("vector", f"unn_{g}_{hp}_{qt}", "ACT")
                        op("VEC", lambda e, g=g, hp=hp, qt=qt: e.tensor_tensor(
                            agf[:, g * 2 + hp, qt * 512:(qt + 1) * 512], unn[:],
                            pmisc[:, :], OP.mult))
                        mk(f"norm_{g}_{hp}_{qt}")
                        maybe_cut(f"norm_{g}_{hp}_{qt}")
                mk(f"lastexp_{g}")
                mk(f"lastdone_{g}")

            wo = bfv(O_WO, [8, 1024])
            h1b = bfv(O_H1B, [1024])
            xres = fv(F_XRES, [2, 1024])
            h1t = fv(F_H1T, [1024])
            scal = fv(F_SCAL, [16])
            wt("sync", f"projMM_{NG - 1}_{NTC - 1}", "MM")
            for fc in range(8):
                op("WLD", lambda e, fc=fc: e.dma_start(
                    out=wo[:, fc, :], in_=d["wo_c"][fc * 128:(fc + 1) * 128, :]))
            mk("wo_ld")
            for t32 in range(32):
                par = t32 % 2
                if t32 >= 2:
                    wt("sync", f"h1add_{t32 - 2}", "VEC")
                op("LD", lambda e, t32=t32, par=par: e.dma_start(
                    out=xres[:, par, :], in_=d["x_tm"][t32 * 128:(t32 + 1) * 128, :]))
                mk(f"xtm_{t32}")
                wt("tensor", "wo_ld", "WLD")
                if t32 == 0:
                    wt("tensor", f"lastdone_{NG - 1}", "VEC")
                else:
                    wt("tensor", f"h1add_{t32 - 1}", "VEC")
                for oc in range(2):
                    for fc in range(8):
                        def bo(e, oc=oc, fc=fc, t32=t32):
                            return e.matmul(
                                pavd[:, oc * 512:(oc + 1) * 512],
                                agf[:, fc, t32 * 128:(t32 + 1) * 128],
                                wo[:, fc, oc * 512:(oc + 1) * 512],
                                start=(fc == 0), stop=(fc == 7))
                        if fc == 7 and oc == 1:
                            op("MM", bo)
                        else:
                            opn("MM", bo)
                mk(f"opmm_{t32}")
                wt("vector", f"opmm_{t32}", "MM")
                wt("vector", f"xtm_{t32}", "LD")
                if t32 > 0:
                    wt("vector", f"h1sq_{t32 - 1}", "ACT")
                op("VEC", lambda e, par=par: e.tensor_tensor(h1t[:], xres[:, par, :],
                                                             pavd[:, 0:1024], OP.add))
                op("VEC", lambda e: e.tensor_reduce(scal[:, 0:1], h1t[:], AX.X, OP.add))
                mk(f"h1add_{t32}")
                wt("scalar", f"h1add_{t32}", "VEC")
                op("ACT", lambda e: e.activation(h1b[:], h1t[:], AF.Square, 0.0, 1.0,
                                                 accum_out=scal[:, 1:2]))
                mk(f"h1sq_{t32}")
                wt("vector", f"h1sq_{t32}", "ACT")
                op("VEC", lambda e: e.tensor_scalar(scal[:, 2:3], scal[:, 0:1],
                                                    1.0 / D, None, OP.mult))
                op("VEC", lambda e: e.tensor_tensor(scal[:, 3:4], scal[:, 2:3],
                                                    scal[:, 2:3], OP.mult))
                op("VEC", lambda e: e.tensor_scalar(scal[:, 4:5], scal[:, 1:2],
                                                    1.0 / D, None, OP.mult))
                op("VEC", lambda e: e.tensor_tensor(scal[:, 5:6], scal[:, 4:5],
                                                    scal[:, 3:4], OP.subtract))
                op("VEC", lambda e: e.tensor_scalar(scal[:, 6:7], scal[:, 5:6],
                                                    1e-5, None, OP.add))
                mk(f"h1v_{t32}")
                wt("scalar", f"h1v_{t32}", "VEC")
                op("ACT", lambda e: e.activation(scal[:, 7:8], scal[:, 6:7],
                                                 AF.Sqrt, 0.0, 1.0))
                mk(f"h1sd_{t32}")
                wt("vector", f"h1sd_{t32}", "ACT")
                op("VEC", lambda e: e.reciprocal(scal[:, 8:9], scal[:, 7:8]))
                op("VEC", lambda e: e.tensor_scalar(scal[:, 9:10], scal[:, 2:3],
                                                    scal[:, 8:9], -1.0, OP.mult, OP.mult))
                mk(f"h1rs_{t32}")
                wt("scalar", f"h1rs_{t32}", "VEC")
                if t32 > 0:
                    wt("vector", f"h1st_{t32 - 1}", "SST")
                op("VEC", lambda e: e.tensor_scalar(fv(F_TMP, [1024])[:, :],
                                                    h1t[:], scal[:, 8:9],
                                                    scal[:, 9:10], OP.mult, OP.add))
                mk(f"opev_{t32}")
                op("VEC", lambda e: e.tensor_copy(h1b[:], fv(F_TMP, [1024])[:, :]))
                mk(f"h1b_{t32}")
                wt("sync", f"opev_{t32}", "VEC")
                op("SST", lambda e, t32=t32: e.dma_start(
                    out=d["h1_tm_d"][t32 * 128:(t32 + 1) * 128, :],
                    in_=fv(F_TMP, [1024])[:, :]))
                if "dbg_h1" in dbg:
                    op("SST", lambda e, t32=t32: e.dma_start(
                        out=d["dbg_h1"][t32 * 128:(t32 + 1) * 128, :],
                        in_=fv(F_TMP, [1024])[:, :]))
                wt("sync", f"h1b_{t32}", "VEC")
                op("SST", lambda e, t32=t32: e.dma_start(
                    out=d["h1_pub"][t32 * 128:(t32 + 1) * 128, :], in_=h1b[:]))
                mk(f"h1st_{t32}")
                maybe_cut(f"h1_{t32}")
            mk("h1_done")
            maybe_cut("h1only")

            h1fm = bfv(O_H1FM, [8, S])
            shact = bfv(O_SHACT2, [4, S])
            disp = bfv(O_DISP2, [2, 8, 512])
            wgu_sb = bfv(FB_WGU, [8, 1024])
            shdn_sb = bfv(FB_WGU, [4, 1024])
            wdn_sb = bfv(FB_WDN, [4, 1024])
            acte = bfv(FB_ACTE, [4, 1024])
            eost = bfv(FB_EOST, [1024])
            silb = bfv(FB_PAY, [1024])
            pay = bfv(FB_PAY, [2, 4, 128])
            ohb = bfv(FB_OH, [4, 32])
            decb = bfv(FB_DEC, [64, 2])
            gwsb = bfv(FB_GW, [8, 32])
            postm = fv(F_POS, [32, 32])
            iw = fv(F_IW, [32, 8])
            gsv = fv(F_GS, [4, 32])
            gcv = fv(F_GC, [4, 32])
            ftt = fv(F_TT, [128])
            ofsb = fv(F_OFS, [32])
            hlv = fv(F_HL, [2, 32])
            keptv = fv(F_KEPT, [4, 32])
            ltile = fv(F_TMP, [1024])
            h1tf = fv(F_H1TF, [1024])
            xrf = fv(F_XRESF, [1024])
            outf = fv(F_TMPF, [1024])

            wt("gpsimd", "h1_done", "SST")
            op("COL", lambda e: e.collective_compute(
                "AllGather", OP.bypass, RGRP, [d["h1_pub"][:]], [d["h1_all"][:]]))
            mk("ag_h1")
            op("XLD", lambda e: e.dma_start(
                out=gwsb[:, :, :],
                in_=d["gate_wT"][:].rearrange("(a p) b -> p a b", p=128)))
            mk("gw_ld")
            op("XLD", lambda e: e.dma_start(
                out=wgu_sb[:, :, :],
                in_=d["sh_gu"][:].rearrange("(a p) b -> p a b", p=128)))
            mk("shgu_ld")
            wt("vector", "h1_done", "SST")
            op("VEC", lambda e: e.memset(AB[:, FB_GK:FB_GK + 4096], 0.0))
            mk("zb")
            wt("gpsimd", "zb", "VEC")
            for i in range(8):
                op("GLD", lambda e, i=i: e.dma_start(
                    out=d["slot_loc"][i * 4096:(i + 1) * 4096, :].rearrange(
                        "(p a) f -> p (a f)", p=128),
                    in_=AB[:, FB_GK:FB_GK + 4096]))
            mk("slotz")

            wt("sync", "opmm_31", "MM")
            wt("sync", "h1_done", "SST")
            for fc in range(8):
                op("LD", lambda e, fc=fc: e.dma_start(
                    out=h1fm[:, fc, :],
                    in_=d["h1_pub"][:, fc * 128:(fc + 1) * 128], transpose=True))
            mk("h1fm_ld")
            maybe_cut("h1fm")

            for ti in range(32):
                wt("tensor", "h1fm_ld", "LD")
                wt("tensor", "gw_ld", "XLD")
                if ti:
                    wt("tensor", f"lgev_{ti - 1}", "ACT")
                for fc in range(8):
                    def bl(e, fc=fc, ti=ti):
                        return e.matmul(pmisc[:, 0:32],
                                        h1fm[:, fc, ti * 128:(ti + 1) * 128],
                                        gwsb[:, fc, :],
                                        start=(fc == 0), stop=(fc == 7))
                    if fc == 7:
                        op("MM", bl)
                    else:
                        opn("MM", bl)
                mk(f"lgmm_{ti}")
                wt("scalar", f"lgmm_{ti}", "MM")
                if ti:
                    wt("scalar", f"top_{ti - 1}", "VEC")
                op("ACT", lambda e: e.copy(ltile[:, 0:32], pmisc[:, 0:32]))
                mk(f"lgev_{ti}")
                wt("vector", f"lgev_{ti}", "ACT")
                op("VEC", lambda e: e.max(ltile[:, 32:40], ltile[:, 0:32]))
                op("VEC", lambda e: e.max_index(idxu[:, 0:8], ltile[:, 32:40],
                                                ltile[:, 0:32]))
                op("VEC", lambda e, ti=ti: e.tensor_copy(iw[:, ti, 0:4], idxu[:, 0:4]))
                op("VEC", lambda e: e.tensor_scalar(ltile[:, 40:41], ltile[:, 32:33],
                                                    -1.0, None, OP.mult))
                mk(f"topv_{ti}")
                wt("scalar", f"topv_{ti}", "VEC")
                op("ACT", lambda e: e.activation(ltile[:, 44:48], ltile[:, 32:36],
                                                 AF.Exp, ltile[:, 40:41], 1.0))
                mk(f"tope_{ti}")
                wt("vector", f"tope_{ti}", "ACT")
                op("VEC", lambda e: e.tensor_reduce(ltile[:, 41:42], ltile[:, 44:48],
                                                    AX.X, OP.add))
                op("VEC", lambda e: e.reciprocal(ltile[:, 42:43], ltile[:, 41:42]))
                op("VEC", lambda e, ti=ti: e.tensor_scalar(iw[:, ti, 4:8],
                                                           ltile[:, 44:48],
                                                           ltile[:, 42:43], None,
                                                           OP.mult))
                mk(f"top_{ti}")
            mk("route_done")
            maybe_cut("route")

            for ti in range(32):
                wt("vector", "route_done", "VEC")
                if ti:
                    wt("vector", f"posev_{ti - 1}", "ACT")
                for k in range(4):
                    op("VEC", lambda e, ti=ti, k=k: e.tensor_scalar(
                        ohb[:, k, :], cb["iota32"][:], iw[:, ti, k:k + 1], None,
                        OP.is_equal))
                op("VEC", lambda e: e.tensor_tensor(ohb[:, 0, :], ohb[:, 0, :],
                                                    ohb[:, 1, :], OP.add))
                op("VEC", lambda e: e.tensor_tensor(ohb[:, 2, :], ohb[:, 2, :],
                                                    ohb[:, 3, :], OP.add))
                op("VEC", lambda e: e.tensor_tensor(ohb[:, 0, :], ohb[:, 0, :],
                                                    ohb[:, 2, :], OP.add))
                mk(f"oh_{ti}")
                wt("tensor", f"oh_{ti}", "VEC")
                if ti:
                    wt("tensor", f"runq_{ti - 1}", "VEC")
                op("MM", lambda e: e.matmul(pmisc[0:32, 32:164], ohb[:, 0, :],
                                            cb["trilT"][:], start=True, stop=True))
                mk(f"pre_{ti}")
                wt("vector", f"pre_{ti}", "MM")
                if ti == 0:
                    op("VEC", lambda e: e.memset(ltile[0:32, 50:51], 0.0))
                op("VEC", lambda e: e.tensor_scalar(ltile[0:32, 52:180],
                                                    pmisc[0:32, 32:160],
                                                    ltile[0:32, 50:51], None, OP.add))
                op("VEC", lambda e: e.tensor_tensor(ltile[0:32, 50:51],
                                                    ltile[0:32, 50:51],
                                                    pmisc[0:32, 160:161], OP.add))
                mk(f"runq_{ti}")
                wt("tensor", f"runq_{ti}", "VEC")
                op("MM", lambda e: e.transpose(pavd[0:128, 0:32],
                                               ltile[0:32, 52:180],
                                               cb["identf"][0:32, 0:32]))
                mk(f"post_{ti}")
                wt("scalar", f"post_{ti}", "MM")
                op("ACT", lambda e, ti=ti: e.copy(postm[:, ti, :], pavd[:, 0:32]))
                mk(f"posev_{ti}")
            wt("sync", "runq_31", "VEC")
            op("SST", lambda e: e.dma_start(
                out=d["cnt_loc"][:].rearrange("a b -> b a"),
                in_=ltile[0:32, 50:51]))
            mk("cnt_st")
            wt("gpsimd", "cnt_st", "SST")
            op("COL", lambda e: e.collective_compute(
                "AllGather", OP.bypass, RGRP, [d["cnt_loc"][:]], [d["cnt_all"][:]]))
            mk("ag_cnt")
            wt("sync", "ag_cnt", "COL")
            op("LD", lambda e: e.dma_start(
                out=ltile[0:32, 56:64],
                in_=d["cnt_all"][:].rearrange("c e -> e c"),
                allow_slow_non_contiguous=True))
            mk("cnt_ld")
            wt("vector", "cnt_ld", "LD")
            op("VEC", lambda e: e.tensor_tensor(ltile[0:32, 56:64], ltile[0:32, 56:64],
                                                cb["prevmask"][:], OP.mult))
            op("VEC", lambda e: e.tensor_reduce(ltile[0:32, 64:65], ltile[0:32, 56:64],
                                                AX.X, OP.add))
            mk("offs_v")
            wt("tensor", "offs_v", "VEC")
            wt("tensor", "posev_31", "ACT")
            op("MM", lambda e: e.transpose(pavd[0:1, 32:64], ltile[0:32, 64:65],
                                           cb["identf"][0:32, 0:32]))
            mk("offs_t")
            wt("scalar", "offs_t", "MM")
            op("ACT", lambda e: e.copy(ftt[0:1, 0:32], pavd[0:1, 32:64]))
            mk("offs_row")
            wt("tensor", "offs_row", "ACT")
            op("MM", lambda e: e.matmul(pmisc[:, 192:224], cb["ones_row1f"][:],
                                        ftt[0:1, 0:32], start=True, stop=True))
            mk("offs_bc")
            wt("scalar", "offs_bc", "MM")
            op("ACT", lambda e: e.copy(ofsb[:, :], pmisc[:, 192:224]))
            mk("offs_done")
            wt("vector", "offs_done", "ACT")
            mk("hl_done")
            maybe_cut("offs")

            for ti in range(32):
                wt("vector", "hl_done", "VEC")
                for k in range(4):
                    op("VEC", lambda e, ti=ti, k=k: e.tensor_scalar(
                        ohb[:, k, :], cb["iota32"][:], iw[:, ti, k:k + 1], None,
                        OP.is_equal))
                op("VEC", lambda e, ti=ti: e.tensor_tensor(
                    ltile[:, 96:128], postm[:, ti, :], ofsb[:, :], OP.add))
                for k in range(4):
                    op("VEC", lambda e, k=k: e.tensor_tensor(
                        ltile[:, 128:160], ltile[:, 96:128], ohb[:, k, :], OP.mult))
                    op("VEC", lambda e, k=k: e.tensor_reduce(
                        ltile[:, 160 + k:161 + k], ltile[:, 128:160], AX.X, OP.add))
                op("VEC", lambda e: e.tensor_scalar(ltile[:, 164:168],
                                                    ltile[:, 160:164], float(CAP),
                                                    None, OP.is_lt))
                op("VEC", lambda e, ti=ti: e.tensor_copy(keptv[:, 0:4, ti],
                                                         ltile[:, 164:168]))
                op("VEC", lambda e, ti=ti: e.tensor_tensor(
                    iw[:, ti, 4:8], iw[:, ti, 4:8], ltile[:, 164:168], OP.mult))
                op("VEC", lambda e, ti=ti: e.scalar_tensor_tensor(
                    ltile[:, 184:188], iw[:, ti, 0:4], float(CAP),
                    ltile[:, 160:164], OP.mult, OP.add))
                op("VEC", lambda e: e.tensor_tensor(ltile[:, 188:192],
                                                    ltile[:, 184:188],
                                                    ltile[:, 164:168], OP.mult))
                op("VEC", lambda e, ti=ti: e.tensor_copy(gcv[:, 0:4, ti],
                                                         ltile[:, 188:192]))
                op("VEC", lambda e: e.tensor_scalar(ltile[:, 184:188],
                                                    ltile[:, 184:188], 32767.0,
                                                    None, OP.min))
                op("VEC", lambda e, ti=ti: e.tensor_copy(gsv[:, 0:4, ti],
                                                         ltile[:, 184:188]))
                mk(f"gidx_{ti}")
            mk("gidx_done")
            maybe_cut("gidx")

            def fold_pass(src, base_slot, extra_xld_wait):
                for k in range(4):
                    sl = base_slot + k
                    wt("tensor", "gidx_done", "VEC")
                    if sl:
                        wt("tensor", f"fev_{sl - 1}", "ACT")
                    op("MM", lambda e, src=src, k=k: e.transpose(
                        pavd[0:32, 64:192], src[:, k, :], cb["identf"][:, 0:128]))
                    mk(f"ftr_{sl}")
                    wt("scalar", f"ftr_{sl}", "MM")
                    if sl:
                        wt("scalar", f"fst_{sl - 1}", "SST")
                    op("ACT", lambda e: e.copy(
                        ftt[0:32, :].rearrange("p (q a) -> p a q", q=16),
                        pavd[0:32, 64:192].rearrange("p (a q) -> p a q", a=8)))
                    mk(f"fev_{sl}")
                    wt("sync", f"fev_{sl}", "ACT")
                    op("SST", lambda e, sl=sl: e.dma_start(
                        out=d["rt_scr"][sl].rearrange("(q t a) -> t q a", q=16, a=8),
                        in_=ftt[0:32, :].rearrange("p (q a) -> p q a", q=16)))
                    mk(f"fst_{sl}")
                    wt("gpsimd", f"fst_{sl}", "SST")
                    if extra_xld_wait:
                        wt("gpsimd", extra_xld_wait, "GST")
                    for grp in range(8):
                        op("XLD", lambda e, sl=sl, grp=grp, k=k: e.dma_start(
                            out=idx16[grp * 16:(grp + 1) * 16,
                                      k * 256:(k + 1) * 256],
                            in_=d["rt_scr"][sl].rearrange("(p c) -> p c", p=16)))
                    mk(f"fld_{sl}")

            fold_pass(gsv, 0, None)
            mk("fold_done")
            maybe_cut("fold")

            for k in range(4):
                wt("vector", "hl_done", "VEC")
                wt("vector", "gidx_done", "VEC")
                op("VEC", lambda e, k=k: e.tensor_tensor(
                    ltile[:, 224:256], cb["gid_hi"][:], keptv[:, k, :], OP.mult))
                op("VEC", lambda e, k=k: e.tensor_tensor(
                    ltile[:, 256:288], cb["gid_lo"][:], keptv[:, k, :], OP.mult))
                mk(f"hlk_{k}")
                for jj in range(8):
                    par = jj % 2
                    if k or jj >= 2:
                        pk, pj = (k, jj - 2) if jj >= 2 else (k - 1, jj + 6)
                        wt("vector", f"scat_{pk}_{pj}", "GST")
                    op("VEC", lambda e, par=par: e.memset(pay[:, par, :, :], 0.0))
                    op("VEC", lambda e, par=par, jj=jj: e.tensor_copy(
                        pay[:, par, :, 0:1],
                        ltile[:, 224 + jj * 4:224 + (jj + 1) * 4].rearrange(
                            "p (a o) -> p a o", o=1)))
                    op("VEC", lambda e, par=par, jj=jj: e.tensor_copy(
                        pay[:, par, :, 1:2],
                        ltile[:, 256 + jj * 4:256 + (jj + 1) * 4].rearrange(
                            "p (a o) -> p a o", o=1)))
                    mk(f"payb_{k}_{jj}")
                    wt("gpsimd", f"payb_{k}_{jj}", "VEC")
                    wt("gpsimd", "slotz", "GLD")
                    wt("gpsimd", f"fld_{k}", "XLD")
                    op("GST", lambda e, k=k, jj=jj, par=par: e.dma_scatter_add(
                        d["slot_loc"][:], pay[:, par, :, :],
                        idx16[:, k * 256 + jj * 32:k * 256 + (jj + 1) * 32],
                        512, 512, 128))
                    mk(f"scat_{k}_{jj}")
            mk("scat_done")
            wt("gpsimd", "scat_done", "GST")
            op("COL", lambda e: e.collective_compute(
                "ReduceScatter", OP.add, RGRP, [d["slot_loc"][:]], [d["slot_rs"][:]]))
            mk("rs_slot")
            fold_pass(gcv, 4, "scat_done")
            mk("cfold_done")
            maybe_cut("scat")

            for tc in range(8):
                for oc in range(4):
                    wt("tensor", "shgu_ld", "XLD")
                    wt("tensor", "h1fm_ld", "LD")
                    if tc or oc:
                        po, pt = (oc - 1, tc) if oc else (3, tc - 1)
                        wt("tensor", f"shmul_{pt}_{po}", "VEC")
                    for half, pp in ((0, ps0), (1, ps1)):
                        for fc in range(8):
                            def bg(e, oc=oc, fc=fc, tc=tc, half=half, pp=pp):
                                return e.matmul(
                                    pp[:, 0:512],
                                    wgu_sb[:, fc,
                                           half * 512 + oc * 128:
                                           half * 512 + (oc + 1) * 128],
                                    h1fm[:, fc, tc * 512:(tc + 1) * 512],
                                    start=(fc == 0), stop=(fc == 7))
                            if fc == 7 and half == 1:
                                op("MM", bg)
                            else:
                                opn("MM", bg)
                    mk(f"shmm_{tc}_{oc}")
                    wt("scalar", f"shmm_{tc}_{oc}", "MM")
                    if tc == 0 and oc == 0:
                        wt("scalar", "scat_done", "GST")
                    if tc or oc:
                        po, pt = (oc - 1, tc) if oc else (3, tc - 1)
                        wt("scalar", f"shmul_{pt}_{po}", "VEC")
                    op("ACT", lambda e: e.activation(silb[:, 0:512], ps0[:, 0:512],
                                                     AF.Silu, 0.0, 1.0))
                    mk(f"shsl_{tc}_{oc}")
                    wt("vector", f"shsl_{tc}_{oc}", "ACT")
                    op("VEC", lambda e, tc=tc, oc=oc: e.tensor_tensor(
                        shact[:, oc, tc * 512:(tc + 1) * 512], silb[:, 0:512],
                        ps1[:, 0:512], OP.mult))
                    mk(f"shmul_{tc}_{oc}")
            mk("sh_done")
            maybe_cut("shared")

            for j in range(EPC):
                wt("sync", "rs_slot", "COL")
                if j:
                    wt("sync", f"dec_v_{j - 1}", "VEC")
                op("LD", lambda e, j=j: e.dma_start(
                    out=decb[0:16, :, :],
                    in_=d["slot_rs"][j * CAP:(j + 1) * CAP, 0:2].rearrange(
                        "(c p) x -> p c x", p=16)))
                mk(f"dec_ld_{j}")
                wt("vector", f"dec_ld_{j}", "LD")
                if j:
                    wt("vector", f"dec_st_{j - 1}", "SST")
                op("VEC", lambda e: e.scalar_tensor_tensor(
                    ltile[0:16, 200:264], decb[0:16, :, 0], 128.0,
                    decb[0:16, :, 1], OP.mult, OP.add))
                op("VEC", lambda e: e.tensor_scalar(
                    ltile[0:16, 200:264], ltile[0:16, 200:264], -1.0, 0.0,
                    OP.add, OP.max))
                mk(f"dec_v_{j}")
                wt("sync", f"dec_v_{j}", "VEC")
                op("SST", lambda e, j=j: e.dma_start(
                    out=d["rt_scr"][8 + j, 0:1024].rearrange("(p c) -> p c", p=16),
                    in_=ltile[0:16, 200:264]))
                mk(f"dec_st_{j}")
                wt("gpsimd", f"dec_st_{j}", "SST")
                for grp in range(8):
                    op("XLD", lambda e, j=j, grp=grp: e.dma_start(
                        out=idx16[grp * 16:(grp + 1) * 16,
                                  1024 + j * 64:1024 + (j + 1) * 64],
                        in_=d["rt_scr"][8 + j, 0:1024].rearrange("(p c) -> p c",
                                                                 p=16)))
                mk(f"dec_ix_{j}")
            mk("dec_done")
            maybe_cut("decode")

            for j in range(EPC):
                wt("gpsimd", "sh_done" if j == 0 else f"egmm_{j - 1}_1_3", "MM")
                op("XLD", lambda e, j=j: e.dma_start(
                    out=wgu_sb[:, :, :],
                    in_=d["w_gu"][j].rearrange("(a p) b -> p a b", p=128)))
                mk(f"ewgu_{j}")
                if j == 0:
                    wt("gpsimd", "slotz", "GLD")
                else:
                    wt("gpsimd", f"edmm_{j - 1}_7", "MM")
                op("XLD", lambda e, j=j: e.dma_start(
                    out=wdn_sb[:, :, :],
                    in_=d["w_dn"][j].rearrange("(a p) b -> p a b", p=128)))
                mk(f"ewdn_{j}")
                wt("gpsimd", f"dec_ix_{j}", "XLD")
                wt("gpsimd", "ag_h1", "COL")
                if j:
                    wt("gpsimd", f"egmm_{j - 1}_1_3", "MM")
                for hf in range(2):
                    op("GLD", lambda e, j=j, hf=hf: e.dma_gather(
                        disp[:, hf, :, :], d["h1_all"][:],
                        idx16[:, 1024 + j * 64 + hf * 32:
                              1024 + j * 64 + (hf + 1) * 32],
                        512, 512, 1024, transpose=True))
                mk(f"dgath_{j}")
                for hf in range(2):
                    for oc in range(4):
                        wt("tensor", f"ewgu_{j}", "XLD")
                        wt("tensor", f"dgath_{j}", "GLD")
                        if hf or oc:
                            po, ph = (oc - 1, hf) if oc else (3, hf - 1)
                            wt("tensor", f"emul_{j}_{ph}_{po}", "VEC")
                        elif j:
                            wt("tensor", f"emul_{j - 1}_1_3", "VEC")
                        for half, pp in ((0, ps0), (1, ps1)):
                            for fc in range(8):
                                def bg(e, oc=oc, fc=fc, hf=hf, half=half, pp=pp):
                                    return e.matmul(
                                        pp[:, 0:512],
                                        wgu_sb[:, fc,
                                               half * 512 + oc * 128:
                                               half * 512 + (oc + 1) * 128],
                                        disp[:, hf, fc, :],
                                        start=(fc == 0), stop=(fc == 7))
                                if fc == 7 and half == 1:
                                    op("MM", bg)
                                else:
                                    opn("MM", bg)
                        mk(f"egmm_{j}_{hf}_{oc}")
                        wt("scalar", f"egmm_{j}_{hf}_{oc}", "MM")
                        if hf or oc:
                            po, ph = (oc - 1, hf) if oc else (3, hf - 1)
                            wt("scalar", f"emul_{j}_{ph}_{po}", "VEC")
                        op("ACT", lambda e: e.activation(silb[:, 0:512],
                                                         ps0[:, 0:512],
                                                         AF.Silu, 0.0, 1.0))
                        mk(f"esil_{j}_{hf}_{oc}")
                        wt("vector", f"esil_{j}_{hf}_{oc}", "ACT")
                        op("VEC", lambda e, hf=hf, oc=oc: e.tensor_tensor(
                            acte[:, oc, hf * 512:(hf + 1) * 512], silb[:, 0:512],
                            ps1[:, 0:512], OP.mult))
                        mk(f"emul_{j}_{hf}_{oc}")
                for g in range(8):
                    wt("tensor", f"ewdn_{j}", "XLD")
                    wt("tensor", f"emul_{j}_1_3", "VEC")
                    if g:
                        wt("tensor", f"eoev_{j}_{g - 1}", "ACT")
                    elif j:
                        wt("tensor", f"eoev_{j - 1}_7", "ACT")
                    for oc in range(2):
                        for ic in range(4):
                            def bd(e, g=g, oc=oc, ic=ic):
                                return e.matmul(
                                    pavd[:, oc * 512:(oc + 1) * 512],
                                    acte[:, ic, g * 128:(g + 1) * 128],
                                    wdn_sb[:, ic, oc * 512:(oc + 1) * 512],
                                    start=(ic == 0), stop=(ic == 3))
                            if oc == 1 and ic == 3:
                                op("MM", bd)
                            else:
                                opn("MM", bd)
                    mk(f"edmm_{j}_{g}")
                    wt("scalar", f"edmm_{j}_{g}", "MM")
                    if g:
                        wt("scalar", f"eo_st_{j}_{g - 1}", "SST")
                    elif j:
                        wt("scalar", f"eo_st_{j - 1}_7", "SST")
                    op("ACT", lambda e: e.copy(eost[:, :], pavd[:, 0:1024]))
                    mk(f"eoev_{j}_{g}")
                    wt("sync", f"eoev_{j}_{g}", "ACT")
                    op("SST", lambda e, j=j, g=g: e.dma_start(
                        out=d["eo_loc"][(j * 8 + g) * 128:(j * 8 + g + 1) * 128, :],
                        in_=eost[:, :]))
                    mk(f"eo_st_{j}_{g}")
            mk("eo_done")
            wt("gpsimd", "eo_st_3_7", "SST")
            op("COL", lambda e: e.collective_compute(
                "AllGather", OP.bypass, RGRP, [d["eo_loc"][:]], [d["eo_all"][:]]))
            mk("ag_eo")
            maybe_cut("experts")

            wt("gpsimd", "egmm_3_1_3", "MM")
            op("XLD", lambda e: e.dma_start(
                out=shdn_sb[:, :, :],
                in_=d["sh_dn"][:].rearrange("(a p) b -> p a b", p=128)))
            mk("shdn_ld")
            for j in range(8):
                for k in range(4):
                    par = k % 2
                    gk = bfv(FB_GK + par * 4096, [4, 1024])
                    wt("gpsimd", "ag_eo", "COL")
                    wt("gpsimd", "cfold_done", "XLD")
                    if j or k >= 2:
                        pj, pk2 = (j, k - 2) if k >= 2 else (j - 1, k + 2)
                        wt("gpsimd", f"cacc_{pj}_{pk2}", "VEC")
                    if j == 0 and k < 2:
                        wt("gpsimd", f"edmm_3_7", "MM")
                    op("GLD", lambda e, j=j, k=k, gk=gk: e.dma_gather(
                        gk[:, :, :], d["eo_all"][:],
                        idx16[:, k * 256 + j * 32:k * 256 + (j + 1) * 32],
                        512, 512, 1024))
                    mk(f"cg_{j}_{k}")
                    wt("vector", f"cg_{j}_{k}", "GLD")
                    if k == 0 and j:
                        wt("vector", f"fln_{j * 4 - 1}", "VEC")
                    for b in range(4):
                        accb = fv(F_ACC0 + b * 1024, [1024])
                        if k == 0:
                            op("VEC", lambda e, j=j, k=k, b=b, gk=gk, accb=accb:
                               e.tensor_scalar(accb[:, :], gk[:, b, :],
                                               iw[:, j * 4 + b, 4 + k:5 + k],
                                               None, OP.mult))
                        else:
                            op("VEC", lambda e, j=j, k=k, b=b, gk=gk, accb=accb:
                               e.scalar_tensor_tensor(accb[:, :], gk[:, b, :],
                                                      iw[:, j * 4 + b, 4 + k:5 + k],
                                                      accb[:, :], OP.mult, OP.add))
                    mk(f"cacc_{j}_{k}")
                for b in range(4):
                    t32 = j * 4 + b
                    accb = fv(F_ACC0 + b * 1024, [1024])
                    wt("tensor", "shdn_ld", "XLD")
                    wt("tensor", "sh_done", "VEC")
                    if t32:
                        wt("tensor", f"fadd_{t32 - 1}", "VEC")
                    for oc in range(2):
                        for ic in range(4):
                            def bsd(e, t32=t32, oc=oc, ic=ic):
                                return e.matmul(
                                    pavd[:, oc * 512:(oc + 1) * 512],
                                    shact[:, ic, t32 * 128:(t32 + 1) * 128],
                                    shdn_sb[:, ic, oc * 512:(oc + 1) * 512],
                                    start=(ic == 0), stop=(ic == 3))
                            if oc == 1 and ic == 3:
                                op("MM", bsd)
                            else:
                                opn("MM", bsd)
                    mk(f"cdmm_{t32}")
                    if t32:
                        wt("sync", f"fadd_{t32 - 1}", "VEC")
                    else:
                        wt("sync", "cfold_done", "ACT")
                    op("LD", lambda e, t32=t32: e.dma_start(
                        out=xrf[:, :],
                        in_=d["h1_tm_d"][t32 * 128:(t32 + 1) * 128, :]))
                    mk(f"cxld_{t32}")
                    wt("vector", f"cdmm_{t32}", "MM")
                    wt("vector", f"cxld_{t32}", "LD")
                    if t32:
                        wt("vector", f"fsq_{t32 - 1}", "ACT")
                    op("VEC", lambda e: e.tensor_tensor(h1tf[:, :], xrf[:, :],
                                                        pavd[:, 0:1024], OP.add))
                    op("VEC", lambda e, accb=accb: e.tensor_tensor(
                        h1tf[:, :], h1tf[:, :], accb[:, :], OP.add))
                    op("VEC", lambda e: e.tensor_reduce(scal[:, 0:1], h1tf[:, :],
                                                        AX.X, OP.add))
                    mk(f"fadd_{t32}")
                    wt("scalar", f"fadd_{t32}", "VEC")
                    op("ACT", lambda e: e.activation(eost[:, :], h1tf[:, :],
                                                     AF.Square, 0.0, 1.0,
                                                     accum_out=scal[:, 1:2]))
                    mk(f"fsq_{t32}")
                    wt("vector", f"fsq_{t32}", "ACT")
                    op("VEC", lambda e: e.tensor_scalar(scal[:, 2:3], scal[:, 0:1],
                                                        1.0 / D, None, OP.mult))
                    op("VEC", lambda e: e.tensor_tensor(scal[:, 3:4], scal[:, 2:3],
                                                        scal[:, 2:3], OP.mult))
                    op("VEC", lambda e: e.tensor_scalar(scal[:, 4:5], scal[:, 1:2],
                                                        1.0 / D, None, OP.mult))
                    op("VEC", lambda e: e.tensor_tensor(scal[:, 5:6], scal[:, 4:5],
                                                        scal[:, 3:4], OP.subtract))
                    op("VEC", lambda e: e.tensor_scalar(scal[:, 6:7], scal[:, 5:6],
                                                        1e-5, None, OP.add))
                    mk(f"fv_{t32}")
                    wt("scalar", f"fv_{t32}", "VEC")
                    op("ACT", lambda e: e.activation(scal[:, 7:8], scal[:, 6:7],
                                                     AF.Sqrt, 0.0, 1.0))
                    mk(f"fsd_{t32}")
                    wt("vector", f"fsd_{t32}", "ACT")
                    op("VEC", lambda e: e.reciprocal(scal[:, 8:9], scal[:, 7:8]))
                    op("VEC", lambda e: e.tensor_scalar(scal[:, 9:10], scal[:, 2:3],
                                                        scal[:, 8:9], -1.0,
                                                        OP.mult, OP.mult))
                    mk(f"frs_{t32}")
                    wt("scalar", f"frs_{t32}", "VEC")
                    if t32:
                        wt("vector", f"out_st_{t32 - 1}", "SST")
                    op("VEC", lambda e: e.tensor_scalar(outf[:, :], h1tf[:, :],
                                                        scal[:, 8:9], scal[:, 9:10],
                                                        OP.mult, OP.add))
                    mk(f"fln_{t32}")
                    wt("sync", f"fln_{t32}", "VEC")
                    op("SST", lambda e, t32=t32: e.dma_start(
                        out=d["out"][t32 * 128:(t32 + 1) * 128, :],
                        in_=outf[:, :]))
                    mk(f"out_st_{t32}")
            mk("ffn_done")

        def run_prog():
            try:
                program()
            except _Cut:
                for t32 in range(32):
                    op("SST", lambda e, t32=t32: e.dma_start(
                        out=d["out"][t32 * 128:(t32 + 1) * 128, :],
                        in_=d["x_tm"][t32 * 128:(t32 + 1) * 128, :]))
            if state["emit"] and state["en"] == "sync":
                for kk in sorted(val):
                    if val[kk] and kk != "SST":
                        state["e"].wait_ge(sems[kk], val[kk])
            if "dbg_ag" in dbg:
                for i in range(8):
                    op("SST", lambda e, i=i: e.dma_start(
                        out=d["dbg_ag"][:, i * S:(i + 1) * S],
                        in_=AB[:, i * S:(i + 1) * S]))
                op("SST", lambda e: e.dma_start(out=d["dbg_f32"][:], in_=A32[:]))
            if "dbg_cnt" in dbg:
                op("SST", lambda e: e.dma_start(out=d["dbg_cnt"][:],
                                                in_=d["cnt_all"][:]))
                op("SST", lambda e: e.dma_start(out=d["dbg_srs"][:],
                                                in_=d["slot_rs"][:]))
                op("SST", lambda e: e.dma_start(out=d["dbg_eo"][:],
                                                in_=d["eo_loc"][:]))
                op("SST", lambda e: e.dma_start(out=d["dbg_rt"][0:8],
                                                in_=d["rt_scr"][0:8]))
                op("SST", lambda e: e.dma_start(out=d["dbg_rt"][8:12, 0:1024],
                                                in_=d["rt_scr"][8:12, 0:1024]))
            if state["emit"] and state["en"] == "sync":
                state["e"].wait_ge(sems["SST"], val["SST"])

        run_prog()
        state["emit"] = True
        final_counts = dict(val)

        def run_engine(en):
            def body(e):
                for k in val:
                    val[k] = 0
                state["en"] = en
                state["e"] = e
                run_prog()
            return body

        blk.sync(run_engine("sync"))
        blk.vector(run_engine("vector"))
        blk.tensor(run_engine("tensor"))
        blk.scalar(run_engine("scalar"))
        blk.gpsimd(run_engine("gpsimd"))

    nc.compile()
    return nc, dbg


_CACHE = {}


def kernel(**inputs):
    key = "main"
    if key not in _CACHE:
        _CACHE[key] = build_nc()
    nc, dbg = _CACHE[key]
    in_maps = percore_inputs(inputs)
    res = run_bass_kernel_spmd(nc, in_maps, core_ids=list(range(N_CORES)))
    out = np.stack([res.results[c]["out"] for c in range(N_CORES)], axis=0)
    return out


# version 03
# speedup vs baseline: 1.0223x; 1.0223x over previous
from contextlib import ExitStack
import numpy as np
import ml_dtypes
import concourse.bacc as bacc
import concourse.mybir as mybir
from concourse.bass_utils import run_bass_kernel_spmd

F32 = mybir.dt.float32
BF16 = mybir.dt.bfloat16
I16 = mybir.dt.int16
AF = mybir.ActivationFunctionType
OP = mybir.AluOpType
AX = mybir.AxisListType
BF = ml_dtypes.bfloat16

S, D, H, DH, ROT = 4096, 1024, 16, 64, 32
NG, HPG = 4, 4
E, EPC, TOPK, CAP, I_FF = 32, 4, 4, 1024, 512
RGRP = [[0, 1, 2, 3, 4, 5, 6, 7]]
N_CORES, NQT, NTC = 8, 8, 8

NBF, NF32 = 78848, 7536
O_WG, O_XB, O_QG, O_KG, O_VG = 0, 10240, 18432, 26624, 34816
O_AG, O_PT, O_UNN, O_RB = 43008, 75776, 77824, 78336
O_WO, O_H1B = 0, 8192
O_H1FM, O_SHACT, O_DISP, O_EOUT = 43008, 18432, 34816, 0
FB_EOST, FB_PAY, FB_WDN, FB_GK, FB_ACTE, FB_WGU = 0, 1024, 2048, 2048, 6144, 10240
O_SHACT2, O_DISP2 = 18432, 34816
FB_OH, FB_DEC, FB_GW = 75776, 76032, 76160
F_POS, F_IW, F_GS, F_GC, F_TT, F_OFS, F_HL, F_KEPT = 5232, 6256, 6512, 6640, 6768, 6896, 6928, 6992
F_ACC0, F_H1TF, F_TMPF, F_XRESF = 0, 4192, 5232, 6512
F_RECIP, F_MBIAS, F_RM, F_XRES, F_TMP, F_H1T, F_SCAL, F_RT = \
    0, 1024, 1060, 1120, 3168, 4192, 5216, 5232

SEM_OWNER = {"LD": "sync", "WLD": "sync", "SST": "sync",
             "XLD": "gpsimd", "GLD": "gpsimd", "GST": "gpsimd", "COL": "gpsimd",
             "MM": "tensor", "ACT": "scalar", "VEC": "vector"}
SEM_BY = {"LD": 16, "WLD": 16, "SST": 16, "XLD": 16, "GLD": 16, "GST": 16,
          "COL": 1, "MM": 1, "ACT": 1, "VEC": 1}
ENGS = ["sync", "gpsimd", "tensor", "scalar", "vector"]


def host_constants():
    c = {}
    t = np.arange(S, dtype=np.float64)
    inv = 1.0 / (10000.0 ** (np.arange(0, ROT, 2, dtype=np.float64) / ROT))
    ang = t[None, :] * inv[:, None]
    cos16, sin16 = np.cos(ang), np.sin(ang)
    p = np.arange(128)
    rot_row = (p % 64) < 32
    ce, se = np.tile(cos16, (8, 1)), np.tile(sin16, (8, 1))
    ce[~rot_row] = 1.0
    se[~rot_row] = 0.0
    c["cos_ext"], c["sin_ext"] = ce.astype(BF), se.astype(BF)
    m = np.zeros((128, 4, 512), np.float32)
    for j in range(4):
        for kp in range(128):
            m[kp, j, j * 128 + kp:] = 1.0
    c["mask01"] = m.astype(BF)
    c["ones_col"] = np.ones((128, 1), BF)
    c["ones_row1"] = np.ones((1, 128), BF)
    c["ones_row1f"] = np.ones((1, 128), np.float32)
    tr = np.zeros((128, 132), np.float32)
    for tp in range(128):
        tr[0:tp, tp] = 1.0
    tr[:, 128] = 1.0
    c["trilT"] = tr.astype(BF)
    c["iota32"] = np.tile(np.arange(32).astype(BF)[None, :], (128, 1))
    c["identf"] = np.eye(128, dtype=np.float32)
    return c


def percore_inputs(inputs):
    x = np.asarray(inputs["L0_attn_x"], dtype=np.float32)
    w_q = np.ascontiguousarray(np.asarray(inputs["L0_attn_w_q"], dtype=np.float32))
    w_kv = np.asarray(inputs["L0_attn_w_kv"], dtype=np.float32)
    w_out = np.ascontiguousarray(np.asarray(inputs["L0_attn_w_out"], dtype=np.float32))
    gate_w = np.asarray(inputs["L0_ffn_gate_w"], dtype=np.float32)
    w_gu = np.asarray(inputs["L0_ffn_w_gate_up"], dtype=np.float32)
    w_dn = np.asarray(inputs["L0_ffn_w_down"], dtype=np.float32)
    sh_gu = np.ascontiguousarray(np.asarray(inputs["L0_ffn_shared_w_gate_up"], dtype=np.float32))
    sh_dn = np.ascontiguousarray(np.asarray(inputs["L0_ffn_shared_w_down"], dtype=np.float32))
    consts = host_constants()
    maps = []
    for cid in range(N_CORES):
        m = dict(consts)
        m["x_tm"] = np.ascontiguousarray(x[cid])
        m["x_fmT"] = np.ascontiguousarray(x[cid].T)
        m["w_q"] = w_q
        m["w_k"] = np.ascontiguousarray(w_kv[:, 0:D])
        m["w_v"] = np.ascontiguousarray(w_kv[:, D:2 * D])
        m["w_out"] = w_out
        m["gate_wT"] = np.ascontiguousarray(gate_w.T)
        m["w_gu"] = np.ascontiguousarray(w_gu[cid * EPC:(cid + 1) * EPC])
        m["w_dn"] = np.ascontiguousarray(w_dn[cid * EPC:(cid + 1) * EPC])
        m["sh_gu"] = sh_gu
        m["sh_dn"] = sh_dn
        pm = np.zeros((32, 8), np.float32)
        pm[:, 0:cid] = 1.0
        m["prevmask"] = pm
        gid = np.zeros((128, 32), np.float32)
        for tau in range(128):
            t = np.arange(32) * 128 + tau
            gid[tau, :] = (t // 512) * (8 * 512) + cid * 512 + (t % 512) + 1
        hi = np.floor_divide(gid, 128.0)
        m["gid_hi"] = hi.astype(BF)
        m["gid_lo"] = (gid - hi * 128.0).astype(BF)
        f = (np.arange(32) % 4) * 8 + np.arange(32) // 4
        m["ftab"] = np.tile((f * 1024.0).astype(BF)[None, :], (128, 1))
        maps.append(m)
    return maps


class _Cut(Exception):
    pass


def build_nc(debug=(), cut=None):
    nc = bacc.Bacc("TRN2", target_bir_lowering=False,
                   detect_race_conditions=False)
    d = {}

    def din(n, s, t):
        d[n] = nc.dram_tensor(n, s, t, kind="ExternalInput")

    def dout(n, s, t):
        d[n] = nc.dram_tensor(n, s, t, kind="ExternalOutput")

    def dint(n, s, t):
        d[n] = nc.dram_tensor(n, s, t)

    din("x_tm", [S, D], F32)
    din("x_fmT", [D, S], F32)
    din("w_q", [D, D], F32)
    din("w_k", [D, D], F32)
    din("w_v", [D, D], F32)
    din("w_out", [D, D], F32)
    din("gate_wT", [D, E], F32)
    din("w_gu", [EPC, D, 2 * I_FF], F32)
    din("w_dn", [EPC, I_FF, D], F32)
    din("sh_gu", [D, 2 * I_FF], F32)
    din("sh_dn", [I_FF, D], F32)
    din("cos_ext", [128, S], BF16)
    din("sin_ext", [128, S], BF16)
    din("mask01", [128, 4, 512], BF16)
    din("ones_col", [128, 1], BF16)
    din("ones_row1", [1, 128], BF16)
    din("ones_row1f", [1, 128], F32)
    din("trilT", [128, 132], BF16)
    din("iota32", [128, 32], BF16)
    din("identf", [128, 128], F32)
    din("prevmask", [32, 8], F32)
    din("gid_hi", [128, 32], BF16)
    din("gid_lo", [128, 32], BF16)
    din("ftab", [128, 32], BF16)
    dint("wb_q", [D, D], BF16)
    dint("wb_k", [D, D], BF16)
    dint("wq_c", [D, D], BF16)
    dint("wk_c", [D, D], BF16)
    dint("wv_c", [D, D], BF16)
    dint("wo_c", [D, D], BF16)
    dint("h1_tm_d", [S, D], F32)
    dint("h1_pub", [S, D], BF16)
    dint("h1_all", [N_CORES * S, D], BF16)
    dint("cnt_loc", [1, E], F32)
    dint("cnt_all", [N_CORES, E], F32)
    dint("slot_loc", [E * CAP, 128], BF16)
    dint("slot_rs", [EPC * CAP, 128], BF16)
    dint("rt_scr", [12, S], F32)
    dint("eo_loc", [EPC * CAP, D], BF16)
    dint("eo_all", [E * CAP, D], BF16)
    dout("out", [S, D], F32)
    dbg = []
    if "h1" in debug:
        dout("dbg_h1", [S, D], F32)
        dbg.append("dbg_h1")
    if "arena" in debug:
        dout("dbg_ag", [128, 8 * S], BF16)
        dout("dbg_f32", [128, NF32], F32)
        dbg.append("dbg_ag")
    if "ffn" in debug:
        dout("dbg_cnt", [N_CORES, E], F32)
        dout("dbg_srs", [EPC * CAP, 128], BF16)
        dout("dbg_eo", [EPC * CAP, D], BF16)
        dout("dbg_rt", [12, S], F32)
        dbg.append("dbg_cnt")

    marks = {}
    state = {"emit": False, "en": None, "e": None}
    val = {}
    CUT = cut

    with ExitStack() as ctx:
        AB = ctx.enter_context(nc.sbuf_tensor("AB", [128, NBF], BF16))
        A32 = ctx.enter_context(nc.sbuf_tensor("A32", [128, NF32], F32))
        cb = {}
        for nm, sh, tt in [("cos_e", [128, S], BF16), ("sin_e", [128, S], BF16),
                           ("mask01", [128, 4, 512], BF16), ("ones_col", [128, 1], BF16),
                           ("ones_row1", [1, 128], BF16), ("ones_row1f", [1, 128], F32),
                           ("trilT", [128, 132], BF16),
                           ("iota32", [128, 32], BF16), ("identf", [128, 128], F32),
                           ("prevmask", [32, 8], F32), ("gid_hi", [128, 32], BF16),
                           ("gid_lo", [128, 32], BF16), ("ftab", [128, 32], BF16)]:
            cb[nm] = ctx.enter_context(nc.sbuf_tensor("sb_" + nm, sh, tt))
        idx16 = ctx.enter_context(nc.sbuf_tensor("idx16", [128, 1280], I16))
        idxu = ctx.enter_context(nc.sbuf_tensor("idxu", [128, 8], mybir.dt.uint32))
        ps0 = ctx.enter_context(nc.psum_tensor("ps0", [128, 1024], F32))
        ps1 = ctx.enter_context(nc.psum_tensor("ps1", [128, 1024], F32))
        pavd = ctx.enter_context(nc.psum_tensor("pavd", [128, 1536], F32))
        pmisc = ctx.enter_context(nc.psum_tensor("pmisc", [128, 512], F32))
        sems = {k: ctx.enter_context(nc.semaphore(k)) for k in SEM_OWNER}
        blk = ctx.enter_context(nc.Block())

        def bfv(off, shape):
            n = int(np.prod(shape))
            ap = AB[:, off:off + n]
            if len(shape) > 1:
                kw = {f"a{i}": shape[i] for i in range(len(shape))}
                pat = " ".join(f"a{i}" for i in range(len(shape)))
                ap = ap.rearrange(f"p ({pat}) -> p {pat}", **kw)
            return ap

        def fv(off, shape):
            n = int(np.prod(shape))
            ap = A32[:, off:off + n]
            if len(shape) > 1:
                kw = {f"a{i}": shape[i] for i in range(len(shape))}
                pat = " ".join(f"a{i}" for i in range(len(shape)))
                ap = ap.rearrange(f"p ({pat}) -> p {pat}", **kw)
            return ap

        def op(sem, builder):
            val[sem] = val.get(sem, 0) + SEM_BY[sem]
            if state["emit"] and state["en"] == SEM_OWNER[sem]:
                builder(state["e"]).then_inc(sems[sem], SEM_BY[sem])
                if sem in ("VEC", "ACT"):
                    state["e"].wait_ge(sems[sem], val[sem])

        def opn(sem, builder):
            if state["emit"] and state["en"] == SEM_OWNER[sem]:
                builder(state["e"])

        def mk(label):
            if not state["emit"]:
                marks[label] = dict(val)

        def wt(eng, label, sem):
            if state["emit"] and state["en"] == eng:
                v = marks[label].get(sem, 0)
                if v > 0:
                    state["e"].wait_ge(sems[sem], v)

        def maybe_cut(name):
            if CUT == name:
                raise _Cut()

        def program():
            if "dbg_ag" in dbg:
                op("VEC", lambda e: e.memset(A32[:, :], 0.0))
                op("VEC", lambda e: e.memset(AB[:, :], 0.0))
                mk("dbginit")
                wt("sync", "dbginit", "VEC")
                wt("gpsimd", "dbginit", "VEC")
            for nm in ["cos_ext", "sin_ext", "mask01", "ones_col", "ones_row1",
                       "ones_row1f", "trilT", "iota32", "identf", "prevmask",
                       "gid_hi", "gid_lo", "ftab"]:
                dst = {"cos_ext": "cos_e", "sin_ext": "sin_e"}.get(nm, nm)
                op("LD", lambda e, nm=nm, dst=dst: e.dma_start(out=cb[dst][:], in_=d[nm][:]))
            mk("consts")
            for src, dst in [("w_q", "wq_c"), ("w_k", "wk_c"), ("w_v", "wv_c"),
                             ("w_out", "wo_c")]:
                for hh in range(2):
                    op("XLD", lambda e, s=src, t=dst, hh=hh: e.dma_start(
                        out=d[t][:, hh * 512:(hh + 1) * 512],
                        in_=d[s][:, hh * 512:(hh + 1) * 512]))
            mk("wcast")
            maybe_cut("consts")

            wstage = bfv(O_WG, [2, 1024])
            for si, (src, dst) in enumerate([("wq_c", "wb_q"), ("wk_c", "wb_k")]):
                for fc in range(8):
                    it = si * 8 + fc
                    wt("sync", "wcast", "XLD")
                    if it:
                        wt("sync", f"wbTT_{it - 1}", "VEC")
                    op("WLD", lambda e, s=src, fc=fc: e.dma_start(
                        out=wstage[:, 0, :], in_=d[s][fc * 128:(fc + 1) * 128, :]))
                    mk(f"wbL_{it}")
                    wt("vector", f"wbL_{it}", "WLD")
                    if it:
                        wt("vector", f"wbS_{it - 1}", "GLD")
                    op("VEC", lambda e: e.memset(wstage[:, 1, :], 0.0))
                    s4 = wstage[:, 0, :].rearrange("p (h q) -> p h q", h=16)
                    d4 = wstage[:, 1, :].rearrange("p (h q) -> p h q", h=16)
                    op("VEC", lambda e: e.tensor_scalar(d4[:, :, 0:16], s4[:, :, 16:32],
                                                        -1.0, None, OP.mult))
                    op("VEC", lambda e: e.tensor_scalar(d4[:, :, 16:32], s4[:, :, 0:16],
                                                        1.0, None, OP.mult))
                    mk(f"wbTT_{it}")
                    wt("gpsimd", f"wbTT_{it}", "VEC")
                    op("GLD", lambda e, t=dst, fc=fc: e.dma_start(
                        out=d[t][fc * 128:(fc + 1) * 128, :], in_=wstage[:, 1, :]))
                    mk(f"wbS_{it}")
            mk("wb_done")
            maybe_cut("wb")

            wg = bfv(O_WG, [8, 5, 256])
            xb = bfv(O_XB, [2, 8, 512])
            qg = bfv(O_QG, [2, S])
            kg = bfv(O_KG, [2, S])
            vg = bfv(O_VG, [32, 256])
            agf = bfv(O_AG, [8, S])
            pt = bfv(O_PT, [2, 2, 512])
            unn = bfv(O_UNN, [512])
            rb16 = bfv(O_RB, [512])
            recip = fv(F_RECIP, [1024])
            mbias = fv(F_MBIAS, [36])
            rmx = fv(F_RM, [56])
            tmpa = fv(F_XRES, [2, 1024])

            for g in range(NG):
                wt("sync", "wb_done", "GLD")
                if g > 0:
                    mkname = f"projMM_{g - 1}_{NTC - 1}"
                    wt("sync", mkname, "MM")
                for fc in range(8):
                    for wi, nm in enumerate(["wq_c", "wb_q", "wk_c", "wb_k", "wv_c"]):
                        op("WLD", lambda e, fc=fc, wi=wi, nm=nm: e.dma_start(
                            out=wg[:, fc, wi, :],
                            in_=d[nm][fc * 128:(fc + 1) * 128, g * 256:(g + 1) * 256]))
                mk(f"wg_{g}")

                for tc in range(NTC):
                    par = tc % 2
                    if tc >= 2:
                        wt("gpsimd", f"projMM_{g}_{tc - 2}", "MM")
                    elif g > 0:
                        wt("gpsimd", f"projMM_{g - 1}_{NTC - 1}", "MM")
                    else:
                        wt("gpsimd", "wb_done", "GLD")
                    for fc in range(8):
                        op("XLD", lambda e, fc=fc, tc=tc, par=par: e.dma_start(
                            out=xb[:, par, fc, :],
                            in_=d["x_fmT"][fc * 128:(fc + 1) * 128,
                                           tc * 512:(tc + 1) * 512]))
                    mk(f"xld_{g}_{tc}")

                    wt("tensor", f"xld_{g}_{tc}", "XLD")
                    wt("tensor", f"wg_{g}", "WLD")
                    if tc > 0:
                        wt("tensor", f"ropek_{g}_{tc - 1}", "VEC")
                    elif g > 0:
                        wt("tensor", f"lastexp_{g - 1}", "ACT")
                    for half in range(2):
                        for wi, pbuf in ((0, ps0), (1, ps1)):
                            for fc in range(8):
                                def bq(e, half=half, wi=wi, pbuf=pbuf, fc=fc, par=par):
                                    return e.matmul(
                                        pbuf[:, half * 512:(half + 1) * 512],
                                        wg[:, fc, wi, half * 128:(half + 1) * 128],
                                        xb[:, par, fc, :],
                                        start=(fc == 0), stop=(fc == 7))
                                if fc == 7:
                                    op("MM", bq)
                                else:
                                    opn("MM", bq)
                    mk(f"qmm_{g}_{tc}")
                    wt("vector", f"qmm_{g}_{tc}", "MM")
                    for half in range(2):
                        sl = slice(half * 512, (half + 1) * 512)
                        tsl = slice(tc * 512, (tc + 1) * 512)
                        op("VEC", lambda e, sl=sl, tsl=tsl: e.tensor_tensor(
                            tmpa[:, 0, 0:512], ps1[:, sl], cb["sin_e"][:, tsl], OP.mult))
                        op("VEC", lambda e, sl=sl, tsl=tsl: e.tensor_tensor(
                            tmpa[:, 0, 512:1024], ps0[:, sl], cb["cos_e"][:, tsl], OP.mult))
                        op("VEC", lambda e, half=half, tsl=tsl: e.tensor_tensor(
                            qg[:, half, tsl], tmpa[:, 0, 512:1024], tmpa[:, 0, 0:512],
                            OP.add))
                    mk(f"ropeq_{g}_{tc}")
                    wt("tensor", f"ropeq_{g}_{tc}", "VEC")
                    for half in range(2):
                        for wi, pbuf in ((2, ps0), (3, ps1)):
                            for fc in range(8):
                                def bk(e, half=half, wi=wi, pbuf=pbuf, fc=fc, par=par):
                                    return e.matmul(
                                        pbuf[:, half * 512:(half + 1) * 512],
                                        wg[:, fc, wi, half * 128:(half + 1) * 128],
                                        xb[:, par, fc, :],
                                        start=(fc == 0), stop=(fc == 7))
                                if fc == 7:
                                    op("MM", bk)
                                else:
                                    opn("MM", bk)
                    mk(f"kmm_{g}_{tc}")
                    wt("vector", f"kmm_{g}_{tc}", "MM")
                    for half in range(2):
                        sl = slice(half * 512, (half + 1) * 512)
                        tsl = slice(tc * 512, (tc + 1) * 512)
                        op("VEC", lambda e, sl=sl, tsl=tsl: e.tensor_tensor(
                            tmpa[:, 0, 0:512], ps1[:, sl], cb["sin_e"][:, tsl], OP.mult))
                        op("VEC", lambda e, sl=sl, tsl=tsl: e.tensor_tensor(
                            tmpa[:, 0, 512:1024], ps0[:, sl], cb["cos_e"][:, tsl], OP.mult))
                        op("VEC", lambda e, half=half, tsl=tsl: e.tensor_tensor(
                            kg[:, half, tsl], tmpa[:, 0, 512:1024], tmpa[:, 0, 0:512],
                            OP.add))
                    mk(f"ropek_{g}_{tc}")
                    wt("tensor", f"ropek_{g}_{tc}", "VEC")
                    if tc == 0 and g > 0:
                        wt("tensor", f"lastdone_{g - 1}", "VEC")
                    for sub in range(4):
                        if tc > 0 or sub > 1:
                            psub = f"vev_{g}_{tc if sub > 1 else tc - 1}_" \
                                   f"{sub - 2 if sub > 1 else sub + 2}"
                            wt("tensor", psub, "ACT")
                        pv = ps0 if sub % 2 == 0 else ps1
                        for fc in range(8):
                            def bv(e, sub=sub, fc=fc, par=par, pv=pv):
                                return e.matmul(
                                    pv[:, 0:256],
                                    xb[:, par, fc, sub * 128:(sub + 1) * 128],
                                    wg[:, fc, 4, :],
                                    start=(fc == 0), stop=(fc == 7))
                            if fc == 7:
                                op("MM", bv)
                            else:
                                opn("MM", bv)
                        mk(f"vmm_{g}_{tc}_{sub}")
                        wt("scalar", f"vmm_{g}_{tc}_{sub}", "MM")
                        op("ACT", lambda e, tc=tc, sub=sub, pv=pv: e.copy(
                            vg[:, tc * 4 + sub, :], pv[:, 0:256]))
                        mk(f"vev_{g}_{tc}_{sub}")
                    mk(f"projMM_{g}_{tc}")
                    maybe_cut(f"proj_{g}_{tc}")
                for ch in range(2):
                    for ti, buf in ((0, qg), (1, kg)):
                        si = 2 * ch + ti
                        for qt in range(NQT):
                            it = si * NQT + qt
                            wt("scalar", f"ropek_{g}_{NTC - 1}", "VEC")
                            wt("scalar", f"projMM_{g}_{NTC - 1}", "MM")
                            if it:
                                wt("scalar", f"nmm_{g}_{it - 1}", "MM")
                            op("ACT", lambda e, ch=ch, buf=buf, qt=qt: e.activation(
                                unn[:], buf[:, ch, qt * 512:(qt + 1) * 512],
                                AF.Square, 0.0, 1.0))
                            mk(f"nsq_{g}_{it}")
                            wt("tensor", f"nsq_{g}_{it}", "ACT")
                            if it:
                                wt("tensor", f"nmx_{g}_{it - 1}", "VEC")
                            op("MM", lambda e: e.matmul(
                                pmisc[0:1, 0:512], cb["ones_col"][:], unn[:],
                                start=True, stop=True))
                            mk(f"nmm_{g}_{it}")
                            wt("vector", f"nmm_{g}_{it}", "MM")
                            if ti == 0:
                                op("VEC", lambda e, ch=ch, qt=qt: e.tensor_reduce(
                                    rmx[0:1, ch * 8 + qt:ch * 8 + qt + 1],
                                    pmisc[0:1, 0:512], AX.X, OP.max))
                            else:
                                op("VEC", lambda e: e.tensor_reduce(
                                    rmx[0:1, 18:19], pmisc[0:1, 0:512], AX.X, OP.max))
                                if qt == 0:
                                    op("VEC", lambda e, ch=ch: e.tensor_copy(
                                        rmx[0:1, 16 + ch:17 + ch], rmx[0:1, 18:19]))
                                else:
                                    op("VEC", lambda e, ch=ch: e.tensor_tensor(
                                        rmx[0:1, 16 + ch:17 + ch],
                                        rmx[0:1, 16 + ch:17 + ch],
                                        rmx[0:1, 18:19], OP.max))
                            mk(f"nmx_{g}_{it}")
                wt("scalar", f"nmx_{g}_{4 * NQT - 1}", "VEC")
                op("ACT", lambda e: e.activation(rmx[0:1, 19:37], rmx[0:1, 0:18],
                                                 AF.Sqrt, 0.0, 1.0))
                mk(f"msq_{g}")
                wt("vector", f"msq_{g}", "ACT")
                for ch in range(2):
                    op("VEC", lambda e, ch=ch: e.tensor_scalar(
                        rmx[0:1, 37 + ch * 8:45 + ch * 8],
                        rmx[0:1, 19 + ch * 8:27 + ch * 8],
                        rmx[0:1, 35 + ch:36 + ch], -0.125, OP.mult, OP.mult))
                op("VEC", lambda e: e.tensor_scalar(rmx[0:1, 37:53], rmx[0:1, 37:53],
                                                    -1.0, None, OP.add))
                op("VEC", lambda e: e.tensor_copy(rb16[0:1, 16:32], rmx[0:1, 37:53]))
                mk(f"mrow_{g}")
                wt("tensor", f"mrow_{g}", "VEC")
                op("MM", lambda e: e.matmul(pmisc[:, 16:32], cb["ones_row1"][:],
                                            rb16[0:1, 16:32], start=True, stop=True))
                mk(f"mbc_{g}")
                wt("scalar", f"mbc_{g}", "MM")
                wt("scalar", f"vev_{g}_{NTC - 1}_3", "ACT")
                op("ACT", lambda e: e.copy(mbias[:, 0:16], pmisc[:, 16:32]))
                mk(f"mbias_{g}")
                maybe_cut(f"mbias_{g}")

                for hp in range(2):
                    for qt in range(NQT):
                        nkj = (qt + 1) * 4
                        for kj in range(nkj):
                            par = kj % 2
                            psc = ps0 if par == 0 else ps1
                            if kj >= 2:
                                wt("tensor", f"exp_{g}_{hp}_{qt}_{kj - 2}", "ACT")
                                wt("tensor", f"expm_{g}_{hp}_{qt}_{kj - 2}", "VEC")
                            else:
                                if hp == 0 and qt == 0:
                                    wt("tensor", f"mbias_{g}", "ACT")
                                else:
                                    pq = (hp, qt - 1) if qt else (hp - 1, NQT - 1)
                                    pk = (pq[1] + 1) * 4 - 2 + kj
                                    wt("tensor", f"exp_{g}_{pq[0]}_{pq[1]}_{pk}", "ACT")
                                    wt("tensor", f"expm_{g}_{pq[0]}_{pq[1]}_{pk}", "VEC")
                            for hd in range(2):
                                def bs(e, hd=hd, hp=hp, qt=qt, kj=kj, psc=psc):
                                    base = hd * 64
                                    return e.matmul(
                                        psc[:, hd * 512:(hd + 1) * 512],
                                        kg[base:base + 64, hp, kj * 128:(kj + 1) * 128],
                                        qg[base:base + 64, hp, qt * 512:(qt + 1) * 512],
                                        start=True, stop=True,
                                        tile_position=(base, 0),
                                        skip_group_check=True)
                                if hd == 1:
                                    op("MM", bs)
                                else:
                                    opn("MM", bs)
                            mk(f"sc_{g}_{hp}_{qt}_{kj}")
                            maybe_cut(f"sc_{g}_{hp}_{qt}_{kj}")
                            wt("scalar", f"sc_{g}_{hp}_{qt}_{kj}", "MM")
                            if kj >= 2:
                                wt("scalar", f"avm_{g}_{hp}_{qt}_{kj - 2}", "MM")
                            elif qt > 0 or hp > 0:
                                pq = (hp, qt - 1) if qt else (hp - 1, NQT - 1)
                                pk = (pq[1] + 1) * 4 - 2 + kj
                                wt("scalar", f"avm_{g}_{pq[0]}_{pq[1]}_{pk}", "MM")
                            elif g > 0:
                                pk = NQT * 4 - 2 + kj
                                wt("scalar", f"avm_{g - 1}_1_{NQT - 1}_{pk}", "MM")
                            for hd in range(2):
                                op("ACT", lambda e, par=par, hd=hd, qt=qt, hp=hp, psc=psc:
                                   e.activation(pt[:, par, hd, :],
                                                psc[:, hd * 512:(hd + 1) * 512], AF.Exp,
                                                mbias[:, hp * 8 + qt:hp * 8 + qt + 1],
                                                0.125))
                            mk(f"exp_{g}_{hp}_{qt}_{kj}")
                            maybe_cut(f"exp_{g}_{hp}_{qt}_{kj}")
                            diag = kj >= qt * 4
                            if diag:
                                wt("vector", f"exp_{g}_{hp}_{qt}_{kj}", "ACT")
                                for hd in range(2):
                                    op("VEC", lambda e, par=par, hd=hd, j=kj - qt * 4:
                                       e.tensor_tensor(pt[:, par, hd, :], pt[:, par, hd, :],
                                                       cb["mask01"][:, j, :], OP.mult))
                            mk(f"expm_{g}_{hp}_{qt}_{kj}")
                            wt("tensor", f"exp_{g}_{hp}_{qt}_{kj}", "ACT")
                            if diag:
                                wt("tensor", f"expm_{g}_{hp}_{qt}_{kj}", "VEC")
                            if kj == 0:
                                if hp > 0 or qt > 0:
                                    pq = (hp, qt - 1) if qt else (hp - 1, NQT - 1)
                                    wt("tensor", f"unn_{g}_{pq[0]}_{pq[1]}", "ACT")
                                    wt("tensor", f"recip_{g}_{pq[0]}_{pq[1]}", "VEC")
                            first, last = (kj == 0), (kj == nkj - 1)
                            op("MM", lambda e, kj=kj, hp=hp, par=par, first=first, last=last:
                               e.matmul(pavd[0:64, 0:512], vg[:, kj, hp * 128:hp * 128 + 64],
                                        pt[:, par, 0, :], start=first, stop=last,
                                        tile_position=(0, 0), skip_group_check=True))
                            op("MM", lambda e, kj=kj, hp=hp, par=par, first=first, last=last:
                               e.matmul(pavd[64:128, 0:512],
                                        vg[:, kj, hp * 128 + 64:hp * 128 + 128],
                                        pt[:, par, 1, :], start=first, stop=last,
                                        tile_position=(0, 64), skip_group_check=True))
                            op("MM", lambda e, par=par, first=first, last=last:
                               e.matmul(pavd[0:1, 512:1024], cb["ones_col"][:],
                                        pt[:, par, 0, :], start=first, stop=last,
                                        skip_group_check=True))
                            op("MM", lambda e, par=par, first=first, last=last:
                               e.matmul(pavd[0:1, 1024:1536], cb["ones_col"][:],
                                        pt[:, par, 1, :], start=first, stop=last,
                                        skip_group_check=True))
                            mk(f"avm_{g}_{hp}_{qt}_{kj}")
                            maybe_cut(f"avm_{g}_{hp}_{qt}_{kj}")
                        last_avm = f"avm_{g}_{hp}_{qt}_{nkj - 1}"
                        wt("scalar", last_avm, "MM")
                        if hp > 0 or qt > 0:
                            pq = (hp, qt - 1) if qt else (hp - 1, NQT - 1)
                            wt("scalar", f"norm_{g}_{pq[0]}_{pq[1]}", "VEC")
                        op("ACT", lambda e: e.copy(unn[:], pavd[:, 0:512]))
                        mk(f"unn_{g}_{hp}_{qt}")
                        wt("vector", last_avm, "MM")
                        op("VEC", lambda e: e.reciprocal(recip[0:1, :], pavd[0:1, 512:1536]))
                        op("VEC", lambda e: e.tensor_copy(rb16[0:1, 0:512],
                                                          recip[0:1, 0:512]))
                        mk(f"recip_{g}_{hp}_{qt}")
                        wt("tensor", f"recip_{g}_{hp}_{qt}", "VEC")
                        if hp > 0 or qt > 0:
                            pq = (hp, qt - 1) if qt else (hp - 1, NQT - 1)
                            wt("tensor", f"norm_{g}_{pq[0]}_{pq[1]}", "VEC")
                        elif g > 0:
                            wt("tensor", f"norm_{g - 1}_1_{NQT - 1}", "VEC")
                        op("MM", lambda e: e.matmul(pmisc[0:64, :], cb["ones_row1"][0:1, 0:64],
                                                    rb16[0:1, 0:512],
                                                    start=True, stop=True,
                                                    tile_position=(0, 0)))
                        mk(f"bc1_{g}_{hp}_{qt}")
                        wt("vector", f"bc1_{g}_{hp}_{qt}", "MM")
                        op("VEC", lambda e: e.tensor_copy(rb16[0:1, 0:512],
                                                          recip[0:1, 512:1024]))
                        mk(f"recipB_{g}_{hp}_{qt}")
                        wt("tensor", f"recipB_{g}_{hp}_{qt}", "VEC")
                        op("MM", lambda e: e.matmul(pmisc[64:128, :],
                                                    cb["ones_row1"][0:1, 0:64],
                                                    rb16[0:1, 0:512],
                                                    start=True, stop=True,
                                                    tile_position=(0, 64)))
                        mk(f"bc_{g}_{hp}_{qt}")
                        wt("vector", f"bc_{g}_{hp}_{qt}", "MM")
                        wt("vector", f"unn_{g}_{hp}_{qt}", "ACT")
                        op("VEC", lambda e, g=g, hp=hp, qt=qt: e.tensor_tensor(
                            agf[:, g * 2 + hp, qt * 512:(qt + 1) * 512], unn[:],
                            pmisc[:, :], OP.mult))
                        mk(f"norm_{g}_{hp}_{qt}")
                        maybe_cut(f"norm_{g}_{hp}_{qt}")
                mk(f"lastexp_{g}")
                mk(f"lastdone_{g}")

            wo = bfv(O_WO, [8, 1024])
            h1b = bfv(O_H1B, [1024])
            xres = fv(F_XRES, [2, 1024])
            h1t = fv(F_H1T, [1024])
            scal = fv(F_SCAL, [16])
            wt("sync", f"projMM_{NG - 1}_{NTC - 1}", "MM")
            for fc in range(8):
                op("WLD", lambda e, fc=fc: e.dma_start(
                    out=wo[:, fc, :], in_=d["wo_c"][fc * 128:(fc + 1) * 128, :]))
            mk("wo_ld")
            for t32 in range(32):
                par = t32 % 2
                if t32 >= 2:
                    wt("sync", f"h1add_{t32 - 2}", "VEC")
                op("LD", lambda e, t32=t32, par=par: e.dma_start(
                    out=xres[:, par, :], in_=d["x_tm"][t32 * 128:(t32 + 1) * 128, :]))
                mk(f"xtm_{t32}")
                wt("tensor", "wo_ld", "WLD")
                if t32 == 0:
                    wt("tensor", f"lastdone_{NG - 1}", "VEC")
                else:
                    wt("tensor", f"h1add_{t32 - 1}", "VEC")
                for oc in range(2):
                    for fc in range(8):
                        def bo(e, oc=oc, fc=fc, t32=t32):
                            return e.matmul(
                                pavd[:, oc * 512:(oc + 1) * 512],
                                agf[:, fc, t32 * 128:(t32 + 1) * 128],
                                wo[:, fc, oc * 512:(oc + 1) * 512],
                                start=(fc == 0), stop=(fc == 7))
                        if fc == 7 and oc == 1:
                            op("MM", bo)
                        else:
                            opn("MM", bo)
                mk(f"opmm_{t32}")
                wt("vector", f"opmm_{t32}", "MM")
                wt("vector", f"xtm_{t32}", "LD")
                if t32 > 0:
                    wt("vector", f"h1sq_{t32 - 1}", "ACT")
                op("VEC", lambda e, par=par: e.tensor_tensor(h1t[:], xres[:, par, :],
                                                             pavd[:, 0:1024], OP.add))
                op("VEC", lambda e: e.tensor_reduce(scal[:, 0:1], h1t[:], AX.X, OP.add))
                mk(f"h1add_{t32}")
                wt("scalar", f"h1add_{t32}", "VEC")
                op("ACT", lambda e: e.activation(h1b[:], h1t[:], AF.Square, 0.0, 1.0,
                                                 accum_out=scal[:, 1:2]))
                mk(f"h1sq_{t32}")
                wt("vector", f"h1sq_{t32}", "ACT")
                op("VEC", lambda e: e.tensor_scalar(scal[:, 2:3], scal[:, 0:1],
                                                    1.0 / D, None, OP.mult))
                op("VEC", lambda e: e.tensor_tensor(scal[:, 3:4], scal[:, 2:3],
                                                    scal[:, 2:3], OP.mult))
                op("VEC", lambda e: e.tensor_scalar(scal[:, 4:5], scal[:, 1:2],
                                                    1.0 / D, None, OP.mult))
                op("VEC", lambda e: e.tensor_tensor(scal[:, 5:6], scal[:, 4:5],
                                                    scal[:, 3:4], OP.subtract))
                op("VEC", lambda e: e.tensor_scalar(scal[:, 6:7], scal[:, 5:6],
                                                    1e-5, None, OP.add))
                mk(f"h1v_{t32}")
                wt("scalar", f"h1v_{t32}", "VEC")
                op("ACT", lambda e: e.activation(scal[:, 7:8], scal[:, 6:7],
                                                 AF.Sqrt, 0.0, 1.0))
                mk(f"h1sd_{t32}")
                wt("vector", f"h1sd_{t32}", "ACT")
                op("VEC", lambda e: e.reciprocal(scal[:, 8:9], scal[:, 7:8]))
                op("VEC", lambda e: e.tensor_scalar(scal[:, 9:10], scal[:, 2:3],
                                                    scal[:, 8:9], -1.0, OP.mult, OP.mult))
                mk(f"h1rs_{t32}")
                wt("scalar", f"h1rs_{t32}", "VEC")
                if t32 > 0:
                    wt("vector", f"h1st_{t32 - 1}", "SST")
                op("VEC", lambda e: e.tensor_scalar(fv(F_TMP, [1024])[:, :],
                                                    h1t[:], scal[:, 8:9],
                                                    scal[:, 9:10], OP.mult, OP.add))
                mk(f"opev_{t32}")
                op("VEC", lambda e: e.tensor_copy(h1b[:], fv(F_TMP, [1024])[:, :]))
                mk(f"h1b_{t32}")
                wt("sync", f"opev_{t32}", "VEC")
                op("SST", lambda e, t32=t32: e.dma_start(
                    out=d["h1_tm_d"][t32 * 128:(t32 + 1) * 128, :],
                    in_=fv(F_TMP, [1024])[:, :]))
                if "dbg_h1" in dbg:
                    op("SST", lambda e, t32=t32: e.dma_start(
                        out=d["dbg_h1"][t32 * 128:(t32 + 1) * 128, :],
                        in_=fv(F_TMP, [1024])[:, :]))
                wt("sync", f"h1b_{t32}", "VEC")
                op("SST", lambda e, t32=t32: e.dma_start(
                    out=d["h1_pub"][t32 * 128:(t32 + 1) * 128, :], in_=h1b[:]))
                mk(f"h1st_{t32}")
                maybe_cut(f"h1_{t32}")
            mk("h1_done")
            maybe_cut("h1only")

            h1fm = bfv(O_H1FM, [8, S])
            shact = bfv(O_SHACT2, [4, S])
            disp = bfv(O_DISP2, [2, 8, 512])
            wgu_sb = bfv(FB_WGU, [8, 1024])
            shdn_sb = bfv(FB_WGU, [4, 1024])
            wdn_sb = bfv(FB_WDN, [4, 1024])
            acte = bfv(FB_ACTE, [4, 1024])
            eost = bfv(FB_EOST, [1024])
            silb = bfv(FB_PAY, [1024])
            pay = bfv(FB_PAY, [2, 4, 128])
            ohb = bfv(FB_OH, [4, 32])
            decb = bfv(FB_DEC, [64, 2])
            gwsb = bfv(FB_GW, [8, 32])
            postm = fv(F_POS, [32, 32])
            iw = fv(F_IW, [32, 8])
            gsv = fv(F_GS, [4, 32])
            gcv = fv(F_GC, [4, 32])
            ftt = fv(F_TT, [128])
            ofsb = fv(F_OFS, [32])
            hlv = fv(F_HL, [2, 32])
            keptv = fv(F_KEPT, [4, 32])
            ltile = fv(F_TMP, [1024])
            h1tf = fv(F_H1TF, [1024])
            xrf = fv(F_XRESF, [1024])
            outf = fv(F_TMPF, [1024])

            wt("gpsimd", f"lastdone_{NG - 1}", "VEC")
            op("XLD", lambda e: e.dma_start(
                out=gwsb[:, :, :],
                in_=d["gate_wT"][:].rearrange("(a p) b -> p a b", p=128)))
            mk("gw_ld")
            op("XLD", lambda e: e.dma_start(
                out=wgu_sb[:, :, :],
                in_=d["sh_gu"][:].rearrange("(a p) b -> p a b", p=128)))
            mk("shgu_ld")
            wt("vector", f"lastdone_{NG - 1}", "VEC")
            op("VEC", lambda e: e.memset(AB[:, FB_GK:FB_GK + 4096], 0.0))
            mk("zb")
            wt("gpsimd", "zb", "VEC")
            for i in range(8):
                op("GLD", lambda e, i=i: e.dma_start(
                    out=d["slot_loc"][i * 4096:(i + 1) * 4096, :].rearrange(
                        "(p a) f -> p (a f)", p=128),
                    in_=AB[:, FB_GK:FB_GK + 4096]))
            mk("slotz")
            for kc in range(8):
                wt("gpsimd", f"h1st_{kc * 4 + 3}", "SST")
                op("COL", lambda e, kc=kc: e.collective_compute(
                    "AllGather", OP.bypass, RGRP,
                    [d["h1_pub"][kc * 512:(kc + 1) * 512, :]],
                    [d["h1_all"][kc * 4096:(kc + 1) * 4096, :]]))
            mk("ag_h1")

            wt("sync", "opmm_31", "MM")
            wt("sync", "h1_done", "SST")
            for fc in range(8):
                op("LD", lambda e, fc=fc: e.dma_start(
                    out=h1fm[:, fc, :],
                    in_=d["h1_pub"][:, fc * 128:(fc + 1) * 128], transpose=True))
            mk("h1fm_ld")
            maybe_cut("h1fm")

            for ti in range(32):
                wt("tensor", "h1fm_ld", "LD")
                wt("tensor", "gw_ld", "XLD")
                if ti:
                    wt("tensor", f"lgev_{ti - 1}", "ACT")
                for fc in range(8):
                    def bl(e, fc=fc, ti=ti):
                        return e.matmul(pmisc[:, 0:32],
                                        h1fm[:, fc, ti * 128:(ti + 1) * 128],
                                        gwsb[:, fc, :],
                                        start=(fc == 0), stop=(fc == 7))
                    if fc == 7:
                        op("MM", bl)
                    else:
                        opn("MM", bl)
                mk(f"lgmm_{ti}")
                wt("scalar", f"lgmm_{ti}", "MM")
                if ti:
                    wt("scalar", f"top_{ti - 1}", "VEC")
                op("ACT", lambda e: e.copy(ltile[:, 0:32], pmisc[:, 0:32]))
                mk(f"lgev_{ti}")
                wt("vector", f"lgev_{ti}", "ACT")
                op("VEC", lambda e: e.max(ltile[:, 32:40], ltile[:, 0:32]))
                op("VEC", lambda e: e.max_index(idxu[:, 0:8], ltile[:, 32:40],
                                                ltile[:, 0:32]))
                op("VEC", lambda e, ti=ti: e.tensor_copy(iw[:, ti, 0:4], idxu[:, 0:4]))
                op("VEC", lambda e: e.tensor_scalar(ltile[:, 40:41], ltile[:, 32:33],
                                                    -1.0, None, OP.mult))
                mk(f"topv_{ti}")
                wt("scalar", f"topv_{ti}", "VEC")
                op("ACT", lambda e: e.activation(ltile[:, 44:48], ltile[:, 32:36],
                                                 AF.Exp, ltile[:, 40:41], 1.0))
                mk(f"tope_{ti}")
                wt("vector", f"tope_{ti}", "ACT")
                op("VEC", lambda e: e.tensor_reduce(ltile[:, 41:42], ltile[:, 44:48],
                                                    AX.X, OP.add))
                op("VEC", lambda e: e.reciprocal(ltile[:, 42:43], ltile[:, 41:42]))
                op("VEC", lambda e, ti=ti: e.tensor_scalar(iw[:, ti, 4:8],
                                                           ltile[:, 44:48],
                                                           ltile[:, 42:43], None,
                                                           OP.mult))
                mk(f"top_{ti}")
            mk("route_done")
            maybe_cut("route")

            for ti in range(32):
                wt("vector", "route_done", "VEC")
                if ti:
                    wt("vector", f"posev_{ti - 1}", "ACT")
                for k in range(4):
                    op("VEC", lambda e, ti=ti, k=k: e.tensor_scalar(
                        ohb[:, k, :], cb["iota32"][:], iw[:, ti, k:k + 1], None,
                        OP.is_equal))
                op("VEC", lambda e: e.tensor_tensor(ohb[:, 0, :], ohb[:, 0, :],
                                                    ohb[:, 1, :], OP.add))
                op("VEC", lambda e: e.tensor_tensor(ohb[:, 2, :], ohb[:, 2, :],
                                                    ohb[:, 3, :], OP.add))
                op("VEC", lambda e: e.tensor_tensor(ohb[:, 0, :], ohb[:, 0, :],
                                                    ohb[:, 2, :], OP.add))
                mk(f"oh_{ti}")
                wt("tensor", f"oh_{ti}", "VEC")
                if ti:
                    wt("tensor", f"runq_{ti - 1}", "VEC")
                op("MM", lambda e: e.matmul(pmisc[0:32, 32:164], ohb[:, 0, :],
                                            cb["trilT"][:], start=True, stop=True))
                mk(f"pre_{ti}")
                wt("vector", f"pre_{ti}", "MM")
                if ti == 0:
                    op("VEC", lambda e: e.memset(ltile[0:32, 50:51], 0.0))
                op("VEC", lambda e: e.tensor_scalar(ltile[0:32, 52:180],
                                                    pmisc[0:32, 32:160],
                                                    ltile[0:32, 50:51], None, OP.add))
                op("VEC", lambda e: e.tensor_tensor(ltile[0:32, 50:51],
                                                    ltile[0:32, 50:51],
                                                    pmisc[0:32, 160:161], OP.add))
                mk(f"runq_{ti}")
                wt("tensor", f"runq_{ti}", "VEC")
                op("MM", lambda e: e.transpose(pavd[0:128, 0:32],
                                               ltile[0:32, 52:180],
                                               cb["identf"][0:32, 0:32]))
                mk(f"post_{ti}")
                wt("scalar", f"post_{ti}", "MM")
                op("ACT", lambda e, ti=ti: e.copy(postm[:, ti, :], pavd[:, 0:32]))
                mk(f"posev_{ti}")
            wt("sync", "runq_31", "VEC")
            op("SST", lambda e: e.dma_start(
                out=d["cnt_loc"][:].rearrange("a b -> b a"),
                in_=ltile[0:32, 50:51]))
            mk("cnt_st")
            wt("gpsimd", "cnt_st", "SST")
            op("COL", lambda e: e.collective_compute(
                "AllGather", OP.bypass, RGRP, [d["cnt_loc"][:]], [d["cnt_all"][:]]))
            mk("ag_cnt")
            wt("sync", "ag_cnt", "COL")
            op("LD", lambda e: e.dma_start(
                out=ltile[0:32, 56:64],
                in_=d["cnt_all"][:].rearrange("c e -> e c"),
                allow_slow_non_contiguous=True))
            mk("cnt_ld")
            wt("vector", "cnt_ld", "LD")
            op("VEC", lambda e: e.tensor_tensor(ltile[0:32, 56:64], ltile[0:32, 56:64],
                                                cb["prevmask"][:], OP.mult))
            op("VEC", lambda e: e.tensor_reduce(ltile[0:32, 64:65], ltile[0:32, 56:64],
                                                AX.X, OP.add))
            mk("offs_v")
            wt("tensor", "offs_v", "VEC")
            wt("tensor", "posev_31", "ACT")
            op("MM", lambda e: e.transpose(pavd[0:1, 32:64], ltile[0:32, 64:65],
                                           cb["identf"][0:32, 0:32]))
            mk("offs_t")
            wt("scalar", "offs_t", "MM")
            op("ACT", lambda e: e.copy(ftt[0:1, 0:32], pavd[0:1, 32:64]))
            mk("offs_row")
            wt("tensor", "offs_row", "ACT")
            op("MM", lambda e: e.matmul(pmisc[:, 192:224], cb["ones_row1f"][:],
                                        ftt[0:1, 0:32], start=True, stop=True))
            mk("offs_bc")
            wt("scalar", "offs_bc", "MM")
            op("ACT", lambda e: e.copy(ofsb[:, :], pmisc[:, 192:224]))
            mk("offs_done")
            wt("vector", "offs_done", "ACT")
            mk("hl_done")
            maybe_cut("offs")

            for ti in range(32):
                wt("vector", "hl_done", "VEC")
                for k in range(4):
                    op("VEC", lambda e, ti=ti, k=k: e.tensor_scalar(
                        ohb[:, k, :], cb["iota32"][:], iw[:, ti, k:k + 1], None,
                        OP.is_equal))
                op("VEC", lambda e, ti=ti: e.tensor_tensor(
                    ltile[:, 96:128], postm[:, ti, :], ofsb[:, :], OP.add))
                for k in range(4):
                    op("VEC", lambda e, k=k: e.tensor_tensor(
                        ltile[:, 128:160], ltile[:, 96:128], ohb[:, k, :], OP.mult))
                    op("VEC", lambda e, k=k: e.tensor_reduce(
                        ltile[:, 160 + k:161 + k], ltile[:, 128:160], AX.X, OP.add))
                op("VEC", lambda e: e.tensor_scalar(ltile[:, 164:168],
                                                    ltile[:, 160:164], float(CAP),
                                                    None, OP.is_lt))
                op("VEC", lambda e, ti=ti: e.tensor_copy(keptv[:, 0:4, ti],
                                                         ltile[:, 164:168]))
                op("VEC", lambda e, ti=ti: e.tensor_tensor(
                    iw[:, ti, 4:8], iw[:, ti, 4:8], ltile[:, 164:168], OP.mult))
                op("VEC", lambda e, ti=ti: e.scalar_tensor_tensor(
                    ltile[:, 184:188], iw[:, ti, 0:4], float(CAP),
                    ltile[:, 160:164], OP.mult, OP.add))
                for k in range(4):
                    op("VEC", lambda e, k=k: e.tensor_tensor(
                        ltile[:, 128:160], ohb[:, k, :], cb["ftab"][:], OP.mult))
                    op("VEC", lambda e, k=k: e.tensor_reduce(
                        ltile[:, 200 + k:201 + k], ltile[:, 128:160], AX.X, OP.add))
                op("VEC", lambda e: e.tensor_tensor(ltile[:, 188:192],
                                                    ltile[:, 200:204],
                                                    ltile[:, 160:164], OP.add))
                op("VEC", lambda e: e.tensor_tensor(ltile[:, 188:192],
                                                    ltile[:, 188:192],
                                                    ltile[:, 164:168], OP.mult))
                op("VEC", lambda e, ti=ti: e.tensor_copy(gcv[:, 0:4, ti],
                                                         ltile[:, 188:192]))
                op("VEC", lambda e: e.tensor_scalar(ltile[:, 184:188],
                                                    ltile[:, 184:188], 32767.0,
                                                    None, OP.min))
                op("VEC", lambda e, ti=ti: e.tensor_copy(gsv[:, 0:4, ti],
                                                         ltile[:, 184:188]))
                mk(f"gidx_{ti}")
            mk("gidx_done")
            maybe_cut("gidx")

            def fold_pass(src, base_slot, extra_xld_wait):
                for k in range(4):
                    sl = base_slot + k
                    wt("tensor", "gidx_done", "VEC")
                    if sl:
                        wt("tensor", f"fev_{sl - 1}", "ACT")
                    op("MM", lambda e, src=src, k=k: e.transpose(
                        pavd[0:32, 64:192], src[:, k, :], cb["identf"][:, 0:128]))
                    mk(f"ftr_{sl}")
                    wt("scalar", f"ftr_{sl}", "MM")
                    if sl:
                        wt("scalar", f"fst_{sl - 1}", "SST")
                    op("ACT", lambda e: e.copy(
                        ftt[0:32, :].rearrange("p (q a) -> p a q", q=16),
                        pavd[0:32, 64:192].rearrange("p (a q) -> p a q", a=8)))
                    mk(f"fev_{sl}")
                    wt("sync", f"fev_{sl}", "ACT")
                    op("SST", lambda e, sl=sl: e.dma_start(
                        out=d["rt_scr"][sl].rearrange("(q t a) -> t q a", q=16, a=8),
                        in_=ftt[0:32, :].rearrange("p (q a) -> p q a", q=16)))
                    mk(f"fst_{sl}")
                    wt("gpsimd", f"fst_{sl}", "SST")
                    if extra_xld_wait:
                        wt("gpsimd", extra_xld_wait, "GST")
                    for grp in range(8):
                        op("XLD", lambda e, sl=sl, grp=grp, k=k: e.dma_start(
                            out=idx16[grp * 16:(grp + 1) * 16,
                                      k * 256:(k + 1) * 256],
                            in_=d["rt_scr"][sl].rearrange("(p c) -> p c", p=16)))
                    mk(f"fld_{sl}")

            fold_pass(gsv, 0, None)
            mk("fold_done")
            maybe_cut("fold")

            for k in range(4):
                wt("vector", "hl_done", "VEC")
                wt("vector", "gidx_done", "VEC")
                op("VEC", lambda e, k=k: e.tensor_tensor(
                    ltile[:, 224:256], cb["gid_hi"][:], keptv[:, k, :], OP.mult))
                op("VEC", lambda e, k=k: e.tensor_tensor(
                    ltile[:, 256:288], cb["gid_lo"][:], keptv[:, k, :], OP.mult))
                mk(f"hlk_{k}")
                for jj in range(8):
                    par = jj % 2
                    if k or jj >= 2:
                        pk, pj = (k, jj - 2) if jj >= 2 else (k - 1, jj + 6)
                        wt("vector", f"scat_{pk}_{pj}", "GST")
                    op("VEC", lambda e, par=par: e.memset(pay[:, par, :, :], 0.0))
                    op("VEC", lambda e, par=par, jj=jj: e.tensor_copy(
                        pay[:, par, :, 0:1],
                        ltile[:, 224 + jj * 4:224 + (jj + 1) * 4].rearrange(
                            "p (a o) -> p a o", o=1)))
                    op("VEC", lambda e, par=par, jj=jj: e.tensor_copy(
                        pay[:, par, :, 1:2],
                        ltile[:, 256 + jj * 4:256 + (jj + 1) * 4].rearrange(
                            "p (a o) -> p a o", o=1)))
                    mk(f"payb_{k}_{jj}")
                    wt("gpsimd", f"payb_{k}_{jj}", "VEC")
                    wt("gpsimd", "slotz", "GLD")
                    wt("gpsimd", f"fld_{k}", "XLD")
                    op("GST", lambda e, k=k, jj=jj, par=par: e.dma_scatter_add(
                        d["slot_loc"][:], pay[:, par, :, :],
                        idx16[:, k * 256 + jj * 32:k * 256 + (jj + 1) * 32],
                        512, 512, 128))
                    mk(f"scat_{k}_{jj}")
            mk("scat_done")
            wt("gpsimd", "scat_done", "GST")
            op("COL", lambda e: e.collective_compute(
                "ReduceScatter", OP.add, RGRP, [d["slot_loc"][:]], [d["slot_rs"][:]]))
            mk("rs_slot")
            fold_pass(gcv, 4, "scat_done")
            mk("cfold_done")
            maybe_cut("scat")

            for tc in range(8):
                for oc in range(4):
                    wt("tensor", "shgu_ld", "XLD")
                    wt("tensor", "h1fm_ld", "LD")
                    if tc or oc:
                        po, pt = (oc - 1, tc) if oc else (3, tc - 1)
                        wt("tensor", f"shmul_{pt}_{po}", "VEC")
                    for half, pp in ((0, ps0), (1, ps1)):
                        for fc in range(8):
                            def bg(e, oc=oc, fc=fc, tc=tc, half=half, pp=pp):
                                return e.matmul(
                                    pp[:, 0:512],
                                    wgu_sb[:, fc,
                                           half * 512 + oc * 128:
                                           half * 512 + (oc + 1) * 128],
                                    h1fm[:, fc, tc * 512:(tc + 1) * 512],
                                    start=(fc == 0), stop=(fc == 7))
                            if fc == 7 and half == 1:
                                op("MM", bg)
                            else:
                                opn("MM", bg)
                    mk(f"shmm_{tc}_{oc}")
                    wt("scalar", f"shmm_{tc}_{oc}", "MM")
                    if tc == 0 and oc == 0:
                        wt("scalar", "scat_done", "GST")
                    if tc or oc:
                        po, pt = (oc - 1, tc) if oc else (3, tc - 1)
                        wt("scalar", f"shmul_{pt}_{po}", "VEC")
                    op("ACT", lambda e: e.activation(silb[:, 0:512], ps0[:, 0:512],
                                                     AF.Silu, 0.0, 1.0))
                    mk(f"shsl_{tc}_{oc}")
                    wt("vector", f"shsl_{tc}_{oc}", "ACT")
                    op("VEC", lambda e, tc=tc, oc=oc: e.tensor_tensor(
                        shact[:, oc, tc * 512:(tc + 1) * 512], silb[:, 0:512],
                        ps1[:, 0:512], OP.mult))
                    mk(f"shmul_{tc}_{oc}")
            mk("sh_done")
            maybe_cut("shared")

            for j in range(EPC):
                wt("sync", "rs_slot", "COL")
                if j:
                    wt("sync", f"dec_v_{j - 1}", "VEC")
                op("LD", lambda e, j=j: e.dma_start(
                    out=decb[0:16, :, :],
                    in_=d["slot_rs"][j * CAP:(j + 1) * CAP, 0:2].rearrange(
                        "(c p) x -> p c x", p=16)))
                mk(f"dec_ld_{j}")
                wt("vector", f"dec_ld_{j}", "LD")
                if j:
                    wt("vector", f"dec_st_{j - 1}", "SST")
                op("VEC", lambda e: e.scalar_tensor_tensor(
                    ltile[0:16, 200:264], decb[0:16, :, 0], 128.0,
                    decb[0:16, :, 1], OP.mult, OP.add))
                op("VEC", lambda e: e.tensor_scalar(
                    ltile[0:16, 200:264], ltile[0:16, 200:264], -1.0, 0.0,
                    OP.add, OP.max))
                mk(f"dec_v_{j}")
                wt("sync", f"dec_v_{j}", "VEC")
                op("SST", lambda e, j=j: e.dma_start(
                    out=d["rt_scr"][8 + j, 0:1024].rearrange("(p c) -> p c", p=16),
                    in_=ltile[0:16, 200:264]))
                mk(f"dec_st_{j}")
                wt("gpsimd", f"dec_st_{j}", "SST")
                for grp in range(8):
                    op("XLD", lambda e, j=j, grp=grp: e.dma_start(
                        out=idx16[grp * 16:(grp + 1) * 16,
                                  1024 + j * 64:1024 + (j + 1) * 64],
                        in_=d["rt_scr"][8 + j, 0:1024].rearrange("(p c) -> p c",
                                                                 p=16)))
                mk(f"dec_ix_{j}")
            mk("dec_done")
            maybe_cut("decode")

            for j in range(EPC):
                wt("gpsimd", "sh_done" if j == 0 else f"egmm_{j - 1}_1_3", "MM")
                op("XLD", lambda e, j=j: e.dma_start(
                    out=wgu_sb[:, :, :],
                    in_=d["w_gu"][j].rearrange("(a p) b -> p a b", p=128)))
                mk(f"ewgu_{j}")
                if j == 0:
                    wt("gpsimd", "slotz", "GLD")
                else:
                    wt("gpsimd", f"edmm_{j - 1}_7", "MM")
                op("XLD", lambda e, j=j: e.dma_start(
                    out=wdn_sb[:, :, :],
                    in_=d["w_dn"][j].rearrange("(a p) b -> p a b", p=128)))
                mk(f"ewdn_{j}")
                wt("gpsimd", f"dec_ix_{j}", "XLD")
                wt("gpsimd", "ag_h1", "COL")
                if j:
                    wt("gpsimd", f"egmm_{j - 1}_1_3", "MM")
                for hf in range(2):
                    op("GLD", lambda e, j=j, hf=hf: e.dma_gather(
                        disp[:, hf, :, :], d["h1_all"][:],
                        idx16[:, 1024 + j * 64 + hf * 32:
                              1024 + j * 64 + (hf + 1) * 32],
                        512, 512, 1024, transpose=True))
                mk(f"dgath_{j}")
                if j:
                    wt("gpsimd", f"eo_st_{j - 1}_7", "SST")
                    op("COL", lambda e, j=j: e.collective_compute(
                        "AllGather", OP.bypass, RGRP,
                        [d["eo_loc"][(j - 1) * CAP:j * CAP, :]],
                        [d["eo_all"][(j - 1) * 8 * CAP:j * 8 * CAP, :]]))
                    mk(f"ag_eo_{j - 1}")
                for hf in range(2):
                    for oc in range(4):
                        wt("tensor", f"ewgu_{j}", "XLD")
                        wt("tensor", f"dgath_{j}", "GLD")
                        if hf or oc:
                            po, ph = (oc - 1, hf) if oc else (3, hf - 1)
                            wt("tensor", f"emul_{j}_{ph}_{po}", "VEC")
                        elif j:
                            wt("tensor", f"emul_{j - 1}_1_3", "VEC")
                        for half, pp in ((0, ps0), (1, ps1)):
                            for fc in range(8):
                                def bg(e, oc=oc, fc=fc, hf=hf, half=half, pp=pp):
                                    return e.matmul(
                                        pp[:, 0:512],
                                        wgu_sb[:, fc,
                                               half * 512 + oc * 128:
                                               half * 512 + (oc + 1) * 128],
                                        disp[:, hf, fc, :],
                                        start=(fc == 0), stop=(fc == 7))
                                if fc == 7 and half == 1:
                                    op("MM", bg)
                                else:
                                    opn("MM", bg)
                        mk(f"egmm_{j}_{hf}_{oc}")
                        wt("scalar", f"egmm_{j}_{hf}_{oc}", "MM")
                        if hf or oc:
                            po, ph = (oc - 1, hf) if oc else (3, hf - 1)
                            wt("scalar", f"emul_{j}_{ph}_{po}", "VEC")
                        op("ACT", lambda e: e.activation(silb[:, 0:512],
                                                         ps0[:, 0:512],
                                                         AF.Silu, 0.0, 1.0))
                        mk(f"esil_{j}_{hf}_{oc}")
                        wt("vector", f"esil_{j}_{hf}_{oc}", "ACT")
                        op("VEC", lambda e, hf=hf, oc=oc: e.tensor_tensor(
                            acte[:, oc, hf * 512:(hf + 1) * 512], silb[:, 0:512],
                            ps1[:, 0:512], OP.mult))
                        mk(f"emul_{j}_{hf}_{oc}")
                for g in range(8):
                    wt("tensor", f"ewdn_{j}", "XLD")
                    wt("tensor", f"emul_{j}_1_3", "VEC")
                    if g:
                        wt("tensor", f"eoev_{j}_{g - 1}", "ACT")
                    elif j:
                        wt("tensor", f"eoev_{j - 1}_7", "ACT")
                    for oc in range(2):
                        for ic in range(4):
                            def bd(e, g=g, oc=oc, ic=ic):
                                return e.matmul(
                                    pavd[:, oc * 512:(oc + 1) * 512],
                                    acte[:, ic, g * 128:(g + 1) * 128],
                                    wdn_sb[:, ic, oc * 512:(oc + 1) * 512],
                                    start=(ic == 0), stop=(ic == 3))
                            if oc == 1 and ic == 3:
                                op("MM", bd)
                            else:
                                opn("MM", bd)
                    mk(f"edmm_{j}_{g}")
                    wt("scalar", f"edmm_{j}_{g}", "MM")
                    if g:
                        wt("scalar", f"eo_st_{j}_{g - 1}", "SST")
                    elif j:
                        wt("scalar", f"eo_st_{j - 1}_7", "SST")
                    op("ACT", lambda e: e.copy(eost[:, :], pavd[:, 0:1024]))
                    mk(f"eoev_{j}_{g}")
                    wt("sync", f"eoev_{j}_{g}", "ACT")
                    op("SST", lambda e, j=j, g=g: e.dma_start(
                        out=d["eo_loc"][(j * 8 + g) * 128:(j * 8 + g + 1) * 128, :],
                        in_=eost[:, :]))
                    mk(f"eo_st_{j}_{g}")
            mk("eo_done")
            wt("gpsimd", "eo_st_3_7", "SST")
            op("COL", lambda e: e.collective_compute(
                "AllGather", OP.bypass, RGRP,
                [d["eo_loc"][3 * CAP:4 * CAP, :]],
                [d["eo_all"][3 * 8 * CAP:4 * 8 * CAP, :]]))
            mk("ag_eo")
            maybe_cut("experts")

            wt("gpsimd", "egmm_3_1_3", "MM")
            op("XLD", lambda e: e.dma_start(
                out=shdn_sb[:, :, :],
                in_=d["sh_dn"][:].rearrange("(a p) b -> p a b", p=128)))
            mk("shdn_ld")
            for j in range(8):
                for k in range(4):
                    par = k % 2
                    gk = bfv(FB_GK + par * 4096, [4, 1024])
                    wt("gpsimd", "ag_eo", "COL")
                    wt("gpsimd", "cfold_done", "XLD")
                    if j or k >= 2:
                        pj, pk2 = (j, k - 2) if k >= 2 else (j - 1, k + 2)
                        wt("gpsimd", f"cacc_{pj}_{pk2}", "VEC")
                    if j == 0 and k < 2:
                        wt("gpsimd", f"edmm_3_7", "MM")
                    op("GLD", lambda e, j=j, k=k, gk=gk: e.dma_gather(
                        gk[:, :, :], d["eo_all"][:],
                        idx16[:, k * 256 + j * 32:k * 256 + (j + 1) * 32],
                        512, 512, 1024))
                    mk(f"cg_{j}_{k}")
                    wt("vector", f"cg_{j}_{k}", "GLD")
                    if k == 0 and j:
                        wt("vector", f"fln_{j * 4 - 1}", "VEC")
                    for b in range(4):
                        accb = fv(F_ACC0 + b * 1024, [1024])
                        if k == 0:
                            op("VEC", lambda e, j=j, k=k, b=b, gk=gk, accb=accb:
                               e.tensor_scalar(accb[:, :], gk[:, b, :],
                                               iw[:, j * 4 + b, 4 + k:5 + k],
                                               None, OP.mult))
                        else:
                            op("VEC", lambda e, j=j, k=k, b=b, gk=gk, accb=accb:
                               e.scalar_tensor_tensor(accb[:, :], gk[:, b, :],
                                                      iw[:, j * 4 + b, 4 + k:5 + k],
                                                      accb[:, :], OP.mult, OP.add))
                    mk(f"cacc_{j}_{k}")
                for b in range(4):
                    t32 = j * 4 + b
                    accb = fv(F_ACC0 + b * 1024, [1024])
                    wt("tensor", "shdn_ld", "XLD")
                    wt("tensor", "sh_done", "VEC")
                    if t32:
                        wt("tensor", f"fadd_{t32 - 1}", "VEC")
                    for oc in range(2):
                        for ic in range(4):
                            def bsd(e, t32=t32, oc=oc, ic=ic):
                                return e.matmul(
                                    pavd[:, oc * 512:(oc + 1) * 512],
                                    shact[:, ic, t32 * 128:(t32 + 1) * 128],
                                    shdn_sb[:, ic, oc * 512:(oc + 1) * 512],
                                    start=(ic == 0), stop=(ic == 3))
                            if oc == 1 and ic == 3:
                                op("MM", bsd)
                            else:
                                opn("MM", bsd)
                    mk(f"cdmm_{t32}")
                    if t32:
                        wt("sync", f"fadd_{t32 - 1}", "VEC")
                    else:
                        wt("sync", "cfold_done", "ACT")
                    op("LD", lambda e, t32=t32: e.dma_start(
                        out=xrf[:, :],
                        in_=d["h1_tm_d"][t32 * 128:(t32 + 1) * 128, :]))
                    mk(f"cxld_{t32}")
                    wt("vector", f"cdmm_{t32}", "MM")
                    wt("vector", f"cxld_{t32}", "LD")
                    if t32:
                        wt("vector", f"fsq_{t32 - 1}", "ACT")
                    op("VEC", lambda e: e.tensor_tensor(h1tf[:, :], xrf[:, :],
                                                        pavd[:, 0:1024], OP.add))
                    op("VEC", lambda e, accb=accb: e.tensor_tensor(
                        h1tf[:, :], h1tf[:, :], accb[:, :], OP.add))
                    op("VEC", lambda e: e.tensor_reduce(scal[:, 0:1], h1tf[:, :],
                                                        AX.X, OP.add))
                    mk(f"fadd_{t32}")
                    wt("scalar", f"fadd_{t32}", "VEC")
                    op("ACT", lambda e: e.activation(eost[:, :], h1tf[:, :],
                                                     AF.Square, 0.0, 1.0,
                                                     accum_out=scal[:, 1:2]))
                    mk(f"fsq_{t32}")
                    wt("vector", f"fsq_{t32}", "ACT")
                    op("VEC", lambda e: e.tensor_scalar(scal[:, 2:3], scal[:, 0:1],
                                                        1.0 / D, None, OP.mult))
                    op("VEC", lambda e: e.tensor_tensor(scal[:, 3:4], scal[:, 2:3],
                                                        scal[:, 2:3], OP.mult))
                    op("VEC", lambda e: e.tensor_scalar(scal[:, 4:5], scal[:, 1:2],
                                                        1.0 / D, None, OP.mult))
                    op("VEC", lambda e: e.tensor_tensor(scal[:, 5:6], scal[:, 4:5],
                                                        scal[:, 3:4], OP.subtract))
                    op("VEC", lambda e: e.tensor_scalar(scal[:, 6:7], scal[:, 5:6],
                                                        1e-5, None, OP.add))
                    mk(f"fv_{t32}")
                    wt("scalar", f"fv_{t32}", "VEC")
                    op("ACT", lambda e: e.activation(scal[:, 7:8], scal[:, 6:7],
                                                     AF.Sqrt, 0.0, 1.0))
                    mk(f"fsd_{t32}")
                    wt("vector", f"fsd_{t32}", "ACT")
                    op("VEC", lambda e: e.reciprocal(scal[:, 8:9], scal[:, 7:8]))
                    op("VEC", lambda e: e.tensor_scalar(scal[:, 9:10], scal[:, 2:3],
                                                        scal[:, 8:9], -1.0,
                                                        OP.mult, OP.mult))
                    mk(f"frs_{t32}")
                    wt("scalar", f"frs_{t32}", "VEC")
                    if t32:
                        wt("vector", f"out_st_{t32 - 1}", "SST")
                    op("VEC", lambda e: e.tensor_scalar(outf[:, :], h1tf[:, :],
                                                        scal[:, 8:9], scal[:, 9:10],
                                                        OP.mult, OP.add))
                    mk(f"fln_{t32}")
                    wt("sync", f"fln_{t32}", "VEC")
                    op("SST", lambda e, t32=t32: e.dma_start(
                        out=d["out"][t32 * 128:(t32 + 1) * 128, :],
                        in_=outf[:, :]))
                    mk(f"out_st_{t32}")
            mk("ffn_done")

        def run_prog():
            try:
                program()
            except _Cut:
                for t32 in range(32):
                    op("SST", lambda e, t32=t32: e.dma_start(
                        out=d["out"][t32 * 128:(t32 + 1) * 128, :],
                        in_=d["x_tm"][t32 * 128:(t32 + 1) * 128, :]))
            if state["emit"] and state["en"] == "sync":
                for kk in sorted(val):
                    if val[kk] and kk != "SST":
                        state["e"].wait_ge(sems[kk], val[kk])
            if "dbg_ag" in dbg:
                for i in range(8):
                    op("SST", lambda e, i=i: e.dma_start(
                        out=d["dbg_ag"][:, i * S:(i + 1) * S],
                        in_=AB[:, i * S:(i + 1) * S]))
                op("SST", lambda e: e.dma_start(out=d["dbg_f32"][:], in_=A32[:]))
            if "dbg_cnt" in dbg:
                op("SST", lambda e: e.dma_start(out=d["dbg_cnt"][:],
                                                in_=d["cnt_all"][:]))
                op("SST", lambda e: e.dma_start(out=d["dbg_srs"][:],
                                                in_=d["slot_rs"][:]))
                op("SST", lambda e: e.dma_start(out=d["dbg_eo"][:],
                                                in_=d["eo_loc"][:]))
                op("SST", lambda e: e.dma_start(out=d["dbg_rt"][0:8],
                                                in_=d["rt_scr"][0:8]))
                op("SST", lambda e: e.dma_start(out=d["dbg_rt"][8:12, 0:1024],
                                                in_=d["rt_scr"][8:12, 0:1024]))
            if state["emit"] and state["en"] == "sync":
                state["e"].wait_ge(sems["SST"], val["SST"])

        run_prog()
        state["emit"] = True
        final_counts = dict(val)

        def run_engine(en):
            def body(e):
                for k in val:
                    val[k] = 0
                state["en"] = en
                state["e"] = e
                run_prog()
            return body

        blk.sync(run_engine("sync"))
        blk.vector(run_engine("vector"))
        blk.tensor(run_engine("tensor"))
        blk.scalar(run_engine("scalar"))
        blk.gpsimd(run_engine("gpsimd"))

    nc.compile()
    return nc, dbg


_CACHE = {}


def kernel(**inputs):
    key = "main"
    if key not in _CACHE:
        _CACHE[key] = build_nc()
    nc, dbg = _CACHE[key]
    in_maps = percore_inputs(inputs)
    res = run_bass_kernel_spmd(nc, in_maps, core_ids=list(range(N_CORES)))
    out = np.stack([res.results[c]["out"] for c in range(N_CORES)], axis=0)
    return out


# version 04 (final)
# speedup vs baseline: 1.4373x; 1.4060x over previous
from contextlib import ExitStack
import numpy as np
import ml_dtypes
import concourse.bacc as bacc
import concourse.mybir as mybir
from concourse.bass_utils import run_bass_kernel_spmd

F32 = mybir.dt.float32
BF16 = mybir.dt.bfloat16
I16 = mybir.dt.int16
AF = mybir.ActivationFunctionType
OP = mybir.AluOpType
AX = mybir.AxisListType
BF = ml_dtypes.bfloat16

S, D, H, DH, ROT = 4096, 1024, 16, 64, 32
NG, HPG = 4, 4
E, EPC, TOPK, CAP, I_FF = 32, 4, 4, 1024, 512
RGRP = [[0, 1, 2, 3, 4, 5, 6, 7]]
N_CORES, NQT, NTC = 8, 8, 8

NBF, NF32 = 78848, 7536
O_WG, O_XB, O_QG, O_KG, O_VG = 0, 10240, 18432, 26624, 34816
O_AG, O_PT, O_UNN, O_RB = 43008, 75776, 77824, 78336
O_WO, O_H1B = 0, 8192
O_H1FM, O_SHACT, O_DISP, O_EOUT = 43008, 18432, 34816, 0
FB_EOST, FB_PAY, FB_WDN, FB_GK, FB_ACTE, FB_WGU = 0, 1024, 2048, 2048, 6144, 10240
O_SHACT2, O_DISP2 = 18432, 34816
FB_OH, FB_DEC, FB_GW = 75776, 76032, 76160
F_POS, F_IW, F_GS, F_GC, F_TT, F_OFS, F_HL, F_KEPT = 5232, 6256, 6512, 6640, 6768, 6896, 6928, 6992
F_ACC0, F_H1TF, F_TMPF, F_XRESF = 0, 4192, 5232, 6512
F_RECIP, F_MBIAS, F_RM, F_XRES, F_TMP, F_H1T, F_SCAL, F_RT = \
    0, 1024, 1060, 1120, 3168, 4192, 5216, 5232

SEM_OWNER = {"LD": "sync", "WLD": "sync", "SST": "sync",
             "XLD": "gpsimd", "GLD": "gpsimd", "GST": "gpsimd", "COL": "gpsimd",
             "MM": "tensor", "ACT": "scalar", "VEC": "vector"}
SEM_BY = {"LD": 16, "WLD": 16, "SST": 16, "XLD": 16, "GLD": 16, "GST": 16,
          "COL": 1, "MM": 1, "ACT": 1, "VEC": 1}
ENGS = ["sync", "gpsimd", "tensor", "scalar", "vector"]


def host_constants():
    c = {}
    t = np.arange(S, dtype=np.float64)
    inv = 1.0 / (10000.0 ** (np.arange(0, ROT, 2, dtype=np.float64) / ROT))
    ang = t[None, :] * inv[:, None]
    cos16, sin16 = np.cos(ang), np.sin(ang)
    p = np.arange(128)
    rot_row = (p % 64) < 32
    ce, se = np.tile(cos16, (8, 1)), np.tile(sin16, (8, 1))
    ce[~rot_row] = 1.0
    se[~rot_row] = 0.0
    c["cos_ext"], c["sin_ext"] = ce.astype(BF), se.astype(BF)
    m = np.zeros((128, 4, 512), np.float32)
    for j in range(4):
        for kp in range(128):
            m[kp, j, j * 128 + kp:] = 1.0
    c["mask01"] = m.astype(BF)
    c["ones_col"] = np.ones((128, 1), BF)
    c["ones_row1"] = np.ones((1, 128), BF)
    c["ones_row1f"] = np.ones((1, 128), np.float32)
    tr = np.zeros((128, 132), np.float32)
    for tp in range(128):
        tr[0:tp, tp] = 1.0
    tr[:, 128] = 1.0
    c["trilT"] = tr.astype(BF)
    c["iota32"] = np.tile(np.arange(32).astype(BF)[None, :], (128, 1))
    c["identf"] = np.eye(128, dtype=np.float32)
    return c


def percore_inputs(inputs):
    x = np.asarray(inputs["L0_attn_x"], dtype=np.float32)
    w_q = np.ascontiguousarray(np.asarray(inputs["L0_attn_w_q"], dtype=np.float32))
    w_kv = np.asarray(inputs["L0_attn_w_kv"], dtype=np.float32)
    w_out = np.ascontiguousarray(np.asarray(inputs["L0_attn_w_out"], dtype=np.float32))
    gate_w = np.asarray(inputs["L0_ffn_gate_w"], dtype=np.float32)
    w_gu = np.asarray(inputs["L0_ffn_w_gate_up"], dtype=np.float32)
    w_dn = np.asarray(inputs["L0_ffn_w_down"], dtype=np.float32)
    sh_gu = np.ascontiguousarray(np.asarray(inputs["L0_ffn_shared_w_gate_up"], dtype=np.float32))
    sh_dn = np.ascontiguousarray(np.asarray(inputs["L0_ffn_shared_w_down"], dtype=np.float32))
    consts = host_constants()
    maps = []
    for cid in range(N_CORES):
        m = dict(consts)
        m["x_tm"] = np.ascontiguousarray(x[cid])
        m["x_fmT"] = np.ascontiguousarray(x[cid].T)
        m["w_q"] = w_q
        m["w_k"] = np.ascontiguousarray(w_kv[:, 0:D])
        m["w_v"] = np.ascontiguousarray(w_kv[:, D:2 * D])
        m["w_out"] = w_out
        m["gate_wT"] = np.ascontiguousarray(gate_w.T)
        m["w_gu"] = np.ascontiguousarray(w_gu[cid * EPC:(cid + 1) * EPC])
        m["w_dn"] = np.ascontiguousarray(w_dn[cid * EPC:(cid + 1) * EPC])
        m["sh_gu"] = sh_gu
        m["sh_dn"] = sh_dn
        pm = np.zeros((32, 8), np.float32)
        pm[:, 0:cid] = 1.0
        m["prevmask"] = pm
        gid = np.zeros((128, 32), np.float32)
        for tau in range(128):
            t = np.arange(32) * 128 + tau
            gid[tau, :] = (t // 512) * (8 * 512) + cid * 512 + (t % 512) + 1
        hi = np.floor_divide(gid, 128.0)
        m["gid_hi"] = hi.astype(BF)
        m["gid_lo"] = (gid - hi * 128.0).astype(BF)
        f = (np.arange(32) % 4) * 8 + np.arange(32) // 4
        m["ftab"] = np.tile((f * 1024.0).astype(BF)[None, :], (128, 1))
        maps.append(m)
    return maps


class _Cut(Exception):
    pass


def build_nc(debug=(), cut=None):
    nc = bacc.Bacc("TRN2", target_bir_lowering=False,
                   detect_race_conditions=False)
    d = {}

    def din(n, s, t):
        d[n] = nc.dram_tensor(n, s, t, kind="ExternalInput")

    def dout(n, s, t):
        d[n] = nc.dram_tensor(n, s, t, kind="ExternalOutput")

    def dint(n, s, t):
        d[n] = nc.dram_tensor(n, s, t)

    din("x_tm", [S, D], F32)
    din("x_fmT", [D, S], F32)
    din("w_q", [D, D], F32)
    din("w_k", [D, D], F32)
    din("w_v", [D, D], F32)
    din("w_out", [D, D], F32)
    din("gate_wT", [D, E], F32)
    din("w_gu", [EPC, D, 2 * I_FF], F32)
    din("w_dn", [EPC, I_FF, D], F32)
    din("sh_gu", [D, 2 * I_FF], F32)
    din("sh_dn", [I_FF, D], F32)
    din("cos_ext", [128, S], BF16)
    din("sin_ext", [128, S], BF16)
    din("mask01", [128, 4, 512], BF16)
    din("ones_col", [128, 1], BF16)
    din("ones_row1", [1, 128], BF16)
    din("ones_row1f", [1, 128], F32)
    din("trilT", [128, 132], BF16)
    din("iota32", [128, 32], BF16)
    din("identf", [128, 128], F32)
    din("prevmask", [32, 8], F32)
    din("gid_hi", [128, 32], BF16)
    din("gid_lo", [128, 32], BF16)
    din("ftab", [128, 32], BF16)
    dint("wb_q", [D, D], BF16)
    dint("wb_k", [D, D], BF16)
    dint("wq_c", [D, D], BF16)
    dint("wk_c", [D, D], BF16)
    dint("wv_c", [D, D], BF16)
    dint("wo_c", [D, D], BF16)
    dint("h1_tm_d", [S, D], F32)
    dint("h1_pub", [S, D], BF16)
    dint("h1_all", [N_CORES * S, D], BF16)
    dint("cnt_loc", [1, E], F32)
    dint("cnt_all", [N_CORES, E], F32)
    dint("slot_loc", [E * CAP, 128], BF16)
    dint("slot_rs", [EPC * CAP, 128], BF16)
    dint("rt_scr", [12, S], F32)
    dint("eo_loc", [EPC * CAP, D], BF16)
    dint("eo_all", [E * CAP, D], BF16)
    dout("out", [S, D], F32)
    dbg = []
    if "h1" in debug:
        dout("dbg_h1", [S, D], F32)
        dbg.append("dbg_h1")
    if "arena" in debug:
        dout("dbg_ag", [128, 8 * S], BF16)
        dout("dbg_f32", [128, NF32], F32)
        dbg.append("dbg_ag")
    if "ffn" in debug:
        dout("dbg_cnt", [N_CORES, E], F32)
        dout("dbg_srs", [EPC * CAP, 128], BF16)
        dout("dbg_eo", [EPC * CAP, D], BF16)
        dout("dbg_rt", [12, S], F32)
        dbg.append("dbg_cnt")

    marks = {}
    state = {"emit": False, "en": None, "e": None}
    val = {}
    CUT = cut

    with ExitStack() as ctx:
        AB = ctx.enter_context(nc.sbuf_tensor("AB", [128, NBF], BF16))
        A32 = ctx.enter_context(nc.sbuf_tensor("A32", [128, NF32], F32))
        cb = {}
        for nm, sh, tt in [("cos_e", [128, S], BF16), ("sin_e", [128, S], BF16),
                           ("mask01", [128, 4, 512], BF16), ("ones_col", [128, 1], BF16),
                           ("ones_row1", [1, 128], BF16), ("ones_row1f", [1, 128], F32),
                           ("trilT", [128, 132], BF16),
                           ("iota32", [128, 32], BF16), ("identf", [128, 128], F32),
                           ("prevmask", [32, 8], F32), ("gid_hi", [128, 32], BF16),
                           ("gid_lo", [128, 32], BF16), ("ftab", [128, 32], BF16)]:
            cb[nm] = ctx.enter_context(nc.sbuf_tensor("sb_" + nm, sh, tt))
        idx16 = ctx.enter_context(nc.sbuf_tensor("idx16", [128, 1280], I16))
        idxu = ctx.enter_context(nc.sbuf_tensor("idxu", [128, 8], mybir.dt.uint32))
        ps0 = ctx.enter_context(nc.psum_tensor("ps0", [128, 1024], F32))
        ps1 = ctx.enter_context(nc.psum_tensor("ps1", [128, 1024], F32))
        pavd = ctx.enter_context(nc.psum_tensor("pavd", [128, 1536], F32))
        pmisc = ctx.enter_context(nc.psum_tensor("pmisc", [128, 512], F32))
        sems = {k: ctx.enter_context(nc.semaphore(k)) for k in SEM_OWNER}
        blk = ctx.enter_context(nc.Block())

        def bfv(off, shape):
            n = int(np.prod(shape))
            ap = AB[:, off:off + n]
            if len(shape) > 1:
                kw = {f"a{i}": shape[i] for i in range(len(shape))}
                pat = " ".join(f"a{i}" for i in range(len(shape)))
                ap = ap.rearrange(f"p ({pat}) -> p {pat}", **kw)
            return ap

        def fv(off, shape):
            n = int(np.prod(shape))
            ap = A32[:, off:off + n]
            if len(shape) > 1:
                kw = {f"a{i}": shape[i] for i in range(len(shape))}
                pat = " ".join(f"a{i}" for i in range(len(shape)))
                ap = ap.rearrange(f"p ({pat}) -> p {pat}", **kw)
            return ap

        def op(sem, builder):
            val[sem] = val.get(sem, 0) + SEM_BY[sem]
            if state["emit"] and state["en"] == SEM_OWNER[sem]:
                builder(state["e"]).then_inc(sems[sem], SEM_BY[sem])
                if sem in ("VEC", "ACT"):
                    state["e"].wait_ge(sems[sem], val[sem])

        def opn(sem, builder):
            if state["emit"] and state["en"] == SEM_OWNER[sem]:
                builder(state["e"])

        def mk(label):
            if not state["emit"]:
                marks[label] = dict(val)

        def wt(eng, label, sem):
            if state["emit"] and state["en"] == eng:
                v = marks[label].get(sem, 0)
                if v > 0:
                    state["e"].wait_ge(sems[sem], v)

        def maybe_cut(name):
            if CUT == name:
                raise _Cut()

        def program():
            if "dbg_ag" in dbg:
                op("VEC", lambda e: e.memset(A32[:, :], 0.0))
                op("VEC", lambda e: e.memset(AB[:, :], 0.0))
                mk("dbginit")
                wt("sync", "dbginit", "VEC")
                wt("gpsimd", "dbginit", "VEC")
            for nm in ["cos_ext", "sin_ext", "mask01", "ones_col", "ones_row1",
                       "ones_row1f", "trilT", "iota32", "identf", "prevmask",
                       "gid_hi", "gid_lo", "ftab"]:
                dst = {"cos_ext": "cos_e", "sin_ext": "sin_e"}.get(nm, nm)
                op("LD", lambda e, nm=nm, dst=dst: e.dma_start(out=cb[dst][:], in_=d[nm][:]))
            mk("consts")
            for src, dst in [("w_q", "wq_c"), ("w_k", "wk_c"), ("w_v", "wv_c"),
                             ("w_out", "wo_c")]:
                for hh in range(2):
                    op("XLD", lambda e, s=src, t=dst, hh=hh: e.dma_start(
                        out=d[t][:, hh * 512:(hh + 1) * 512],
                        in_=d[s][:, hh * 512:(hh + 1) * 512]))
            mk("wcast")
            maybe_cut("consts")

            wstage = bfv(O_WG, [2, 1024])
            for si, (src, dst) in enumerate([("wq_c", "wb_q"), ("wk_c", "wb_k")]):
                for fc in range(8):
                    it = si * 8 + fc
                    wt("sync", "wcast", "XLD")
                    if it:
                        wt("sync", f"wbTT_{it - 1}", "VEC")
                    op("WLD", lambda e, s=src, fc=fc: e.dma_start(
                        out=wstage[:, 0, :], in_=d[s][fc * 128:(fc + 1) * 128, :]))
                    mk(f"wbL_{it}")
                    wt("vector", f"wbL_{it}", "WLD")
                    if it:
                        wt("vector", f"wbS_{it - 1}", "GLD")
                    op("VEC", lambda e: e.memset(wstage[:, 1, :], 0.0))
                    s4 = wstage[:, 0, :].rearrange("p (h q) -> p h q", h=16)
                    d4 = wstage[:, 1, :].rearrange("p (h q) -> p h q", h=16)
                    op("VEC", lambda e: e.tensor_scalar(d4[:, :, 0:16], s4[:, :, 16:32],
                                                        -1.0, None, OP.mult))
                    op("VEC", lambda e: e.tensor_scalar(d4[:, :, 16:32], s4[:, :, 0:16],
                                                        1.0, None, OP.mult))
                    mk(f"wbTT_{it}")
                    wt("gpsimd", f"wbTT_{it}", "VEC")
                    op("GLD", lambda e, t=dst, fc=fc: e.dma_start(
                        out=d[t][fc * 128:(fc + 1) * 128, :], in_=wstage[:, 1, :]))
                    mk(f"wbS_{it}")
            mk("wb_done")
            maybe_cut("wb")

            wg = bfv(O_WG, [8, 5, 256])
            xb = bfv(O_XB, [2, 8, 512])
            qg = bfv(O_QG, [2, S])
            kg = bfv(O_KG, [2, S])
            vg = bfv(O_VG, [32, 256])
            agf = bfv(O_AG, [8, S])
            pt = bfv(O_PT, [2, 2, 512])
            unn = bfv(O_UNN, [512])
            rb16 = bfv(O_RB, [512])
            recip = fv(F_RECIP, [1024])
            mbias = fv(F_MBIAS, [36])
            rmx = fv(F_RM, [56])
            tmpa = fv(F_XRES, [2, 1024])

            for g in range(NG):
                wt("sync", "wb_done", "GLD")
                if g > 0:
                    mkname = f"projMM_{g - 1}_{NTC - 1}"
                    wt("sync", mkname, "MM")
                for fc in range(8):
                    for wi, nm in enumerate(["wq_c", "wb_q", "wk_c", "wb_k", "wv_c"]):
                        op("WLD", lambda e, fc=fc, wi=wi, nm=nm: e.dma_start(
                            out=wg[:, fc, wi, :],
                            in_=d[nm][fc * 128:(fc + 1) * 128, g * 256:(g + 1) * 256]))
                mk(f"wg_{g}")

                for tc in range(NTC):
                    par = tc % 2
                    if tc >= 2:
                        wt("gpsimd", f"projMM_{g}_{tc - 2}", "MM")
                    elif g > 0:
                        wt("gpsimd", f"projMM_{g - 1}_{NTC - 1}", "MM")
                    else:
                        wt("gpsimd", "wb_done", "GLD")
                    for fc in range(8):
                        op("XLD", lambda e, fc=fc, tc=tc, par=par: e.dma_start(
                            out=xb[:, par, fc, :],
                            in_=d["x_fmT"][fc * 128:(fc + 1) * 128,
                                           tc * 512:(tc + 1) * 512]))
                    mk(f"xld_{g}_{tc}")

                    wt("tensor", f"xld_{g}_{tc}", "XLD")
                    wt("tensor", f"wg_{g}", "WLD")
                    if tc > 0:
                        wt("tensor", f"ropek_{g}_{tc - 1}", "VEC")
                    elif g > 0:
                        wt("tensor", f"lastexp_{g - 1}", "ACT")
                    for half in range(2):
                        for wi, pbuf in ((0, ps0), (1, ps1)):
                            for fc in range(8):
                                def bq(e, half=half, wi=wi, pbuf=pbuf, fc=fc, par=par):
                                    return e.matmul(
                                        pbuf[:, half * 512:(half + 1) * 512],
                                        wg[:, fc, wi, half * 128:(half + 1) * 128],
                                        xb[:, par, fc, :],
                                        start=(fc == 0), stop=(fc == 7))
                                if fc == 7:
                                    op("MM", bq)
                                else:
                                    opn("MM", bq)
                    mk(f"qmm_{g}_{tc}")
                    wt("vector", f"qmm_{g}_{tc}", "MM")
                    for half in range(2):
                        sl = slice(half * 512, (half + 1) * 512)
                        tsl = slice(tc * 512, (tc + 1) * 512)
                        op("VEC", lambda e, sl=sl, tsl=tsl: e.tensor_tensor(
                            tmpa[:, 0, 0:512], ps1[:, sl], cb["sin_e"][:, tsl], OP.mult))
                        op("VEC", lambda e, sl=sl, tsl=tsl: e.tensor_tensor(
                            tmpa[:, 0, 512:1024], ps0[:, sl], cb["cos_e"][:, tsl], OP.mult))
                        op("VEC", lambda e, half=half, tsl=tsl: e.tensor_tensor(
                            qg[:, half, tsl], tmpa[:, 0, 512:1024], tmpa[:, 0, 0:512],
                            OP.add))
                    mk(f"ropeq_{g}_{tc}")
                    wt("tensor", f"ropeq_{g}_{tc}", "VEC")
                    for half in range(2):
                        for wi, pbuf in ((2, ps0), (3, ps1)):
                            for fc in range(8):
                                def bk(e, half=half, wi=wi, pbuf=pbuf, fc=fc, par=par):
                                    return e.matmul(
                                        pbuf[:, half * 512:(half + 1) * 512],
                                        wg[:, fc, wi, half * 128:(half + 1) * 128],
                                        xb[:, par, fc, :],
                                        start=(fc == 0), stop=(fc == 7))
                                if fc == 7:
                                    op("MM", bk)
                                else:
                                    opn("MM", bk)
                    mk(f"kmm_{g}_{tc}")
                    wt("vector", f"kmm_{g}_{tc}", "MM")
                    for half in range(2):
                        sl = slice(half * 512, (half + 1) * 512)
                        tsl = slice(tc * 512, (tc + 1) * 512)
                        op("VEC", lambda e, sl=sl, tsl=tsl: e.tensor_tensor(
                            tmpa[:, 0, 0:512], ps1[:, sl], cb["sin_e"][:, tsl], OP.mult))
                        op("VEC", lambda e, sl=sl, tsl=tsl: e.tensor_tensor(
                            tmpa[:, 0, 512:1024], ps0[:, sl], cb["cos_e"][:, tsl], OP.mult))
                        op("VEC", lambda e, half=half, tsl=tsl: e.tensor_tensor(
                            kg[:, half, tsl], tmpa[:, 0, 512:1024], tmpa[:, 0, 0:512],
                            OP.add))
                    mk(f"ropek_{g}_{tc}")
                    wt("tensor", f"ropek_{g}_{tc}", "VEC")
                    if tc == 0 and g > 0:
                        wt("tensor", f"lastdone_{g - 1}", "VEC")
                    for sub in range(4):
                        if tc > 0 or sub > 1:
                            psub = f"vev_{g}_{tc if sub > 1 else tc - 1}_" \
                                   f"{sub - 2 if sub > 1 else sub + 2}"
                            wt("tensor", psub, "ACT")
                        pv = ps0 if sub % 2 == 0 else ps1
                        for fc in range(8):
                            def bv(e, sub=sub, fc=fc, par=par, pv=pv):
                                return e.matmul(
                                    pv[:, 0:256],
                                    xb[:, par, fc, sub * 128:(sub + 1) * 128],
                                    wg[:, fc, 4, :],
                                    start=(fc == 0), stop=(fc == 7))
                            if fc == 7:
                                op("MM", bv)
                            else:
                                opn("MM", bv)
                        mk(f"vmm_{g}_{tc}_{sub}")
                        wt("scalar", f"vmm_{g}_{tc}_{sub}", "MM")
                        op("ACT", lambda e, tc=tc, sub=sub, pv=pv: e.copy(
                            vg[:, tc * 4 + sub, :], pv[:, 0:256]))
                        mk(f"vev_{g}_{tc}_{sub}")
                    mk(f"projMM_{g}_{tc}")
                    maybe_cut(f"proj_{g}_{tc}")
                for ch in range(2):
                    for ti, buf in ((0, qg), (1, kg)):
                        si = 2 * ch + ti
                        for qt in range(NQT):
                            it = si * NQT + qt
                            wt("scalar", f"ropek_{g}_{NTC - 1}", "VEC")
                            wt("scalar", f"projMM_{g}_{NTC - 1}", "MM")
                            if it:
                                wt("scalar", f"nmm_{g}_{it - 1}", "MM")
                            op("ACT", lambda e, ch=ch, buf=buf, qt=qt: e.activation(
                                unn[:], buf[:, ch, qt * 512:(qt + 1) * 512],
                                AF.Square, 0.0, 1.0))
                            mk(f"nsq_{g}_{it}")
                            wt("tensor", f"nsq_{g}_{it}", "ACT")
                            if it:
                                wt("tensor", f"nmx_{g}_{it - 1}", "VEC")
                            op("MM", lambda e: e.matmul(
                                pmisc[0:1, 0:512], cb["ones_col"][:], unn[:],
                                start=True, stop=True))
                            mk(f"nmm_{g}_{it}")
                            wt("vector", f"nmm_{g}_{it}", "MM")
                            if ti == 0:
                                op("VEC", lambda e, ch=ch, qt=qt: e.tensor_reduce(
                                    rmx[0:1, ch * 8 + qt:ch * 8 + qt + 1],
                                    pmisc[0:1, 0:512], AX.X, OP.max))
                            else:
                                op("VEC", lambda e: e.tensor_reduce(
                                    rmx[0:1, 18:19], pmisc[0:1, 0:512], AX.X, OP.max))
                                if qt == 0:
                                    op("VEC", lambda e, ch=ch: e.tensor_copy(
                                        rmx[0:1, 16 + ch:17 + ch], rmx[0:1, 18:19]))
                                else:
                                    op("VEC", lambda e, ch=ch: e.tensor_tensor(
                                        rmx[0:1, 16 + ch:17 + ch],
                                        rmx[0:1, 16 + ch:17 + ch],
                                        rmx[0:1, 18:19], OP.max))
                            mk(f"nmx_{g}_{it}")
                wt("scalar", f"nmx_{g}_{4 * NQT - 1}", "VEC")
                op("ACT", lambda e: e.activation(rmx[0:1, 19:37], rmx[0:1, 0:18],
                                                 AF.Sqrt, 0.0, 1.0))
                mk(f"msq_{g}")
                wt("vector", f"msq_{g}", "ACT")
                for ch in range(2):
                    op("VEC", lambda e, ch=ch: e.tensor_scalar(
                        rmx[0:1, 37 + ch * 8:45 + ch * 8],
                        rmx[0:1, 19 + ch * 8:27 + ch * 8],
                        rmx[0:1, 35 + ch:36 + ch], -0.125, OP.mult, OP.mult))
                op("VEC", lambda e: e.tensor_scalar(rmx[0:1, 37:53], rmx[0:1, 37:53],
                                                    -1.0, None, OP.add))
                op("VEC", lambda e: e.tensor_copy(rb16[0:1, 16:32], rmx[0:1, 37:53]))
                mk(f"mrow_{g}")
                wt("tensor", f"mrow_{g}", "VEC")
                op("MM", lambda e: e.matmul(pmisc[:, 16:32], cb["ones_row1"][:],
                                            rb16[0:1, 16:32], start=True, stop=True))
                mk(f"mbc_{g}")
                wt("scalar", f"mbc_{g}", "MM")
                wt("scalar", f"vev_{g}_{NTC - 1}_3", "ACT")
                op("ACT", lambda e: e.copy(mbias[:, 0:16], pmisc[:, 16:32]))
                mk(f"mbias_{g}")
                maybe_cut(f"mbias_{g}")

                for hp in range(2):
                    for qt in range(NQT):
                        nkj = (qt + 1) * 4
                        for kj in range(nkj):
                            par = kj % 2
                            psc = ps0 if par == 0 else ps1
                            if kj >= 2:
                                wt("tensor", f"exp_{g}_{hp}_{qt}_{kj - 2}", "ACT")
                                wt("tensor", f"expm_{g}_{hp}_{qt}_{kj - 2}", "VEC")
                            else:
                                if hp == 0 and qt == 0:
                                    wt("tensor", f"mbias_{g}", "ACT")
                                else:
                                    pq = (hp, qt - 1) if qt else (hp - 1, NQT - 1)
                                    pk = (pq[1] + 1) * 4 - 2 + kj
                                    wt("tensor", f"exp_{g}_{pq[0]}_{pq[1]}_{pk}", "ACT")
                                    wt("tensor", f"expm_{g}_{pq[0]}_{pq[1]}_{pk}", "VEC")
                            for hd in range(2):
                                def bs(e, hd=hd, hp=hp, qt=qt, kj=kj, psc=psc):
                                    base = hd * 64
                                    return e.matmul(
                                        psc[:, hd * 512:(hd + 1) * 512],
                                        kg[base:base + 64, hp, kj * 128:(kj + 1) * 128],
                                        qg[base:base + 64, hp, qt * 512:(qt + 1) * 512],
                                        start=True, stop=True,
                                        tile_position=(base, 0),
                                        skip_group_check=True)
                                if hd == 1:
                                    op("MM", bs)
                                else:
                                    opn("MM", bs)
                            mk(f"sc_{g}_{hp}_{qt}_{kj}")
                            maybe_cut(f"sc_{g}_{hp}_{qt}_{kj}")
                            wt("scalar", f"sc_{g}_{hp}_{qt}_{kj}", "MM")
                            if kj >= 2:
                                wt("scalar", f"avm_{g}_{hp}_{qt}_{kj - 2}", "MM")
                            elif qt > 0 or hp > 0:
                                pq = (hp, qt - 1) if qt else (hp - 1, NQT - 1)
                                pk = (pq[1] + 1) * 4 - 2 + kj
                                wt("scalar", f"avm_{g}_{pq[0]}_{pq[1]}_{pk}", "MM")
                            elif g > 0:
                                pk = NQT * 4 - 2 + kj
                                wt("scalar", f"avm_{g - 1}_1_{NQT - 1}_{pk}", "MM")
                            op("ACT", lambda e, par=par, qt=qt, hp=hp, psc=psc:
                               e.activation(pt[:, par, :, :], psc[:, 0:1024], AF.Exp,
                                            mbias[:, hp * 8 + qt:hp * 8 + qt + 1],
                                            0.125))
                            mk(f"exp_{g}_{hp}_{qt}_{kj}")
                            maybe_cut(f"exp_{g}_{hp}_{qt}_{kj}")
                            diag = kj >= qt * 4
                            if diag:
                                wt("vector", f"exp_{g}_{hp}_{qt}_{kj}", "ACT")
                                for hd in range(2):
                                    op("VEC", lambda e, par=par, hd=hd, j=kj - qt * 4:
                                       e.tensor_tensor(pt[:, par, hd, :], pt[:, par, hd, :],
                                                       cb["mask01"][:, j, :], OP.mult))
                            mk(f"expm_{g}_{hp}_{qt}_{kj}")

                            def emit_av(kv):
                                parv = kv % 2
                                wt("tensor", f"exp_{g}_{hp}_{qt}_{kv}", "ACT")
                                if kv >= qt * 4:
                                    wt("tensor", f"expm_{g}_{hp}_{qt}_{kv}", "VEC")
                                if kv == 0:
                                    if hp > 0 or qt > 0:
                                        pq = (hp, qt - 1) if qt else (hp - 1, NQT - 1)
                                        wt("tensor", f"unn_{g}_{pq[0]}_{pq[1]}", "ACT")
                                        wt("tensor", f"recip_{g}_{pq[0]}_{pq[1]}", "VEC")
                                first, last = (kv == 0), (kv == nkj - 1)
                                op("MM", lambda e, kv=kv, hp=hp, parv=parv, first=first,
                                   last=last: e.matmul(
                                       pavd[0:64, 0:512],
                                       vg[:, kv, hp * 128:hp * 128 + 64],
                                       pt[:, parv, 0, :], start=first, stop=last,
                                       tile_position=(0, 0), skip_group_check=True))
                                op("MM", lambda e, kv=kv, hp=hp, parv=parv, first=first,
                                   last=last: e.matmul(
                                       pavd[64:128, 0:512],
                                       vg[:, kv, hp * 128 + 64:hp * 128 + 128],
                                       pt[:, parv, 1, :], start=first, stop=last,
                                       tile_position=(0, 64), skip_group_check=True))
                                op("MM", lambda e, parv=parv, first=first, last=last:
                                   e.matmul(pavd[0:1, 512:1024], cb["ones_col"][:],
                                            pt[:, parv, 0, :], start=first, stop=last,
                                            skip_group_check=True))
                                op("MM", lambda e, parv=parv, first=first, last=last:
                                   e.matmul(pavd[0:1, 1024:1536], cb["ones_col"][:],
                                            pt[:, parv, 1, :], start=first, stop=last,
                                            skip_group_check=True))
                                mk(f"avm_{g}_{hp}_{qt}_{kv}")

                            if kj >= 1:
                                emit_av(kj - 1)
                            if kj == nkj - 1:
                                emit_av(kj)
                        last_avm = f"avm_{g}_{hp}_{qt}_{nkj - 1}"
                        wt("scalar", last_avm, "MM")
                        if hp > 0 or qt > 0:
                            pq = (hp, qt - 1) if qt else (hp - 1, NQT - 1)
                            wt("scalar", f"norm_{g}_{pq[0]}_{pq[1]}", "VEC")
                        op("ACT", lambda e: e.copy(unn[:], pavd[:, 0:512]))
                        mk(f"unn_{g}_{hp}_{qt}")
                        wt("vector", last_avm, "MM")
                        op("VEC", lambda e: e.reciprocal(recip[0:1, :], pavd[0:1, 512:1536]))
                        op("VEC", lambda e: e.tensor_copy(rb16[0:1, 0:512],
                                                          recip[0:1, 0:512]))
                        mk(f"recip_{g}_{hp}_{qt}")
                        wt("tensor", f"recip_{g}_{hp}_{qt}", "VEC")
                        if hp > 0 or qt > 0:
                            pq = (hp, qt - 1) if qt else (hp - 1, NQT - 1)
                            wt("tensor", f"norm_{g}_{pq[0]}_{pq[1]}", "VEC")
                        elif g > 0:
                            wt("tensor", f"norm_{g - 1}_1_{NQT - 1}", "VEC")
                        op("MM", lambda e: e.matmul(pmisc[0:64, :], cb["ones_row1"][0:1, 0:64],
                                                    rb16[0:1, 0:512],
                                                    start=True, stop=True,
                                                    tile_position=(0, 0)))
                        mk(f"bc1_{g}_{hp}_{qt}")
                        wt("vector", f"bc1_{g}_{hp}_{qt}", "MM")
                        op("VEC", lambda e: e.tensor_copy(rb16[0:1, 0:512],
                                                          recip[0:1, 512:1024]))
                        mk(f"recipB_{g}_{hp}_{qt}")
                        wt("tensor", f"recipB_{g}_{hp}_{qt}", "VEC")
                        op("MM", lambda e: e.matmul(pmisc[64:128, :],
                                                    cb["ones_row1"][0:1, 0:64],
                                                    rb16[0:1, 0:512],
                                                    start=True, stop=True,
                                                    tile_position=(0, 64)))
                        mk(f"bc_{g}_{hp}_{qt}")
                        wt("vector", f"bc_{g}_{hp}_{qt}", "MM")
                        wt("vector", f"unn_{g}_{hp}_{qt}", "ACT")
                        op("VEC", lambda e, g=g, hp=hp, qt=qt: e.tensor_tensor(
                            agf[:, g * 2 + hp, qt * 512:(qt + 1) * 512], unn[:],
                            pmisc[:, :], OP.mult))
                        mk(f"norm_{g}_{hp}_{qt}")
                        maybe_cut(f"norm_{g}_{hp}_{qt}")
                mk(f"lastexp_{g}")
                mk(f"lastdone_{g}")

            wo = bfv(O_WO, [8, 1024])
            h1b = bfv(O_H1B, [1024])
            xres = fv(F_XRES, [2, 1024])
            h1t = fv(F_H1T, [1024])
            scal = fv(F_SCAL, [16])
            wt("sync", f"projMM_{NG - 1}_{NTC - 1}", "MM")
            for fc in range(8):
                op("WLD", lambda e, fc=fc: e.dma_start(
                    out=wo[:, fc, :], in_=d["wo_c"][fc * 128:(fc + 1) * 128, :]))
            mk("wo_ld")
            for t32 in range(32):
                par = t32 % 2
                if t32 >= 2:
                    wt("sync", f"h1add_{t32 - 2}", "VEC")
                op("LD", lambda e, t32=t32, par=par: e.dma_start(
                    out=xres[:, par, :], in_=d["x_tm"][t32 * 128:(t32 + 1) * 128, :]))
                mk(f"xtm_{t32}")
                wt("tensor", "wo_ld", "WLD")
                if t32 == 0:
                    wt("tensor", f"lastdone_{NG - 1}", "VEC")
                else:
                    wt("tensor", f"h1add_{t32 - 1}", "VEC")
                for oc in range(2):
                    for fc in range(8):
                        def bo(e, oc=oc, fc=fc, t32=t32):
                            return e.matmul(
                                pavd[:, oc * 512:(oc + 1) * 512],
                                agf[:, fc, t32 * 128:(t32 + 1) * 128],
                                wo[:, fc, oc * 512:(oc + 1) * 512],
                                start=(fc == 0), stop=(fc == 7))
                        if fc == 7 and oc == 1:
                            op("MM", bo)
                        else:
                            opn("MM", bo)
                mk(f"opmm_{t32}")
                wt("vector", f"opmm_{t32}", "MM")
                wt("vector", f"xtm_{t32}", "LD")
                if t32 > 0:
                    wt("vector", f"h1sq_{t32 - 1}", "ACT")
                op("VEC", lambda e, par=par: e.tensor_tensor(h1t[:], xres[:, par, :],
                                                             pavd[:, 0:1024], OP.add))
                op("VEC", lambda e: e.tensor_reduce(scal[:, 0:1], h1t[:], AX.X, OP.add))
                mk(f"h1add_{t32}")
                wt("scalar", f"h1add_{t32}", "VEC")
                op("ACT", lambda e: e.activation(h1b[:], h1t[:], AF.Square, 0.0, 1.0,
                                                 accum_out=scal[:, 1:2]))
                mk(f"h1sq_{t32}")
                wt("vector", f"h1sq_{t32}", "ACT")
                op("VEC", lambda e: e.tensor_scalar(scal[:, 2:3], scal[:, 0:1],
                                                    1.0 / D, None, OP.mult))
                op("VEC", lambda e: e.tensor_tensor(scal[:, 3:4], scal[:, 2:3],
                                                    scal[:, 2:3], OP.mult))
                op("VEC", lambda e: e.tensor_scalar(scal[:, 4:5], scal[:, 1:2],
                                                    1.0 / D, None, OP.mult))
                op("VEC", lambda e: e.tensor_tensor(scal[:, 5:6], scal[:, 4:5],
                                                    scal[:, 3:4], OP.subtract))
                op("VEC", lambda e: e.tensor_scalar(scal[:, 6:7], scal[:, 5:6],
                                                    1e-5, None, OP.add))
                mk(f"h1v_{t32}")
                wt("scalar", f"h1v_{t32}", "VEC")
                op("ACT", lambda e: e.activation(scal[:, 7:8], scal[:, 6:7],
                                                 AF.Sqrt, 0.0, 1.0))
                mk(f"h1sd_{t32}")
                wt("vector", f"h1sd_{t32}", "ACT")
                op("VEC", lambda e: e.reciprocal(scal[:, 8:9], scal[:, 7:8]))
                op("VEC", lambda e: e.tensor_scalar(scal[:, 9:10], scal[:, 2:3],
                                                    scal[:, 8:9], -1.0, OP.mult, OP.mult))
                mk(f"h1rs_{t32}")
                wt("scalar", f"h1rs_{t32}", "VEC")
                if t32 > 0:
                    wt("vector", f"h1st_{t32 - 1}", "SST")
                op("VEC", lambda e: e.tensor_scalar(fv(F_TMP, [1024])[:, :],
                                                    h1t[:], scal[:, 8:9],
                                                    scal[:, 9:10], OP.mult, OP.add))
                mk(f"opev_{t32}")
                op("VEC", lambda e: e.tensor_copy(h1b[:], fv(F_TMP, [1024])[:, :]))
                mk(f"h1b_{t32}")
                wt("sync", f"opev_{t32}", "VEC")
                op("SST", lambda e, t32=t32: e.dma_start(
                    out=d["h1_tm_d"][t32 * 128:(t32 + 1) * 128, :],
                    in_=fv(F_TMP, [1024])[:, :]))
                if "dbg_h1" in dbg:
                    op("SST", lambda e, t32=t32: e.dma_start(
                        out=d["dbg_h1"][t32 * 128:(t32 + 1) * 128, :],
                        in_=fv(F_TMP, [1024])[:, :]))
                wt("sync", f"h1b_{t32}", "VEC")
                op("SST", lambda e, t32=t32: e.dma_start(
                    out=d["h1_pub"][t32 * 128:(t32 + 1) * 128, :], in_=h1b[:]))
                mk(f"h1st_{t32}")
                maybe_cut(f"h1_{t32}")
            mk("h1_done")
            maybe_cut("h1only")

            h1fm = bfv(O_H1FM, [8, S])
            shact = bfv(O_SHACT2, [4, S])
            disp = bfv(O_DISP2, [2, 8, 512])
            wgu_sb = bfv(FB_WGU, [8, 1024])
            shdn_sb = bfv(FB_WGU, [4, 1024])
            wdn_sb = bfv(FB_WDN, [4, 1024])
            acte = bfv(FB_ACTE, [4, 1024])
            eost = bfv(FB_EOST, [1024])
            silb = bfv(FB_PAY, [1024])
            pay = bfv(FB_PAY, [2, 4, 128])
            ohb = bfv(FB_OH, [4, 32])
            decb = bfv(FB_DEC, [64, 2])
            gwsb = bfv(FB_GW, [8, 32])
            postm = fv(F_POS, [32, 32])
            iw = fv(F_IW, [32, 8])
            gsv = fv(F_GS, [4, 32])
            gcv = fv(F_GC, [4, 32])
            ftt = fv(F_TT, [128])
            ofsb = fv(F_OFS, [32])
            hlv = fv(F_HL, [2, 32])
            keptv = fv(F_KEPT, [4, 32])
            ltile = fv(F_TMP, [1024])
            h1tf = fv(F_H1TF, [1024])
            xrf = fv(F_XRESF, [1024])
            outf = fv(F_TMPF, [1024])

            wt("gpsimd", f"lastdone_{NG - 1}", "VEC")
            op("XLD", lambda e: e.dma_start(
                out=gwsb[:, :, :],
                in_=d["gate_wT"][:].rearrange("(a p) b -> p a b", p=128)))
            mk("gw_ld")
            op("XLD", lambda e: e.dma_start(
                out=wgu_sb[:, :, :],
                in_=d["sh_gu"][:].rearrange("(a p) b -> p a b", p=128)))
            mk("shgu_ld")
            wt("vector", f"lastdone_{NG - 1}", "VEC")
            op("VEC", lambda e: e.memset(AB[:, FB_GK:FB_GK + 4096], 0.0))
            mk("zb")
            wt("gpsimd", "zb", "VEC")
            for i in range(8):
                op("GLD", lambda e, i=i: e.dma_start(
                    out=d["slot_loc"][i * 4096:(i + 1) * 4096, :].rearrange(
                        "(p a) f -> p (a f)", p=128),
                    in_=AB[:, FB_GK:FB_GK + 4096]))
            mk("slotz")
            for kc in range(8):
                wt("gpsimd", f"h1st_{kc * 4 + 3}", "SST")
                op("COL", lambda e, kc=kc: e.collective_compute(
                    "AllGather", OP.bypass, RGRP,
                    [d["h1_pub"][kc * 512:(kc + 1) * 512, :]],
                    [d["h1_all"][kc * 4096:(kc + 1) * 4096, :]]))
            mk("ag_h1")

            wt("sync", "opmm_31", "MM")
            wt("sync", "h1_done", "SST")
            for fc in range(8):
                op("LD", lambda e, fc=fc: e.dma_start(
                    out=h1fm[:, fc, :],
                    in_=d["h1_pub"][:, fc * 128:(fc + 1) * 128], transpose=True))
            mk("h1fm_ld")
            maybe_cut("h1fm")

            for ti in range(32):
                wt("tensor", "h1fm_ld", "LD")
                wt("tensor", "gw_ld", "XLD")
                if ti:
                    wt("tensor", f"lgev_{ti - 1}", "ACT")
                for fc in range(8):
                    def bl(e, fc=fc, ti=ti):
                        return e.matmul(pmisc[:, 0:32],
                                        h1fm[:, fc, ti * 128:(ti + 1) * 128],
                                        gwsb[:, fc, :],
                                        start=(fc == 0), stop=(fc == 7))
                    if fc == 7:
                        op("MM", bl)
                    else:
                        opn("MM", bl)
                mk(f"lgmm_{ti}")
                wt("scalar", f"lgmm_{ti}", "MM")
                if ti:
                    wt("scalar", f"top_{ti - 1}", "VEC")
                op("ACT", lambda e: e.copy(ltile[:, 0:32], pmisc[:, 0:32]))
                mk(f"lgev_{ti}")
                wt("vector", f"lgev_{ti}", "ACT")
                op("VEC", lambda e: e.max(ltile[:, 32:40], ltile[:, 0:32]))
                op("VEC", lambda e: e.max_index(idxu[:, 0:8], ltile[:, 32:40],
                                                ltile[:, 0:32]))
                op("VEC", lambda e, ti=ti: e.tensor_copy(iw[:, ti, 0:4], idxu[:, 0:4]))
                op("VEC", lambda e: e.tensor_scalar(ltile[:, 40:41], ltile[:, 32:33],
                                                    -1.0, None, OP.mult))
                mk(f"topv_{ti}")
                wt("scalar", f"topv_{ti}", "VEC")
                op("ACT", lambda e: e.activation(ltile[:, 44:48], ltile[:, 32:36],
                                                 AF.Exp, ltile[:, 40:41], 1.0))
                mk(f"tope_{ti}")
                wt("vector", f"tope_{ti}", "ACT")
                op("VEC", lambda e: e.tensor_reduce(ltile[:, 41:42], ltile[:, 44:48],
                                                    AX.X, OP.add))
                op("VEC", lambda e: e.reciprocal(ltile[:, 42:43], ltile[:, 41:42]))
                op("VEC", lambda e, ti=ti: e.tensor_scalar(iw[:, ti, 4:8],
                                                           ltile[:, 44:48],
                                                           ltile[:, 42:43], None,
                                                           OP.mult))
                mk(f"top_{ti}")
            mk("route_done")
            maybe_cut("route")

            for ti in range(32):
                wt("vector", "route_done", "VEC")
                if ti:
                    wt("vector", f"posev_{ti - 1}", "ACT")
                for k in range(4):
                    op("VEC", lambda e, ti=ti, k=k: e.tensor_scalar(
                        ohb[:, k, :], cb["iota32"][:], iw[:, ti, k:k + 1], None,
                        OP.is_equal))
                op("VEC", lambda e: e.tensor_tensor(ohb[:, 0, :], ohb[:, 0, :],
                                                    ohb[:, 1, :], OP.add))
                op("VEC", lambda e: e.tensor_tensor(ohb[:, 2, :], ohb[:, 2, :],
                                                    ohb[:, 3, :], OP.add))
                op("VEC", lambda e: e.tensor_tensor(ohb[:, 0, :], ohb[:, 0, :],
                                                    ohb[:, 2, :], OP.add))
                mk(f"oh_{ti}")
                wt("tensor", f"oh_{ti}", "VEC")
                if ti:
                    wt("tensor", f"runq_{ti - 1}", "VEC")
                op("MM", lambda e: e.matmul(pmisc[0:32, 32:164], ohb[:, 0, :],
                                            cb["trilT"][:], start=True, stop=True))
                mk(f"pre_{ti}")
                wt("vector", f"pre_{ti}", "MM")
                if ti == 0:
                    op("VEC", lambda e: e.memset(ltile[0:32, 50:51], 0.0))
                op("VEC", lambda e: e.tensor_scalar(ltile[0:32, 52:180],
                                                    pmisc[0:32, 32:160],
                                                    ltile[0:32, 50:51], None, OP.add))
                op("VEC", lambda e: e.tensor_tensor(ltile[0:32, 50:51],
                                                    ltile[0:32, 50:51],
                                                    pmisc[0:32, 160:161], OP.add))
                mk(f"runq_{ti}")
                wt("tensor", f"runq_{ti}", "VEC")
                op("MM", lambda e: e.transpose(pavd[0:128, 0:32],
                                               ltile[0:32, 52:180],
                                               cb["identf"][0:32, 0:32]))
                mk(f"post_{ti}")
                wt("scalar", f"post_{ti}", "MM")
                op("ACT", lambda e, ti=ti: e.copy(postm[:, ti, :], pavd[:, 0:32]))
                mk(f"posev_{ti}")
            wt("sync", "runq_31", "VEC")
            op("SST", lambda e: e.dma_start(
                out=d["cnt_loc"][:].rearrange("a b -> b a"),
                in_=ltile[0:32, 50:51]))
            mk("cnt_st")
            wt("gpsimd", "cnt_st", "SST")
            op("COL", lambda e: e.collective_compute(
                "AllGather", OP.bypass, RGRP, [d["cnt_loc"][:]], [d["cnt_all"][:]]))
            mk("ag_cnt")
            wt("sync", "ag_cnt", "COL")
            op("LD", lambda e: e.dma_start(
                out=ltile[0:32, 56:64],
                in_=d["cnt_all"][:].rearrange("c e -> e c"),
                allow_slow_non_contiguous=True))
            mk("cnt_ld")
            wt("vector", "cnt_ld", "LD")
            op("VEC", lambda e: e.tensor_tensor(ltile[0:32, 56:64], ltile[0:32, 56:64],
                                                cb["prevmask"][:], OP.mult))
            op("VEC", lambda e: e.tensor_reduce(ltile[0:32, 64:65], ltile[0:32, 56:64],
                                                AX.X, OP.add))
            mk("offs_v")
            wt("tensor", "offs_v", "VEC")
            wt("tensor", "posev_31", "ACT")
            op("MM", lambda e: e.transpose(pavd[0:1, 32:64], ltile[0:32, 64:65],
                                           cb["identf"][0:32, 0:32]))
            mk("offs_t")
            wt("scalar", "offs_t", "MM")
            op("ACT", lambda e: e.copy(ftt[0:1, 0:32], pavd[0:1, 32:64]))
            mk("offs_row")
            wt("tensor", "offs_row", "ACT")
            op("MM", lambda e: e.matmul(pmisc[:, 192:224], cb["ones_row1f"][:],
                                        ftt[0:1, 0:32], start=True, stop=True))
            mk("offs_bc")
            wt("scalar", "offs_bc", "MM")
            op("ACT", lambda e: e.copy(ofsb[:, :], pmisc[:, 192:224]))
            mk("offs_done")
            wt("vector", "offs_done", "ACT")
            mk("hl_done")
            maybe_cut("offs")

            for ti in range(32):
                wt("vector", "hl_done", "VEC")
                for k in range(4):
                    op("VEC", lambda e, ti=ti, k=k: e.tensor_scalar(
                        ohb[:, k, :], cb["iota32"][:], iw[:, ti, k:k + 1], None,
                        OP.is_equal))
                op("VEC", lambda e, ti=ti: e.tensor_tensor(
                    ltile[:, 96:128], postm[:, ti, :], ofsb[:, :], OP.add))
                for k in range(4):
                    op("VEC", lambda e, k=k: e.tensor_tensor(
                        ltile[:, 128:160], ltile[:, 96:128], ohb[:, k, :], OP.mult))
                    op("VEC", lambda e, k=k: e.tensor_reduce(
                        ltile[:, 160 + k:161 + k], ltile[:, 128:160], AX.X, OP.add))
                op("VEC", lambda e: e.tensor_scalar(ltile[:, 164:168],
                                                    ltile[:, 160:164], float(CAP),
                                                    None, OP.is_lt))
                op("VEC", lambda e, ti=ti: e.tensor_copy(keptv[:, 0:4, ti],
                                                         ltile[:, 164:168]))
                op("VEC", lambda e, ti=ti: e.tensor_tensor(
                    iw[:, ti, 4:8], iw[:, ti, 4:8], ltile[:, 164:168], OP.mult))
                op("VEC", lambda e, ti=ti: e.scalar_tensor_tensor(
                    ltile[:, 184:188], iw[:, ti, 0:4], float(CAP),
                    ltile[:, 160:164], OP.mult, OP.add))
                for k in range(4):
                    op("VEC", lambda e, k=k: e.tensor_tensor(
                        ltile[:, 128:160], ohb[:, k, :], cb["ftab"][:], OP.mult))
                    op("VEC", lambda e, k=k: e.tensor_reduce(
                        ltile[:, 200 + k:201 + k], ltile[:, 128:160], AX.X, OP.add))
                op("VEC", lambda e: e.tensor_tensor(ltile[:, 188:192],
                                                    ltile[:, 200:204],
                                                    ltile[:, 160:164], OP.add))
                op("VEC", lambda e: e.tensor_tensor(ltile[:, 188:192],
                                                    ltile[:, 188:192],
                                                    ltile[:, 164:168], OP.mult))
                op("VEC", lambda e, ti=ti: e.tensor_copy(gcv[:, 0:4, ti],
                                                         ltile[:, 188:192]))
                op("VEC", lambda e: e.tensor_scalar(ltile[:, 184:188],
                                                    ltile[:, 184:188], 32767.0,
                                                    None, OP.min))
                op("VEC", lambda e, ti=ti: e.tensor_copy(gsv[:, 0:4, ti],
                                                         ltile[:, 184:188]))
                mk(f"gidx_{ti}")
            mk("gidx_done")
            maybe_cut("gidx")

            def fold_pass(src, base_slot, extra_xld_wait):
                for k in range(4):
                    sl = base_slot + k
                    wt("tensor", "gidx_done", "VEC")
                    if sl:
                        wt("tensor", f"fev_{sl - 1}", "ACT")
                    op("MM", lambda e, src=src, k=k: e.transpose(
                        pavd[0:32, 64:192], src[:, k, :], cb["identf"][:, 0:128]))
                    mk(f"ftr_{sl}")
                    wt("scalar", f"ftr_{sl}", "MM")
                    if sl:
                        wt("scalar", f"fst_{sl - 1}", "SST")
                    op("ACT", lambda e: e.copy(
                        ftt[0:32, :].rearrange("p (q a) -> p a q", q=16),
                        pavd[0:32, 64:192].rearrange("p (a q) -> p a q", a=8)))
                    mk(f"fev_{sl}")
                    wt("sync", f"fev_{sl}", "ACT")
                    op("SST", lambda e, sl=sl: e.dma_start(
                        out=d["rt_scr"][sl].rearrange("(q t a) -> t q a", q=16, a=8),
                        in_=ftt[0:32, :].rearrange("p (q a) -> p q a", q=16)))
                    mk(f"fst_{sl}")
                    wt("gpsimd", f"fst_{sl}", "SST")
                    if extra_xld_wait:
                        wt("gpsimd", extra_xld_wait, "GST")
                    for grp in range(8):
                        op("XLD", lambda e, sl=sl, grp=grp, k=k: e.dma_start(
                            out=idx16[grp * 16:(grp + 1) * 16,
                                      k * 256:(k + 1) * 256],
                            in_=d["rt_scr"][sl].rearrange("(p c) -> p c", p=16)))
                    mk(f"fld_{sl}")

            fold_pass(gsv, 0, None)
            mk("fold_done")
            maybe_cut("fold")

            for k in range(4):
                wt("vector", "hl_done", "VEC")
                wt("vector", "gidx_done", "VEC")
                op("VEC", lambda e, k=k: e.tensor_tensor(
                    ltile[:, 224:256], cb["gid_hi"][:], keptv[:, k, :], OP.mult))
                op("VEC", lambda e, k=k: e.tensor_tensor(
                    ltile[:, 256:288], cb["gid_lo"][:], keptv[:, k, :], OP.mult))
                mk(f"hlk_{k}")
                for jj in range(8):
                    par = jj % 2
                    if k or jj >= 2:
                        pk, pj = (k, jj - 2) if jj >= 2 else (k - 1, jj + 6)
                        wt("vector", f"scat_{pk}_{pj}", "GST")
                    op("VEC", lambda e, par=par: e.memset(pay[:, par, :, :], 0.0))
                    op("VEC", lambda e, par=par, jj=jj: e.tensor_copy(
                        pay[:, par, :, 0:1],
                        ltile[:, 224 + jj * 4:224 + (jj + 1) * 4].rearrange(
                            "p (a o) -> p a o", o=1)))
                    op("VEC", lambda e, par=par, jj=jj: e.tensor_copy(
                        pay[:, par, :, 1:2],
                        ltile[:, 256 + jj * 4:256 + (jj + 1) * 4].rearrange(
                            "p (a o) -> p a o", o=1)))
                    mk(f"payb_{k}_{jj}")
                    wt("gpsimd", f"payb_{k}_{jj}", "VEC")
                    wt("gpsimd", "slotz", "GLD")
                    wt("gpsimd", f"fld_{k}", "XLD")
                    op("GST", lambda e, k=k, jj=jj, par=par: e.dma_scatter_add(
                        d["slot_loc"][:], pay[:, par, :, :],
                        idx16[:, k * 256 + jj * 32:k * 256 + (jj + 1) * 32],
                        512, 512, 128))
                    mk(f"scat_{k}_{jj}")
            mk("scat_done")
            wt("gpsimd", "scat_done", "GST")
            op("COL", lambda e: e.collective_compute(
                "ReduceScatter", OP.add, RGRP, [d["slot_loc"][:]], [d["slot_rs"][:]]))
            mk("rs_slot")
            fold_pass(gcv, 4, "scat_done")
            mk("cfold_done")
            maybe_cut("scat")

            for tc in range(8):
                for oc in range(4):
                    wt("tensor", "shgu_ld", "XLD")
                    wt("tensor", "h1fm_ld", "LD")
                    if tc or oc:
                        po, pt = (oc - 1, tc) if oc else (3, tc - 1)
                        wt("tensor", f"shmul_{pt}_{po}", "VEC")
                    for half, pp in ((0, ps0), (1, ps1)):
                        for fc in range(8):
                            def bg(e, oc=oc, fc=fc, tc=tc, half=half, pp=pp):
                                return e.matmul(
                                    pp[:, 0:512],
                                    wgu_sb[:, fc,
                                           half * 512 + oc * 128:
                                           half * 512 + (oc + 1) * 128],
                                    h1fm[:, fc, tc * 512:(tc + 1) * 512],
                                    start=(fc == 0), stop=(fc == 7))
                            if fc == 7 and half == 1:
                                op("MM", bg)
                            else:
                                opn("MM", bg)
                    mk(f"shmm_{tc}_{oc}")
                    wt("scalar", f"shmm_{tc}_{oc}", "MM")
                    if tc == 0 and oc == 0:
                        wt("scalar", "scat_done", "GST")
                    if tc or oc:
                        po, pt = (oc - 1, tc) if oc else (3, tc - 1)
                        wt("scalar", f"shmul_{pt}_{po}", "VEC")
                    op("ACT", lambda e: e.activation(silb[:, 0:512], ps0[:, 0:512],
                                                     AF.Silu, 0.0, 1.0))
                    mk(f"shsl_{tc}_{oc}")
                    wt("vector", f"shsl_{tc}_{oc}", "ACT")
                    op("VEC", lambda e, tc=tc, oc=oc: e.tensor_tensor(
                        shact[:, oc, tc * 512:(tc + 1) * 512], silb[:, 0:512],
                        ps1[:, 0:512], OP.mult))
                    mk(f"shmul_{tc}_{oc}")
            mk("sh_done")
            maybe_cut("shared")

            for j in range(EPC):
                wt("sync", "rs_slot", "COL")
                if j:
                    wt("sync", f"dec_v_{j - 1}", "VEC")
                op("LD", lambda e, j=j: e.dma_start(
                    out=decb[0:16, :, :],
                    in_=d["slot_rs"][j * CAP:(j + 1) * CAP, 0:2].rearrange(
                        "(c p) x -> p c x", p=16)))
                mk(f"dec_ld_{j}")
                wt("vector", f"dec_ld_{j}", "LD")
                if j:
                    wt("vector", f"dec_st_{j - 1}", "SST")
                op("VEC", lambda e: e.scalar_tensor_tensor(
                    ltile[0:16, 200:264], decb[0:16, :, 0], 128.0,
                    decb[0:16, :, 1], OP.mult, OP.add))
                op("VEC", lambda e: e.tensor_scalar(
                    ltile[0:16, 200:264], ltile[0:16, 200:264], -1.0, 0.0,
                    OP.add, OP.max))
                mk(f"dec_v_{j}")
                wt("sync", f"dec_v_{j}", "VEC")
                op("SST", lambda e, j=j: e.dma_start(
                    out=d["rt_scr"][8 + j, 0:1024].rearrange("(p c) -> p c", p=16),
                    in_=ltile[0:16, 200:264]))
                mk(f"dec_st_{j}")
                wt("gpsimd", f"dec_st_{j}", "SST")
                for grp in range(8):
                    op("XLD", lambda e, j=j, grp=grp: e.dma_start(
                        out=idx16[grp * 16:(grp + 1) * 16,
                                  1024 + j * 64:1024 + (j + 1) * 64],
                        in_=d["rt_scr"][8 + j, 0:1024].rearrange("(p c) -> p c",
                                                                 p=16)))
                mk(f"dec_ix_{j}")
            mk("dec_done")
            maybe_cut("decode")

            for j in range(EPC):
                wt("gpsimd", "sh_done" if j == 0 else f"egmm_{j - 1}_1_3", "MM")
                op("XLD", lambda e, j=j: e.dma_start(
                    out=wgu_sb[:, :, :],
                    in_=d["w_gu"][j].rearrange("(a p) b -> p a b", p=128)))
                mk(f"ewgu_{j}")
                if j == 0:
                    wt("gpsimd", "slotz", "GLD")
                else:
                    wt("gpsimd", f"edmm_{j - 1}_7", "MM")
                op("XLD", lambda e, j=j: e.dma_start(
                    out=wdn_sb[:, :, :],
                    in_=d["w_dn"][j].rearrange("(a p) b -> p a b", p=128)))
                mk(f"ewdn_{j}")
                wt("gpsimd", f"dec_ix_{j}", "XLD")
                wt("gpsimd", "ag_h1", "COL")
                if j:
                    wt("gpsimd", f"egmm_{j - 1}_1_3", "MM")
                for hf in range(2):
                    op("GLD", lambda e, j=j, hf=hf: e.dma_gather(
                        disp[:, hf, :, :], d["h1_all"][:],
                        idx16[:, 1024 + j * 64 + hf * 32:
                              1024 + j * 64 + (hf + 1) * 32],
                        512, 512, 1024, transpose=True))
                mk(f"dgath_{j}")
                if j:
                    wt("gpsimd", f"eo_st_{j - 1}_7", "SST")
                    op("COL", lambda e, j=j: e.collective_compute(
                        "AllGather", OP.bypass, RGRP,
                        [d["eo_loc"][(j - 1) * CAP:j * CAP, :]],
                        [d["eo_all"][(j - 1) * 8 * CAP:j * 8 * CAP, :]]))
                    mk(f"ag_eo_{j - 1}")
                for hf in range(2):
                    for oc in range(4):
                        wt("tensor", f"ewgu_{j}", "XLD")
                        wt("tensor", f"dgath_{j}", "GLD")
                        if hf or oc:
                            po, ph = (oc - 1, hf) if oc else (3, hf - 1)
                            wt("tensor", f"emul_{j}_{ph}_{po}", "VEC")
                        elif j:
                            wt("tensor", f"emul_{j - 1}_1_3", "VEC")
                        for half, pp in ((0, ps0), (1, ps1)):
                            for fc in range(8):
                                def bg(e, oc=oc, fc=fc, hf=hf, half=half, pp=pp):
                                    return e.matmul(
                                        pp[:, 0:512],
                                        wgu_sb[:, fc,
                                               half * 512 + oc * 128:
                                               half * 512 + (oc + 1) * 128],
                                        disp[:, hf, fc, :],
                                        start=(fc == 0), stop=(fc == 7))
                                if fc == 7 and half == 1:
                                    op("MM", bg)
                                else:
                                    opn("MM", bg)
                        mk(f"egmm_{j}_{hf}_{oc}")
                        wt("scalar", f"egmm_{j}_{hf}_{oc}", "MM")
                        if hf or oc:
                            po, ph = (oc - 1, hf) if oc else (3, hf - 1)
                            wt("scalar", f"emul_{j}_{ph}_{po}", "VEC")
                        op("ACT", lambda e: e.activation(silb[:, 0:512],
                                                         ps0[:, 0:512],
                                                         AF.Silu, 0.0, 1.0))
                        mk(f"esil_{j}_{hf}_{oc}")
                        wt("vector", f"esil_{j}_{hf}_{oc}", "ACT")
                        op("VEC", lambda e, hf=hf, oc=oc: e.tensor_tensor(
                            acte[:, oc, hf * 512:(hf + 1) * 512], silb[:, 0:512],
                            ps1[:, 0:512], OP.mult))
                        mk(f"emul_{j}_{hf}_{oc}")
                for g in range(8):
                    wt("tensor", f"ewdn_{j}", "XLD")
                    wt("tensor", f"emul_{j}_1_3", "VEC")
                    if g:
                        wt("tensor", f"eoev_{j}_{g - 1}", "ACT")
                    elif j:
                        wt("tensor", f"eoev_{j - 1}_7", "ACT")
                    for oc in range(2):
                        for ic in range(4):
                            def bd(e, g=g, oc=oc, ic=ic):
                                return e.matmul(
                                    pavd[:, oc * 512:(oc + 1) * 512],
                                    acte[:, ic, g * 128:(g + 1) * 128],
                                    wdn_sb[:, ic, oc * 512:(oc + 1) * 512],
                                    start=(ic == 0), stop=(ic == 3))
                            if oc == 1 and ic == 3:
                                op("MM", bd)
                            else:
                                opn("MM", bd)
                    mk(f"edmm_{j}_{g}")
                    wt("scalar", f"edmm_{j}_{g}", "MM")
                    if g:
                        wt("scalar", f"eo_st_{j}_{g - 1}", "SST")
                    elif j:
                        wt("scalar", f"eo_st_{j - 1}_7", "SST")
                    op("ACT", lambda e: e.copy(eost[:, :], pavd[:, 0:1024]))
                    mk(f"eoev_{j}_{g}")
                    wt("sync", f"eoev_{j}_{g}", "ACT")
                    op("SST", lambda e, j=j, g=g: e.dma_start(
                        out=d["eo_loc"][(j * 8 + g) * 128:(j * 8 + g + 1) * 128, :],
                        in_=eost[:, :]))
                    mk(f"eo_st_{j}_{g}")
            mk("eo_done")
            wt("gpsimd", "eo_st_3_7", "SST")
            op("COL", lambda e: e.collective_compute(
                "AllGather", OP.bypass, RGRP,
                [d["eo_loc"][3 * CAP:4 * CAP, :]],
                [d["eo_all"][3 * 8 * CAP:4 * 8 * CAP, :]]))
            mk("ag_eo")
            maybe_cut("experts")

            wt("gpsimd", "egmm_3_1_3", "MM")
            op("XLD", lambda e: e.dma_start(
                out=shdn_sb[:, :, :],
                in_=d["sh_dn"][:].rearrange("(a p) b -> p a b", p=128)))
            mk("shdn_ld")
            for j in range(8):
                for k in range(4):
                    par = k % 2
                    gk = bfv(FB_GK + par * 4096, [4, 1024])
                    wt("gpsimd", "ag_eo", "COL")
                    wt("gpsimd", "cfold_done", "XLD")
                    if j or k >= 2:
                        pj, pk2 = (j, k - 2) if k >= 2 else (j - 1, k + 2)
                        wt("gpsimd", f"cacc_{pj}_{pk2}", "VEC")
                    if j == 0 and k < 2:
                        wt("gpsimd", f"edmm_3_7", "MM")
                    op("GLD", lambda e, j=j, k=k, gk=gk: e.dma_gather(
                        gk[:, :, :], d["eo_all"][:],
                        idx16[:, k * 256 + j * 32:k * 256 + (j + 1) * 32],
                        512, 512, 1024))
                    mk(f"cg_{j}_{k}")
                    wt("vector", f"cg_{j}_{k}", "GLD")
                    if k == 0 and j:
                        wt("vector", f"fln_{j * 4 - 1}", "VEC")
                    for b in range(4):
                        accb = fv(F_ACC0 + b * 1024, [1024])
                        if k == 0:
                            op("VEC", lambda e, j=j, k=k, b=b, gk=gk, accb=accb:
                               e.tensor_scalar(accb[:, :], gk[:, b, :],
                                               iw[:, j * 4 + b, 4 + k:5 + k],
                                               None, OP.mult))
                        else:
                            op("VEC", lambda e, j=j, k=k, b=b, gk=gk, accb=accb:
                               e.scalar_tensor_tensor(accb[:, :], gk[:, b, :],
                                                      iw[:, j * 4 + b, 4 + k:5 + k],
                                                      accb[:, :], OP.mult, OP.add))
                    mk(f"cacc_{j}_{k}")
                for b in range(4):
                    t32 = j * 4 + b
                    accb = fv(F_ACC0 + b * 1024, [1024])
                    wt("tensor", "shdn_ld", "XLD")
                    wt("tensor", "sh_done", "VEC")
                    if t32:
                        wt("tensor", f"fadd_{t32 - 1}", "VEC")
                    for oc in range(2):
                        for ic in range(4):
                            def bsd(e, t32=t32, oc=oc, ic=ic):
                                return e.matmul(
                                    pavd[:, oc * 512:(oc + 1) * 512],
                                    shact[:, ic, t32 * 128:(t32 + 1) * 128],
                                    shdn_sb[:, ic, oc * 512:(oc + 1) * 512],
                                    start=(ic == 0), stop=(ic == 3))
                            if oc == 1 and ic == 3:
                                op("MM", bsd)
                            else:
                                opn("MM", bsd)
                    mk(f"cdmm_{t32}")
                    if t32:
                        wt("sync", f"fadd_{t32 - 1}", "VEC")
                    else:
                        wt("sync", "cfold_done", "ACT")
                    op("LD", lambda e, t32=t32: e.dma_start(
                        out=xrf[:, :],
                        in_=d["h1_tm_d"][t32 * 128:(t32 + 1) * 128, :]))
                    mk(f"cxld_{t32}")
                    wt("vector", f"cdmm_{t32}", "MM")
                    wt("vector", f"cxld_{t32}", "LD")
                    if t32:
                        wt("vector", f"fsq_{t32 - 1}", "ACT")
                    op("VEC", lambda e: e.tensor_tensor(h1tf[:, :], xrf[:, :],
                                                        pavd[:, 0:1024], OP.add))
                    op("VEC", lambda e, accb=accb: e.tensor_tensor(
                        h1tf[:, :], h1tf[:, :], accb[:, :], OP.add))
                    op("VEC", lambda e: e.tensor_reduce(scal[:, 0:1], h1tf[:, :],
                                                        AX.X, OP.add))
                    mk(f"fadd_{t32}")
                    wt("scalar", f"fadd_{t32}", "VEC")
                    op("ACT", lambda e: e.activation(eost[:, :], h1tf[:, :],
                                                     AF.Square, 0.0, 1.0,
                                                     accum_out=scal[:, 1:2]))
                    mk(f"fsq_{t32}")
                    wt("vector", f"fsq_{t32}", "ACT")
                    op("VEC", lambda e: e.tensor_scalar(scal[:, 2:3], scal[:, 0:1],
                                                        1.0 / D, None, OP.mult))
                    op("VEC", lambda e: e.tensor_tensor(scal[:, 3:4], scal[:, 2:3],
                                                        scal[:, 2:3], OP.mult))
                    op("VEC", lambda e: e.tensor_scalar(scal[:, 4:5], scal[:, 1:2],
                                                        1.0 / D, None, OP.mult))
                    op("VEC", lambda e: e.tensor_tensor(scal[:, 5:6], scal[:, 4:5],
                                                        scal[:, 3:4], OP.subtract))
                    op("VEC", lambda e: e.tensor_scalar(scal[:, 6:7], scal[:, 5:6],
                                                        1e-5, None, OP.add))
                    mk(f"fv_{t32}")
                    wt("scalar", f"fv_{t32}", "VEC")
                    op("ACT", lambda e: e.activation(scal[:, 7:8], scal[:, 6:7],
                                                     AF.Sqrt, 0.0, 1.0))
                    mk(f"fsd_{t32}")
                    wt("vector", f"fsd_{t32}", "ACT")
                    op("VEC", lambda e: e.reciprocal(scal[:, 8:9], scal[:, 7:8]))
                    op("VEC", lambda e: e.tensor_scalar(scal[:, 9:10], scal[:, 2:3],
                                                        scal[:, 8:9], -1.0,
                                                        OP.mult, OP.mult))
                    mk(f"frs_{t32}")
                    wt("scalar", f"frs_{t32}", "VEC")
                    if t32:
                        wt("vector", f"out_st_{t32 - 1}", "SST")
                    op("VEC", lambda e: e.tensor_scalar(outf[:, :], h1tf[:, :],
                                                        scal[:, 8:9], scal[:, 9:10],
                                                        OP.mult, OP.add))
                    mk(f"fln_{t32}")
                    wt("sync", f"fln_{t32}", "VEC")
                    op("SST", lambda e, t32=t32: e.dma_start(
                        out=d["out"][t32 * 128:(t32 + 1) * 128, :],
                        in_=outf[:, :]))
                    mk(f"out_st_{t32}")
            mk("ffn_done")

        def run_prog():
            try:
                program()
            except _Cut:
                for t32 in range(32):
                    op("SST", lambda e, t32=t32: e.dma_start(
                        out=d["out"][t32 * 128:(t32 + 1) * 128, :],
                        in_=d["x_tm"][t32 * 128:(t32 + 1) * 128, :]))
            if state["emit"] and state["en"] == "sync":
                for kk in sorted(val):
                    if val[kk] and kk != "SST":
                        state["e"].wait_ge(sems[kk], val[kk])
            if "dbg_ag" in dbg:
                for i in range(8):
                    op("SST", lambda e, i=i: e.dma_start(
                        out=d["dbg_ag"][:, i * S:(i + 1) * S],
                        in_=AB[:, i * S:(i + 1) * S]))
                op("SST", lambda e: e.dma_start(out=d["dbg_f32"][:], in_=A32[:]))
            if "dbg_cnt" in dbg:
                op("SST", lambda e: e.dma_start(out=d["dbg_cnt"][:],
                                                in_=d["cnt_all"][:]))
                op("SST", lambda e: e.dma_start(out=d["dbg_srs"][:],
                                                in_=d["slot_rs"][:]))
                op("SST", lambda e: e.dma_start(out=d["dbg_eo"][:],
                                                in_=d["eo_loc"][:]))
                op("SST", lambda e: e.dma_start(out=d["dbg_rt"][0:8],
                                                in_=d["rt_scr"][0:8]))
                op("SST", lambda e: e.dma_start(out=d["dbg_rt"][8:12, 0:1024],
                                                in_=d["rt_scr"][8:12, 0:1024]))
            if state["emit"] and state["en"] == "sync":
                state["e"].wait_ge(sems["SST"], val["SST"])

        run_prog()
        state["emit"] = True
        final_counts = dict(val)

        def run_engine(en):
            def body(e):
                for k in val:
                    val[k] = 0
                state["en"] = en
                state["e"] = e
                run_prog()
            return body

        blk.sync(run_engine("sync"))
        blk.vector(run_engine("vector"))
        blk.tensor(run_engine("tensor"))
        blk.scalar(run_engine("scalar"))
        blk.gpsimd(run_engine("gpsimd"))

    nc.compile()
    return nc, dbg


_CACHE = {}


def kernel(**inputs):
    key = "main"
    if key not in _CACHE:
        _CACHE[key] = build_nc()
    nc, dbg = _CACHE[key]
    in_maps = percore_inputs(inputs)
    res = run_bass_kernel_spmd(nc, in_maps, core_ids=list(range(N_CORES)))
    out = np.stack([res.results[c]["out"] for c in range(N_CORES)], axis=0)
    return out
